# version 04 (final)
# baseline (speedup 1.0000x reference)
import os
os.environ.setdefault("GAUGE_NTFF_IGNORE_DMA", "1")
os.environ.setdefault("JAX_COMPILATION_CACHE_DIR", "/tmp/jaxcache")
os.environ.setdefault("JAX_PERSISTENT_CACHE_MIN_ENTRY_SIZE_BYTES", "-1")
os.environ.setdefault("JAX_PERSISTENT_CACHE_MIN_COMPILE_TIME_SECS", "0")

import numpy as np
import ml_dtypes
import concourse.bass as bass
import concourse.mybir as mybir
from concourse.bass_utils import run_bass_kernel_spmd

try:
    import jax as _jax
    _jax.config.update("jax_compilation_cache_dir", "/tmp/jaxcache")
except Exception:
    pass

F32 = mybir.dt.float32
BF16 = mybir.dt.bfloat16
AF = mybir.ActivationFunctionType
ALU = mybir.AluOpType
AX = mybir.AxisListType

N_CORES = 8
B, S, D = 4, 4096, 1024
TOK_PER_CORE = B * S // N_CORES
NT = TOK_PER_CORE // 128

ENG2BLK = {"pe": "tensor", "dve": "vector", "act": "scalar", "pool": "gpsimd", "sp": "sync"}


class Prog:

    def __init__(self, nc):
        self.nc = nc
        self.ops = {e: [] for e in ENG2BLK}
        self.sems = {}
        self.counts = {}
        self.seen = {e: {} for e in ENG2BLK}
        for e in ENG2BLK:
            self.sem("c_" + e)

    def sem(self, name):
        if name not in self.sems:
            self.sems[name] = self.nc.alloc_semaphore(name)
            self.counts[name] = 0
        return name

    def _filter(self, eng, wait):
        seen = self.seen[eng]
        out = []
        for ev in wait:
            if ev is None:
                continue
            s, v = ev
            if seen.get(s, -1) >= v:
                continue
            seen[s] = v
            out.append(ev)
        return out

    def wait(self, eng, ev):
        for s, v in self._filter(eng, (ev,)):
            h = self.sems[s]
            self.ops[eng].append(lambda E, h=h, v=v: E.wait_ge(h, v))

    def op(self, eng, build, wait=(), inc=None, inc_amt=1):
        evs = self._filter(eng, wait)
        for s, v in evs[:-1]:
            h = self.sems[s]
            self.ops[eng].append(lambda E, h=h, v=v: E.wait_ge(h, v))
        att = None
        if evs:
            s, v = evs[-1]
            att = (self.sems[s], v)
        if inc is True:
            inc = "c_" + eng
        if inc:
            self.sem(inc)
            self.counts[inc] += inc_amt
            h, v = self.sems[inc], inc_amt
            self.ops[eng].append(lambda E, b=build, h=h, a=v, att=att:
                                 (b(E).then_inc(h, a)._wait_ge(*att) if att else b(E).then_inc(h, a)))
            return (inc, self.counts[inc])
        else:
            self.ops[eng].append(lambda E, b=build, att=att:
                                 (b(E)._wait_ge(*att) if att else b(E)))
            return None

    def dma(self, eng, out, in_, sem, wait=(), **kw):
        return self.op(eng, lambda E: E.dma_start(out=out, in_=in_, **kw),
                       wait=wait, inc=sem, inc_amt=16)

    def ev(self, sname):
        return (sname, self.counts[sname])

    def barrier(self):
        for eng in ENG2BLK:
            for sname in list(self.counts):
                if self.counts[sname] > 0:
                    self.wait(eng, (sname, self.counts[sname]))

    def finalize(self):
        with self.nc.Block() as blk:
            for eng, attr in ENG2BLK.items():
                ops = self.ops[eng]
                getattr(blk, attr)(lambda E, ops=ops: [f(E) for f in ops])
        return self.nc


class NS:
    def __init__(self, **kw):
        self.__dict__.update(kw)


class FZ:

    def __init__(self, p, nc):
        self.p = p
        self.nc = nc
        self.input_names = []
        self.tensors = {}
        self.redirect = {}

    def env(self, pre):
        fz = self
        sb0 = self.nc.sbuf_base
        ps0 = self.nc.psum_base

        def pin(name, shape, dt=F32):
            full = pre + name
            tgt = fz.redirect.get(full)
            if tgt is not None:
                return fz.tensors[tgt]
            fz.input_names.append(full)
            h = self.nc.dram_tensor(full, shape, dt, kind="ExternalInput")
            fz.tensors[full] = h
            return h

        def dramt(name, shape, dt):
            return self.nc.dram_tensor(pre + name, shape, dt)

        def sbuf(name, shape, dt):
            return self.nc.alloc_sbuf_tensor(pre + name, shape, dt)

        def psum(name, shape, dt=F32):
            return self.nc.alloc_psum_tensor(pre + name, shape, dt)

        def fin():
            self.p.barrier()
            self.nc.sbuf_base = sb0
            self.nc.psum_base = ps0
            return None
        return self.p, self.nc, pin, dramt, sbuf, psum, fin


def _env(fz, pre):
    if fz is not None:
        return fz.env(pre)
    nc = bass.Bass("TRN2", target_bir_lowering=False)
    p = Prog(nc)

    def pin(name, shape, dt=F32):
        return nc.dram_tensor(name, shape, dt, kind="ExternalInput")

    def dramt(name, shape, dt):
        return nc.dram_tensor(name, shape, dt)

    def sbuf(name, shape, dt):
        return nc.alloc_sbuf_tensor(name, shape, dt)

    def psum(name, shape, dt=F32):
        return nc.alloc_psum_tensor(name, shape, dt)

    def fin():
        return p.finalize()
    return p, nc, pin, dramt, sbuf, psum, fin



def emit_prelude(p, nc, x_in, *, alloc=None, with_rms=False, router=False, router_bias=False,
                 store_ln=False, w8_scale=1.0, w_scale_dram=None, w_router_dram=None,
                 router_b_dram=None, ident_dram=None, n_rows=TOK_PER_CORE, xnT_buf=None,
                 row_base=0, reuse=None, first_dep=None):
    NTL = n_rows // 128
    if reuse is not None:
        s = reuse
        pv = dict(s._pv_end)
        s.ev_w8 = []
        s.ev_lnT = []
        for i in range(NTL):
            tile_dep = p.dma("sp", s.sb_x[:, :], x_in[row_base + 128 * i:row_base + 128 * (i + 1), :], p.sem("d_in"),
                             wait=(pv.get("e_xc"), first_dep if i == 0 else None))
            e1 = p.op("dve", lambda E: E.reduce_sum(s.sb_stat[:, 0:1], s.sb_x[:, :], axis=AX.X),
                      wait=(tile_dep, pv.get("e_xc"), first_dep if i == 0 else None), inc=True)
            e2 = p.op("dve", lambda E: E.tensor_scalar(s.sb_stat[:, 0:1], s.sb_stat[:, 0:1], -1.0 / D, None, ALU.mult),
                      wait=(e1,), inc=True)
            e_xc = p.op("act", lambda E: E.activation(s.sb_scr[:, :], s.sb_x[:, :], AF.Identity, bias=s.sb_stat[:, 0:1]),
                        wait=(e2,), inc=True)
            e_v = p.op("act", lambda E: E.activation(s.sb_dump[:, :], s.sb_scr[:, :], AF.Square, accum_out=s.sb_stat[:, 1:2]),
                       wait=(e_xc,), inc=True)
            e5 = p.op("dve", lambda E: E.tensor_scalar(s.sb_stat[:, 1:2], s.sb_stat[:, 1:2], 1.0 / D, 1e-5, ALU.mult, ALU.add),
                      wait=(e_v,), inc=True)
            e6 = p.op("act", lambda E: E.activation(s.sb_stat[:, 2:3], s.sb_stat[:, 1:2], AF.Sqrt), wait=(e5,), inc=True)
            e7 = p.op("dve", lambda E: E.reciprocal(s.sb_stat[:, 3:4], s.sb_stat[:, 2:3]), wait=(e6,), inc=True)
            e_ln = p.op("dve", lambda E: E.tensor_scalar(s.sb_ln[:, :], s.sb_scr[:, :], s.sb_stat[:, 3:4], None, ALU.mult),
                        wait=(e7, pv.get("e_lastT")), inc=True)
            e_xnp = e_ln
            evA = None
            for dc in range(8):
                wl = [e_xnp] if dc == 0 else [e_xnp, evA]
                et = p.op("pe", lambda E, dc=dc: E.transpose(s.ps_t[:, 0:128], s.sb_ln[:, dc * 128:(dc + 1) * 128], s.sb_id32[:, :]),
                          wait=tuple(wl), inc=True)
                evA = p.op("dve", lambda E, dc=dc, i=i: E.tensor_scalar(
                    s.sb_xnT[:, dc * n_rows + 128 * i: dc * n_rows + 128 * (i + 1)],
                    s.ps_t[:, 0:128], s.sb_nsc[:, dc:dc + 1], None, ALU.mult), wait=(et,), inc=True)
            e_lastT = et
            s.ev_lnT.append(evA)
            pv = dict(e_xc=e_xc, lnstore=None, e_lnsq=None, e_lastT=e_lastT, e_w8=None)
        s._pv_end = pv
        s.ev_dve_done = p.ev("c_dve")
        return s
    s = NS()
    if alloc is not None:
        _sba, _psa, _dra = alloc
    else:
        _sba, _psa, _dra = nc.alloc_sbuf_tensor, nc.alloc_psum_tensor, nc.dram_tensor
    s.sb_x = _sba("pl_x", [128, D], F32)
    s.sb_scr = _sba("pl_scr", [128, D], F32)
    s.sb_dump = _sba("pl_dump", [128, D], F32)
    s.sb_ln = _sba("pl_ln", [128, D], F32)
    s.sb_xnp = _sba("pl_xnp", [128, D], F32) if with_rms else None
    s.sb_stat = _sba("pl_stat", [128, 16], F32)
    s.sb_xnT = xnT_buf if xnT_buf is not None else _sba("pl_xnT", [128, 8 * n_rows], BF16)
    s.sb_xnpT = _sba("pl_xnpT", [128, 128], F32)
    s.sb_id32 = _sba("pl_id32", [128, 128], F32)
    s.sb_nsc = _sba("pl_nsc", [128, 8], F32)
    s.ps_t = _psa("pl_ps_t", [128, 512], F32)
    if router:
        s.sb_wrT = _sba("pl_wrT", [128, 8 * 8], F32)
        s.sb_wr = _sba("pl_wr", [8, D], F32)
        s.sb_g = _sba("pl_g", [128, 8], F32)
        s.sb_gt = _sba("pl_gt", [128, 8], F32)
        s.sb_w8 = _sba("pl_w8", [128, NT * 8], F32)
        s.ps_r = _psa("pl_ps_r", [128, 512], F32)
        if router_bias:
            s.sb_rbr = _sba("pl_rbr", [1, 8], F32)
            s.sb_rb = _sba("pl_rb", [128, 8], F32)
            s.sb_one1 = _sba("pl_one1", [1, 128], F32)
    if store_ln:
        s.ln_dram = _dra("pl_ln_dram", [n_rows, D], F32)

    d_in = p.sem("d_in")
    d_o = p.sem("d_o")

    p.dma("sp", s.sb_id32[:, :], ident_dram[:, :], d_in)
    if router:
        p.dma("sp", s.sb_wr[:, :], w_router_dram[:, :], d_in)
    if with_rms:
        p.dma("sp", s.sb_nsc[:, :], w_scale_dram.ap().rearrange("(i p) -> p i", p=128), d_in,
              allow_slow_non_contiguous=True)
        e_nsc = None
    else:
        e_nsc = p.op("dve", lambda E: E.memset(s.sb_nsc[:, :], 1.0), inc=True)
    if router and router_bias:
        p.dma("sp", s.sb_rbr[:, :], router_b_dram.ap().rearrange("(o e) -> o e", o=1), d_in)
    e_in0 = p.ev("d_in")
    s.e_in0 = e_in0
    p.wait("sp", e_in0)

    ev_rb = None
    if router:
        prev_ev = None
        for dc in range(8):
            et = p.op("pe", lambda E, dc=dc: E.transpose(s.ps_t[:, 0:8], s.sb_wr[0:8, dc * 128:(dc + 1) * 128], s.sb_id32[0:8, 0:8]),
                      wait=(e_in0,) if dc == 0 else (prev_ev,), inc=True)
            prev_ev = p.op("dve", lambda E, dc=dc: E.tensor_scalar(s.sb_wrT[:, dc * 8:(dc + 1) * 8], s.ps_t[:, 0:8], s.sb_nsc[:, dc:dc + 1], None, ALU.mult),
                           wait=(et, e_nsc) if dc == 0 else (et,), inc=True)
        s.ev_wrT = prev_ev
        if router_bias:
            e_o1 = p.op("dve", lambda E: E.memset(s.sb_one1[:, :], 1.0), inc=True)
            e_bm = p.op("pe", lambda E: E.matmul(s.ps_r[:, 0:8], s.sb_one1[:, :], s.sb_rbr[:, :], start=True, stop=True),
                        wait=(e_o1, e_in0, prev_ev), inc=True)
            ev_rb = p.op("dve", lambda E: E.tensor_copy(s.sb_rb[:, :], s.ps_r[:, 0:8]), wait=(e_bm,), inc=True)

    s.ev_w8 = []
    s.ev_lnT = []
    pv = {}
    for i in range(NTL):
        tile_dep = p.dma("sp", s.sb_x[:, :], x_in[128 * i:128 * (i + 1), :], d_in,
                         wait=(pv.get("e_xc"),))
        e1 = p.op("dve", lambda E: E.reduce_sum(s.sb_stat[:, 0:1], s.sb_x[:, :], axis=AX.X),
                  wait=(tile_dep, pv.get("e_xc")), inc=True)
        e2 = p.op("dve", lambda E: E.tensor_scalar(s.sb_stat[:, 0:1], s.sb_stat[:, 0:1], -1.0 / D, None, ALU.mult),
                  wait=(e1,), inc=True)
        e_xc = p.op("act", lambda E: E.activation(s.sb_scr[:, :], s.sb_x[:, :], AF.Identity, bias=s.sb_stat[:, 0:1]),
                    wait=(e2,), inc=True)
        e_v = p.op("act", lambda E: E.activation(s.sb_dump[:, :], s.sb_scr[:, :], AF.Square, accum_out=s.sb_stat[:, 1:2]),
                   wait=(e_xc,), inc=True)
        e5 = p.op("dve", lambda E: E.tensor_scalar(s.sb_stat[:, 1:2], s.sb_stat[:, 1:2], 1.0 / D, 1e-5, ALU.mult, ALU.add),
                  wait=(e_v,), inc=True)
        e6 = p.op("act", lambda E: E.activation(s.sb_stat[:, 2:3], s.sb_stat[:, 1:2], AF.Sqrt),
                  wait=(e5,), inc=True)
        e7 = p.op("dve", lambda E: E.reciprocal(s.sb_stat[:, 3:4], s.sb_stat[:, 2:3]), wait=(e6,), inc=True)
        e_ln = p.op("dve", lambda E: E.tensor_scalar(s.sb_ln[:, :], s.sb_scr[:, :], s.sb_stat[:, 3:4], None, ALU.mult),
                    wait=(e7, pv.get("lnstore"), pv.get("e_lnsq"), pv.get("e_lastT")), inc=True)
        lnstore = p.dma("act", s.ln_dram[128 * i:128 * (i + 1), :], s.sb_ln[:, :], d_o, wait=(e_ln,)) if store_ln else None
        if with_rms:
            e_lnsq = p.op("act", lambda E: E.activation(s.sb_dump[:, :], s.sb_ln[:, :], AF.Square, accum_out=s.sb_stat[:, 5:6]),
                          wait=(e_ln,), inc=True)
            e_ms = p.op("dve", lambda E: E.tensor_scalar(s.sb_stat[:, 5:6], s.sb_stat[:, 5:6], 1.0 / D, 1e-6, ALU.mult, ALU.add),
                        wait=(e_lnsq,), inc=True)
            e_sd = p.op("act", lambda E: E.activation(s.sb_stat[:, 6:7], s.sb_stat[:, 5:6], AF.Sqrt), wait=(e_ms,), inc=True)
            e_rr = p.op("dve", lambda E: E.reciprocal(s.sb_stat[:, 7:8], s.sb_stat[:, 6:7]), wait=(e_sd,), inc=True)
            e_xnp = p.op("dve", lambda E: E.tensor_scalar(s.sb_xnp[:, :], s.sb_ln[:, :], s.sb_stat[:, 7:8], None, ALU.mult),
                         wait=(e_rr, pv.get("e_lastT")), inc=True)
            src_xn = s.sb_xnp
        else:
            e_lnsq = None
            e_xnp = e_ln
            src_xn = s.sb_ln
        em = None
        evA = evB = None
        for dc in range(8):
            wl = [e_xnp] if dc == 0 else [e_xnp, evA, evB]
            et = p.op("pe", lambda E, dc=dc: E.transpose(s.ps_t[:, 0:128], src_xn[:, dc * 128:(dc + 1) * 128], s.sb_id32[:, :]),
                      wait=tuple(wl), inc=True)
            evA = p.op("dve", lambda E, dc=dc, i=i: E.tensor_scalar(
                s.sb_xnT[:, dc * n_rows + 128 * i: dc * n_rows + 128 * (i + 1)],
                s.ps_t[:, 0:128], s.sb_nsc[:, dc:dc + 1], None, ALU.mult), wait=(et,), inc=True)
            if router:
                evB = p.op("dve", lambda E: E.tensor_copy(s.sb_xnpT[:, :], s.ps_t[:, 0:128]), wait=(et,), inc=True)
                em = p.op("pe", lambda E, dc=dc: E.matmul(s.ps_r[:, 0:8], s.sb_xnpT[:, :], s.sb_wrT[:, dc * 8:(dc + 1) * 8],
                                                          start=(dc == 0), stop=(dc == 7)),
                          wait=(evB, s.ev_wrT, ev_rb) if dc == 0 else (evB,), inc=(dc == 7))
            else:
                evB = evA
        e_lastT = et
        s.ev_lnT.append(evA)
        if router:
            guard = (pv.get("e_w8"),)
            e_m1n = None
            if router_bias:
                e_gs = p.op("act", lambda E: E.activation(s.sb_g[:, :], s.ps_r[:, 0:8], AF.Copy), wait=(em, ev_rb) + guard, inc=True)
                e_gb = p.op("dve", lambda E: E.tensor_add(s.sb_g[:, :], s.sb_g[:, :], s.sb_rb[:, :]), wait=(e_gs,) + guard, inc=True)
                e_m1n = p.op("dve", lambda E: E.reduce_max(s.sb_stat[:, 8:9], s.sb_g[:, :], axis=AX.X, negate=True),
                             wait=(e_gb,), inc=True)
            else:
                e_m1n = p.op("dve", lambda E: E.reduce_max(s.sb_stat[:, 8:9], s.ps_r[:, 0:8], axis=AX.X, negate=True),
                             wait=(em,) + guard, inc=True)
                e_gs = p.op("act", lambda E: E.activation(s.sb_g[:, :], s.ps_r[:, 0:8], AF.Copy), wait=(em,) + guard, inc=True)
            e_m1 = p.op("dve", lambda E: E.tensor_scalar(s.sb_stat[:, 9:10], s.sb_stat[:, 8:9], -1.0, None, ALU.mult),
                        wait=(e_m1n,), inc=True)
            e_eq = p.op("dve", lambda E: E.tensor_scalar(s.sb_gt[:, :], s.sb_g[:, :], s.sb_stat[:, 9:10], None, ALU.is_ge),
                        wait=(e_m1, e_gs), inc=True)
            e_s2 = p.op("dve", lambda E: E.scalar_tensor_tensor(s.sb_gt[:, :], s.sb_gt[:, :], -1e9, s.sb_g[:, :], ALU.mult, ALU.add),
                        wait=(e_eq,), inc=True)
            e_m2 = p.op("dve", lambda E: E.reduce_max(s.sb_stat[:, 10:11], s.sb_gt[:, :], axis=AX.X), wait=(e_s2,), inc=True)
            e_msk = p.op("dve", lambda E: E.tensor_scalar(s.sb_gt[:, :], s.sb_g[:, :], s.sb_stat[:, 10:11], None, ALU.is_ge),
                         wait=(e_m2,), inc=True)
            e_e1 = p.op("act", lambda E: E.activation(s.sb_g[:, :], s.sb_g[:, :], AF.Exp, bias=s.sb_stat[:, 8:9]), wait=(e_msk,), inc=True)
            e_p = p.op("dve", lambda E: E.tensor_mul(s.sb_g[:, :], s.sb_g[:, :], s.sb_gt[:, :]), wait=(e_e1,), inc=True)
            e_z = p.op("dve", lambda E: E.reduce_sum(s.sb_stat[:, 11:12], s.sb_g[:, :], axis=AX.X), wait=(e_p,), inc=True)
            e_rz = p.op("dve", lambda E: E.reciprocal(s.sb_stat[:, 12:13], s.sb_stat[:, 11:12]), wait=(e_z,), inc=True)
            e_w8 = p.op("dve", lambda E, i=i: E.tensor_scalar(s.sb_w8[:, 8 * i:8 * (i + 1)], s.sb_g[:, :], s.sb_stat[:, 12:13], w8_scale,
                                                              ALU.mult, ALU.mult),
                        wait=(e_rz,), inc=True)
            s.ev_w8.append(e_w8)
        else:
            e_w8 = None
        pv = dict(e_xc=e_xc, lnstore=lnstore, e_lnsq=e_lnsq, e_lastT=e_lastT, e_w8=e_w8)

    s._pv_end = pv
    s.ev_dve_done = p.ev("c_dve")
    s.ln_all_stored = p.ev("d_o")
    return s



def build_moe_ffn(with_rms, with_skip, split_ab, fz=None, xio=None, pre="mo_"):
    p, nc, pin, dramt, sbuf, psum, fin = _env(fz, pre)

    NTB = 4
    TBT = NT // NTB
    TB = 128 * TBT
    E_, HID = 8, 2048

    x_in = xio["x"] if xio else pin("x", [TOK_PER_CORE, D], F32)
    w_router = pin("w_router", [E_, D], F32)
    if split_ab:
        w_a = pin("w_a", [E_, HID, D], F32)
        w_b = pin("w_b", [E_, HID, D], F32)
        a_bf = dramt("a_bf", [E_, HID, D], BF16)
        b_bf = dramt("b_bf", [E_, HID, D], BF16)
    else:
        w_up = pin("w_up", [E_, 2 * HID, D], F32)
        up_bf = dramt("up_bf", [E_, 2 * HID, D], BF16)
    w_down = pin("w_down", [E_, D, HID], F32)
    dn_bf = dramt("dn_bf", [E_, D, HID], BF16)
    w_scale = pin("norm_scale", [D], F32) if with_rms else None
    ident = pin("ident", [128, 128], F32)
    x_out = xio["x_out"] if xio else nc.dram_tensor("x_out", [TOK_PER_CORE, D], F32, kind="ExternalOutput")

    sb_upT = sbuf("sb_upT", [128, 8 * 2 * HID], BF16)
    sb_dnT = sbuf("sb_dnT", [128, 16 * D], BF16)
    sb_hT = sbuf("sb_hT", [128, 16 * TB], BF16)
    sb_sg = sbuf("sb_sg", [128, TB], BF16)
    sb_acc = sbuf("sb_acc", [128, TBT * D], F32)
    sb_tmp = sbuf("sb_tmp", [128, 512], F32)
    sb_out = sbuf("sb_out", [128, D], F32)
    sb_xc = sbuf("sb_xc", [128, D], F32)
    sb_lnld = sbuf("sb_lnld", [128, D], F32)

    ps_u = psum("ps_u", [128, 512], F32)
    ps_g = psum("ps_g", [128, 512], F32)
    ps_d = psum("ps_d", [128, 512], F32)

    d_in = p.sem("d_in")
    d_w = p.sem("d_w")
    d_c = p.sem("d_c")
    d_o = p.sem("d_o")

    for e in range(E_):
        if split_ab:
            p.dma("pool", a_bf[e], w_a[e], d_c)
            p.dma("pool", b_bf[e], w_b[e], d_c)
        else:
            p.dma("pool", up_bf[e], w_up[e], d_c)
        p.dma("pool", dn_bf[e], w_down[e], d_c)
    cast_done = p.ev("d_c")

    s = emit_prelude(p, nc, x_in, alloc=(sbuf, psum, dramt), with_rms=with_rms, router=True, store_ln=with_skip,
                     w_scale_dram=w_scale, w_router_dram=w_router, ident_dram=ident)

    ev_block_out = None
    ev_last_up_use = None
    ev_last_dn_use = None
    e_h = None
    ev_d = None
    for tb in range(NTB):
        e_z0 = p.op("dve", lambda E: E.memset(sb_acc[:, :], 0.0),
                    wait=(s.ev_dve_done,) if tb == 0 else (ev_block_out,), inc=True)
        last_acc = e_z0
        last_evac_read = None
        for e in range(E_):
            p.wait("sp", ev_last_up_use)
            p.wait("sp", ev_last_dn_use)
            if tb == 0 and e == 0:
                p.wait("sp", cast_done)
            for dc in range(8):
                if split_ab:
                    p.dma("sp", sb_upT[:, dc * 2 * HID: dc * 2 * HID + HID],
                          b_bf[e][:, dc * 128:(dc + 1) * 128], d_w, transpose=True)
                    p.dma("sp", sb_upT[:, dc * 2 * HID + HID: (dc + 1) * 2 * HID],
                          a_bf[e][:, dc * 128:(dc + 1) * 128], d_w, transpose=True)
                else:
                    p.dma("sp", sb_upT[:, dc * 2 * HID:(dc + 1) * 2 * HID],
                          up_bf[e][:, dc * 128:(dc + 1) * 128], d_w, transpose=True)
            for hc in range(16):
                p.dma("sp", sb_dnT[:, hc * D:(hc + 1) * D], dn_bf[e][:, hc * 128:(hc + 1) * 128], d_w, transpose=True)
            e_wload = p.ev("d_w")

            for j in range(16):
                ev_u = ev_gt = None
                for dc in range(8):
                    xr = s.sb_xnT[:, dc * TOK_PER_CORE + TB * tb: dc * TOK_PER_CORE + TB * (tb + 1)]
                    wl = []
                    if dc == 0:
                        wl.append(e_wload)
                        if tb == 0 and e == 0 and j == 0:
                            wl.append(s.ev_dve_done)
                        if j > 0:
                            wl.append(e_h)
                    ev_u = p.op("pe", lambda E, dc=dc, j=j, xr=xr: E.matmul(
                        ps_u[:, 0:TB], sb_upT[:, dc * 2 * HID + 128 * j: dc * 2 * HID + 128 * (j + 1)], xr,
                        start=(dc == 0), stop=(dc == 7)),
                        wait=tuple(wl), inc=(dc == 7))
                for dc in range(8):
                    xr = s.sb_xnT[:, dc * TOK_PER_CORE + TB * tb: dc * TOK_PER_CORE + TB * (tb + 1)]
                    ev_gt = p.op("pe", lambda E, dc=dc, j=j, xr=xr: E.matmul(
                        ps_g[:, 0:TB], sb_upT[:, dc * 2 * HID + HID + 128 * j: dc * 2 * HID + HID + 128 * (j + 1)], xr,
                        start=(dc == 0), stop=(dc == 7)),
                        inc=(dc == 7))
                e_sg = p.op("act", lambda E: E.activation(sb_sg[:, :], ps_g[:, 0:TB], AF.Silu),
                            wait=(ev_gt, e_h), inc=True)
                e_h = p.op("dve", lambda E, j=j: E.tensor_tensor(sb_hT[:, TB * j:TB * (j + 1)], ps_u[:, 0:TB], sb_sg[:, :], ALU.mult),
                           wait=(ev_u, e_sg), inc=True)
            e_hT_done = e_h
            ev_last_up_use = e_h

            for tc in range(TBT):
                for do in range(2):
                    for hc in range(16):
                        wl = [e_hT_done, last_evac_read] if hc == 0 else []
                        ev_d = p.op("pe", lambda E, hc=hc, tc=tc, do=do: E.matmul(
                            ps_d[:, :], sb_hT[:, TB * hc + 128 * tc: TB * hc + 128 * (tc + 1)],
                            sb_dnT[:, hc * D + 512 * do: hc * D + 512 * (do + 1)],
                            start=(hc == 0), stop=(hc == 15)),
                            wait=tuple(wl), inc=(hc == 15))
                    wcol = s.sb_w8[:, (TBT * tb + tc) * 8 + e: (TBT * tb + tc) * 8 + e + 1]
                    e_t = p.op("act", lambda E, wcol=wcol: E.activation(sb_tmp[:, :], ps_d[:, :], AF.Copy, scale=wcol),
                               wait=(ev_d, s.ev_w8[TBT * tb + tc], last_evac_read), inc=True)
                    e_a = p.op("dve", lambda E, tc=tc, do=do: E.tensor_add(
                        sb_acc[:, tc * D + 512 * do: tc * D + 512 * (do + 1)],
                        sb_acc[:, tc * D + 512 * do: tc * D + 512 * (do + 1)], sb_tmp[:, :]),
                        wait=(e_t, last_acc), inc=True)
                    last_acc = e_a
                    last_evac_read = e_a
            ev_last_dn_use = ev_d

        prev_store = None
        for tc in range(TBT):
            ti = TBT * tb + tc
            e_lx = p.dma("sp", sb_xc[:, :], x_in[128 * ti:128 * (ti + 1), :], p.sem("d_x2"),
                         wait=(last_acc, prev_store))
            if with_skip:
                e_ll = p.dma("sp", sb_lnld[:, :], s.ln_dram[128 * ti:128 * (ti + 1), :], p.sem("d_x2"),
                             wait=(s.ln_all_stored,))
                joint = e_ll
            else:
                e_ll = None
                joint = e_lx
            e_c1 = p.op("dve", lambda E, tc=tc: E.scalar_tensor_tensor(
                sb_out[:, :], sb_acc[:, tc * D:(tc + 1) * D], 0.5, sb_xc[:, :], ALU.mult, ALU.add),
                wait=(joint, last_acc), inc=True)
            if with_skip:
                e_c2 = p.op("dve", lambda E: E.scalar_tensor_tensor(sb_out[:, :], sb_lnld[:, :], 0.5, sb_out[:, :], ALU.mult, ALU.add),
                            wait=(e_ll, e_c1), inc=True)
            else:
                e_c2 = e_c1
            prev_store = p.dma("act", x_out[128 * ti:128 * (ti + 1), :], sb_out[:, :], d_o, wait=(e_c2,))
        ev_block_out = prev_store

    p.wait("act", p.ev("d_o"))
    return fin()



def build_l1_ffn(fz=None, xio=None, pre="f1_"):
    p, nc, pin, dramt, sbuf, psum, fin = _env(fz, pre)
    E_, R = 8, 16

    x_in = xio["x"] if xio else pin("x", [TOK_PER_CORE, D], F32)
    base_w = pin("base_w", [D, D], F32)
    base_b = pin("base_b", [D], F32)
    lora_A = pin("lora_A", [E_, D, R], F32)
    lora_B = pin("lora_B", [E_, R, D], F32)
    w_router = pin("w_router", [E_, D], F32)
    router_b = pin("router_b", [E_], F32)
    ident = pin("ident", [128, 128], F32)
    x_out = xio["x_out"] if xio else nc.dram_tensor("x_out", [TOK_PER_CORE, D], F32, kind="ExternalOutput")

    bw_bf = dramt("bw_bf", [D, D], BF16)
    la_bf = dramt("la_bf", [E_, D, R], BF16)
    lb_bf = dramt("lb_bf", [E_, R, D], BF16)

    sb_bwT = sbuf("sb_bwT", [128, 8 * D], BF16)
    sb_A = sbuf("sb_A", [128, 8 * E_ * R], BF16)
    sb_Bm = sbuf("sb_Bm", [16, E_ * D], BF16)
    sb_ax = sbuf("sb_ax", [128, E_ * R], BF16)
    sb_axT = sbuf("sb_axT", [16, E_ * 128], BF16)
    sb_bb = sbuf("sb_bb", [128, D], F32)
    sb_bbr = sbuf("sb_bbr", [1, D], F32)
    sb_one1 = sbuf("sb_one1b", [1, 128], F32)
    sb_xc = sbuf("sb_xc", [128, D], F32)
    sb_t1 = sbuf("sb_t1", [128, D], F32)
    sb_out = sbuf("sb_out", [128, D], F32)
    sb_id16 = sbuf("sb_id16", [128, 128], BF16)

    ps_o = psum("ps_o", [128, 512], F32)
    ps_o2 = psum("ps_o2", [128, 512], F32)
    ps_ax = psum("ps_ax", [128, 512], F32)
    ps_tr = psum("ps_tr", [128, 512], BF16)

    d_in = p.sem("d_in")
    d_w = p.sem("d_w")
    d_c = p.sem("d_c")
    d_o = p.sem("d_o")

    p.dma("pool", bw_bf[:, :], base_w[:, :], d_c)
    p.dma("pool", la_bf.ap(), lora_A.ap(), d_c)
    p.dma("pool", lb_bf.ap(), lora_B.ap(), d_c)
    cast_done = p.ev("d_c")

    s = emit_prelude(p, nc, x_in, alloc=(sbuf, psum, dramt), with_rms=False, router=True, router_bias=True,
                     store_ln=False, w8_scale=2.0,
                     w_router_dram=w_router, router_b_dram=router_b, ident_dram=ident)

    p.wait("sp", cast_done)
    for dc in range(8):
        p.dma("sp", sb_bwT[:, dc * D:(dc + 1) * D], bw_bf[:, dc * 128:(dc + 1) * 128], d_w, transpose=True)
    for dc in range(8):
        for e in range(E_):
            p.dma("sp", sb_A[:, (dc * E_ + e) * R:(dc * E_ + e + 1) * R], la_bf[e][dc * 128:(dc + 1) * 128, :], d_w)
    for e in range(E_):
        p.dma("sp", sb_Bm[:, e * D:(e + 1) * D], lb_bf[e][:, :], d_w)
    e_wload = p.ev("d_w")

    e_id = p.op("dve", lambda E: E.tensor_copy(sb_id16[:, :], s.sb_id32[:, :]), wait=(s.e_in0,), inc=True)
    p.dma("sp", sb_bbr[:, :], base_b.ap().rearrange("(o d) -> o d", o=1), p.sem("d_bb"))
    e_bbl = p.ev("d_bb")
    e_o1 = p.op("dve", lambda E: E.memset(sb_one1[:, :], 1.0), inc=True)
    e_bm = p.op("pe", lambda E: E.matmul(ps_o[:, :], sb_one1[:, :], sb_bbr[0:1, 0:512], start=True, stop=True),
                wait=(e_o1, e_bbl, s.ev_dve_done), inc=True)
    e_bm2 = p.op("pe", lambda E: E.matmul(ps_o2[:, :], sb_one1[:, :], sb_bbr[0:1, 512:1024], start=True, stop=True),
                 wait=(), inc=True)
    e_bb1 = p.op("dve", lambda E: E.tensor_copy(sb_bb[:, 0:512], ps_o[:, :]), wait=(e_bm,), inc=True)
    e_bb2 = p.op("dve", lambda E: E.tensor_copy(sb_bb[:, 512:1024], ps_o2[:, :]), wait=(e_bm2,), inc=True)

    pv_store = None
    pv_evac = None
    pv_axT_read = e_bb2
    for i in range(NT):
        lnT_ = [s.sb_xnT[:, dc * TOK_PER_CORE + 128 * i: dc * TOK_PER_CORE + 128 * (i + 1)] for dc in range(8)]
        ev_mm = None
        for e in range(E_):
            for dc in range(8):
                wl = []
                if e == 0 and dc == 0:
                    wl = [e_wload, s.ev_dve_done, pv_evac] if i == 0 else [pv_evac]
                ev_mm = p.op("pe", lambda E, dc=dc, e=e, xr=lnT_[dc]: E.matmul(
                    ps_ax[:, e * R:(e + 1) * R], xr, sb_A[:, (dc * E_ + e) * R:(dc * E_ + e + 1) * R],
                    start=(dc == 0), stop=(dc == 7)),
                    wait=tuple(wl), inc=(dc == 7 and e == E_ - 1))
        ev_axs = []
        for e in range(E_):
            wcol = s.sb_w8[:, 8 * i + e: 8 * i + e + 1]
            e_axs = p.op("act", lambda E, e=e, wcol=wcol: E.activation(
                sb_ax[:, e * R:(e + 1) * R], ps_ax[:, e * R:(e + 1) * R], AF.Copy, scale=wcol),
                wait=(ev_mm, s.ev_w8[i], pv_axT_read) if e == 0 else (), inc=True)
            ev_axs.append(e_axs)
        pv_evac = ev_axs[-1]
        ev_trs = []
        prev_tr_evac = None
        for e in range(E_):
            e_tr = p.op("pe", lambda E, e=e: E.transpose(ps_tr[0:16, 0:128], sb_ax[:, e * R:(e + 1) * R], sb_id16[:, :]),
                        wait=(ev_axs[e], e_id) if e == 0 else (ev_axs[e], prev_tr_evac), inc=True)
            prev_tr_evac = p.op("dve", lambda E, e=e: E.tensor_copy(sb_axT[:, e * 128:(e + 1) * 128], ps_tr[0:16, 0:128]),
                                wait=(e_tr,), inc=True)
            ev_trs.append(prev_tr_evac)
        ev_half = []
        for half in range(2):
            pso = ps_o if half == 0 else ps_o2
            ev_o = None
            for dc in range(8):
                wl = []
                if dc == 0 and i == 0 and half == 0:
                    wl = [e_bb1, e_bb2]
                ev_o = p.op("pe", lambda E, dc=dc, pso=pso, half=half, xr=lnT_[dc]: E.matmul(
                    pso[:, :], xr, sb_bwT[:, dc * D + 512 * half: dc * D + 512 * (half + 1)],
                    start=(dc == 0), stop=False),
                    wait=tuple(wl), inc=False)
            for e in range(E_):
                ev_o = p.op("pe", lambda E, e=e, pso=pso, half=half: E.matmul(
                    pso[:, :], sb_axT[:, e * 128:(e + 1) * 128], sb_Bm[:, e * D + 512 * half: e * D + 512 * (half + 1)],
                    start=False, stop=(e == E_ - 1)),
                    wait=(ev_trs[e],), inc=(e == E_ - 1))
            ev_half.append(ev_o)
        pv_axT_read = ev_half[1]
        e_lx = p.dma("sp", sb_xc[:, :], x_in[128 * i:128 * (i + 1), :], p.sem("d_x2"), wait=(pv_store,))
        e_ev1 = p.op("dve", lambda E: E.tensor_add(sb_t1[:, 0:512], ps_o[:, :], sb_bb[:, 0:512]),
                     wait=(ev_half[0], pv_store) if i else (ev_half[0],), inc=True)
        e_ev2 = p.op("dve", lambda E: E.tensor_add(sb_t1[:, 512:1024], ps_o2[:, :], sb_bb[:, 512:1024]),
                     wait=(ev_half[1],), inc=True)
        e_c1 = p.op("dve", lambda E: E.scalar_tensor_tensor(sb_out[:, :], sb_t1[:, :], 0.5, sb_xc[:, :], ALU.mult, ALU.add),
                    wait=(e_lx, e_ev1, e_ev2), inc=True)
        pv_store = p.dma("act", x_out[128 * i:128 * (i + 1), :], sb_out[:, :], d_o, wait=(e_c1,))
        p.wait("pe", e_ev1)
        p.wait("pe", e_ev2)

    p.wait("act", p.ev("d_o"))
    return fin()



def build_l0_attn(fz=None, xio=None, pre="a0_"):
    p, nc, pin, dramt, sbuf, psum, fin = _env(fz, pre)
    NR, NQ = 4096, 2048
    NTK = NR // 128
    NTQ = NQ // 128

    xb = xio["xb"] if xio else pin("xb", [NR, D], F32)
    kpos = pin("kpos", [NR], F32)
    w_qkv = pin("w_qkv", [D, 3 * D], F32)
    b_qkv = pin("b_qkv", [3 * D], F32)
    w_o = pin("w_o", [D, D], F32)
    b_o = pin("b_o", [D], F32)
    ident = pin("ident", [128, 128], F32)
    x_out = xio["x_out"] if xio else nc.dram_tensor("x_out", [NQ, D], F32, kind="ExternalOutput")

    wqkv_bf = dramt("wqkv_bf", [D, 3 * D], BF16)
    wo_bf = dramt("wo_bf", [D, D], BF16)
    kT_sp = dramt("kT_sp", [D, NR], BF16)
    vv_sp = dramt("vv_sp", [NR, D], BF16)
    qT_sp = dramt("qT_sp", [D, NQ], BF16)

    slab = sbuf("slab", [128, 8 * NR], BF16)
    sb_S = sbuf("sb_S", [128, NR], F32)
    sb_msk = sbuf("sb_msk", [128, NR], F32)
    sb_P = sbuf("sb_P", [128, NR], BF16)
    sb_PT = sbuf("sb_PT", [128, 2 * 512], BF16)
    sb_kT = sbuf("sb_kT", [128, NR], BF16)
    sb_qTs = sbuf("sb_qTs", [128, 128], BF16)
    sb_oq = sbuf("sb_oq", [128, D], F32)
    sb_oqb = sbuf("sb_oqb", [128, D], BF16)
    sb_oT = sbuf("sb_oT", [128, 8 * 128], BF16)
    sb_kpB = sbuf("sb_kpB", [128, NR], F32)
    sb_cut = sbuf("sb_cut", [128, NTQ], F32)
    sb_wnat = sbuf("sb_wnat", [128, 2 * 4096], BF16)
    sb_wo = sbuf("sb_wo", [128, 8 * D], BF16)
    sb_stg = sbuf("sb_stg", [128, 2 * 512], BF16)
    sb_res = sbuf("sb_res", [128, D], F32)
    sb_bq = sbuf("sb_bq", [128, 24], F32)
    sb_bv = sbuf("sb_bv", [128, D], F32)
    sb_bo = sbuf("sb_bo", [128, D], F32)
    sb_xo = sbuf("sb_xo", [128, D], F32)
    sb_st = sbuf("sb_st", [128, 8], F32)
    sb_id16 = sbuf("sb_id16", [128, 128], BF16)

    ps_a = psum("ps_a", [128, 512], F32)
    ps_b = psum("ps_b", [128, 512], F32)
    ps_av = psum("ps_av", [128, 512], F32)
    ps_tr = psum("ps_tr", [128, 512], BF16)
    ps_tr2 = psum("ps_tr2", [128, 512], BF16)

    d_c = p.sem("d_c")
    d_w = p.sem("d_w")
    d_cn = p.sem("d_cn")
    d_sp = p.sem("d_sp")
    d_kv = p.sem("d_kv")
    d_x2 = p.sem("d_x2")
    d_o = p.sem("d_o")

    p.dma("pool", wqkv_bf.ap(), w_qkv.ap(), d_c)
    p.dma("pool", wo_bf.ap(), w_o.ap(), d_c)
    cast_done = p.ev("d_c")

    s = emit_prelude(p, nc, xb, alloc=(sbuf, psum, dramt), router=False, store_ln=False, ident_dram=ident,
                     n_rows=NR, xnT_buf=slab)

    p.dma("act", sb_kpB[:, :], bass.AP(kpos, 0, [[0, 128], [1, NR]]), d_cn)
    p.dma("act", sb_cut[:, :], kpos.ap()[0:NQ].rearrange("(i p) -> p i", p=128), d_cn,
          allow_slow_non_contiguous=True)
    p.dma("act", sb_bq[:, :], b_qkv.ap().rearrange("(i p) -> p i", p=128), d_cn,
          allow_slow_non_contiguous=True)
    p.dma("act", sb_bv[:, :], bass.AP(b_qkv, 2048, [[0, 128], [1, D]]), d_cn)
    p.dma("act", sb_bo[:, :], bass.AP(b_o, 0, [[0, 128], [1, D]]), d_cn)
    e_k0 = p.ev("d_cn")
    ev_const = (e_k0, e_k0)
    psum_last = {0: None, 1: None}
    e_id16 = p.op("dve", lambda E: E.tensor_copy(sb_id16[:, :], s.sb_id32[:, :]), wait=(s.e_in0,), inc=True)

    p.wait("sp", cast_done)
    for dc in range(8):
        p.dma("sp", sb_wo[:, dc * D:(dc + 1) * D], wo_bf[dc * 128:(dc + 1) * 128, :], p.sem("d_wo"))
    e_wo = p.ev("d_wo")

    stg_last = {0: None, 1: None}
    wnat_use = {0: None, 1: None}
    ev_wl = {}
    for oc in range(16):
        par = oc % 2
        ev_wl[oc] = p.dma("sp", sb_wnat[:, par * 4096:par * 4096 + 1024].rearrange("p (i c) -> p i c", c=128),
                          wqkv_bf.ap().rearrange("(i p) c -> p i c", p=128)[:, :, 128 * oc:128 * (oc + 1)], p.sem("d_w%d" % par),
                          wait=(wnat_use[par],))
        ntf = 4 if oc < 8 else 8
        for tf in range(ntf):
            psx = ps_a if tf % 2 == 0 else ps_b
            ev_mm = None
            for dc in range(8):
                wl2 = []
                if dc == 0:
                    wl2 = [ev_wl[oc], psum_last[tf % 2]]
                    if oc == 0 and tf == 0:
                        wl2 += [s.ev_dve_done] + list(ev_const)
                ev_mm = p.op("pe", lambda E, dc=dc, oc=oc, tf=tf, par=par, psx=psx: E.matmul(
                    psx[:, :], sb_wnat[:, par * 4096 + dc * 128:par * 4096 + dc * 128 + 128],
                    slab[:, dc * NR + 512 * tf: dc * NR + 512 * (tf + 1)],
                    start=(dc == 0), stop=(dc == 7)),
                    wait=tuple([w for w in wl2 if w]), inc=(dc == 7))
            spar = tf % 2
            e_ev = p.op("act", lambda E, oc=oc, spar=spar, psx=psx: E.activation(
                sb_stg[:, spar * 512:(spar + 1) * 512], psx[:, :], AF.Identity, bias=sb_bq[:, oc:oc + 1]),
                wait=tuple([w for w in (ev_mm, e_k0, stg_last[spar]) if w]), inc=True)
            psum_last[tf % 2] = e_ev
            if oc < 8:
                dst = qT_sp[128 * oc:128 * (oc + 1), 512 * tf:512 * (tf + 1)]
            else:
                dst = kT_sp[128 * (oc - 8):128 * (oc - 7), 512 * tf:512 * (tf + 1)]
            e_st = p.dma("act", dst, sb_stg[:, spar * 512:(spar + 1) * 512], p.sem("d_sp%d" % spar), wait=(e_ev,))
            stg_last[spar] = e_st
        wnat_use[par] = ev_mm

    ev_vw = {}
    for vh in range(2):
        ev_vw[vh] = p.dma("sp", sb_wnat[:, vh * 4096:(vh + 1) * 4096].rearrange("p (i c) -> p i c", c=512),
                          wqkv_bf.ap().rearrange("(i p) c -> p i c", p=128)[:, :, 2048 + 512 * vh:2048 + 512 * (vh + 1)], p.sem("d_w%d" % vh),
                          wait=(wnat_use[vh],))
    for tc in range(NTK):
        for vh in range(2):
            psx = ps_a if vh == 0 else ps_b
            ev_mm = None
            for dc in range(8):
                wl2 = []
                if dc == 0:
                    wl2 = [ev_vw[vh], psum_last[vh]]
                ev_mm = p.op("pe", lambda E, dc=dc, tc=tc, vh=vh, psx=psx: E.matmul(
                    psx[:, :], slab[:, dc * NR + 128 * tc: dc * NR + 128 * (tc + 1)],
                    sb_wnat[:, vh * 4096 + dc * 512: vh * 4096 + dc * 512 + 512],
                    start=(dc == 0), stop=(dc == 7)),
                    wait=tuple([w for w in wl2 if w]), inc=(dc == 7))
            e_ev = p.op("dve", lambda E, vh=vh, psx=psx: E.tensor_add(sb_stg[:, vh * 512:(vh + 1) * 512], psx[:, :], sb_bv[:, vh * 512:(vh + 1) * 512]),
                        wait=tuple([w for w in (ev_mm, stg_last[vh]) if w]), inc=True)
            psum_last[vh] = e_ev
            e_st = p.dma("act", vv_sp[128 * tc:128 * (tc + 1), 512 * vh:512 * (vh + 1)],
                         sb_stg[:, vh * 512:(vh + 1) * 512], p.sem("d_sp%d" % vh), wait=(e_ev,))
            stg_last[vh] = e_st
    ev_proj_done = (p.ev("d_sp0"), p.ev("d_sp1"))

    p.wait("sp", ev_proj_done[0])
    p.wait("sp", ev_proj_done[1])
    p.wait("sp", s.ev_dve_done)
    for tc in range(NTK):
        p.dma("sp", slab[:, 1024 * tc:1024 * (tc + 1)], vv_sp[128 * tc:128 * (tc + 1), :], p.sem("d_vl"))
    e_vload = p.ev("d_vl")

    kv_last = {0: None}
    res_last = {0: None, 1: None}
    ev_msk = None
    ev_msk_prev_use = None
    prev_store = None
    prev_oq_written = None
    prev_oT_used = None
    for qt in range(NTQ):
        ev_msk = p.op("dve", lambda E, qt=qt: E.tensor_scalar(
            sb_msk[:, :], sb_kpB[:, :], sb_cut[:, qt:qt + 1], -1e9, ALU.is_gt, ALU.mult),
            wait=tuple([w for w in (ev_const[0], ev_const[1], ev_msk_prev_use) if w]), inc=True)
        for hp in range(8):
            par = 0
            e_kl = p.dma("sp", sb_kT[:, :], kT_sp[128 * hp:128 * (hp + 1), :], p.sem("d_kv0"),
                         wait=(kv_last[0],))
            e_ql = p.dma("sp", sb_qTs[:, :], qT_sp[128 * hp:128 * (hp + 1), 128 * qt:128 * (qt + 1)], p.sem("d_kv0"))
            e_kv = p.ev("d_kv0")
            for h2 in range(2):
                po = 64 * h2
                sc_evac = {0: psum_last[0], 1: psum_last[1]}
                prev_eex = locals().get("e_ex")
                for c in range(8):
                    cpar = c % 2
                    psx = ps_a if cpar == 0 else ps_b
                    wl2 = [e_kv] if c == 0 else []
                    if c == 0 and qt == 0 and hp == 0 and h2 == 0:
                        wl2.append(e_vload)
                    wl2.append(sc_evac[cpar])
                    ev_mm = p.op("pe", lambda E, c=c, po=po, psx=psx: E.matmul(
                        psx[:, :], sb_qTs[po:po + 64, :],
                        sb_kT[po:po + 64, 512 * c: 512 * (c + 1)],
                        start=True, stop=True),
                        wait=tuple([w for w in wl2 if w]), inc=True)
                    e_sv = p.op("dve", lambda E, c=c, psx=psx: E.scalar_tensor_tensor(
                        sb_S[:, 512 * c:512 * (c + 1)], psx[:, :], 0.125, sb_msk[:, 512 * c:512 * (c + 1)],
                        ALU.mult, ALU.add),
                        wait=(ev_mm, ev_msk) + ((prev_eex,) if c == 0 and prev_eex else ()), inc=True)
                    sc_evac[cpar] = e_sv
                psum_last[0], psum_last[1] = sc_evac[0], sc_evac[1]
                e_nm = p.op("dve", lambda E: E.reduce_max(sb_st[:, 1:2], sb_S[:, :], axis=AX.X, negate=True),
                            wait=(sc_evac[0], sc_evac[1]), inc=True)
                e_ex = p.op("act", lambda E: E.activation(sb_P[:, :], sb_S[:, :], AF.Exp, bias=sb_st[:, 1:2], accum_out=sb_st[:, 0:1]),
                            wait=(e_nm,), inc=True)
                e_rz = p.op("dve", lambda E: E.reciprocal(sb_st[:, 2:3], sb_st[:, 0:1]), wait=(e_ex,), inc=True)
                e_avmm = None
                pt_evac = {0: None, 1: None}
                for g in range(8):
                    gpar = g % 2
                    pstr = ps_tr if gpar == 0 else ps_tr2
                    for a in range(4):
                        wl2 = []
                        if a == 0:
                            wl2 = [e_ex, pt_evac[gpar]]
                        if qt == 0 and hp == 0 and h2 == 0 and g == 0 and a == 0:
                            wl2.append(e_id16)
                        ev_tr = p.op("pe", lambda E, g=g, a=a, pstr=pstr: E.transpose(
                            pstr[:, 128 * a:128 * (a + 1)], sb_P[:, 512 * g + 128 * a: 512 * g + 128 * (a + 1)], sb_id16[:, :]),
                            wait=tuple([w for w in wl2 if w]), inc=(a == 3))
                    e_pt = p.op("dve", lambda E, gpar=gpar, pstr=pstr: E.tensor_copy(sb_PT[:, gpar * 512:(gpar + 1) * 512], pstr[:, :]),
                                wait=(ev_tr, pt_evac[gpar]), inc=True)
                    pt_evac[gpar] = e_pt
                    for a in range(4):
                        tc4 = 4 * g + a
                        wl2 = [e_pt] if a == 0 else []
                        if g == 0 and a == 0:
                            wl2.append(prev_oq_written)
                        ev_avmm = p.op("pe", lambda E, g=g, a=a, tc4=tc4, hp=hp, h2=h2: E.matmul(
                            ps_av[:, 0:64], sb_PT[:, (g % 2) * 512 + 128 * a: (g % 2) * 512 + 128 * (a + 1)],
                            slab[:, 1024 * tc4 + 128 * hp + 64 * h2: 1024 * tc4 + 128 * hp + 64 * h2 + 64],
                            start=(g == 0 and a == 0), stop=(g == 7 and a == 3)),
                            wait=tuple([w for w in wl2 if w]), inc=(g == 7 and a == 3))
                h = 2 * hp + h2
                prev_oq_written = p.op("dve", lambda E, h=h: E.tensor_scalar(
                    sb_oq[:, 64 * h:64 * (h + 1)], ps_av[:, 0:64], sb_st[:, 2:3], None, ALU.mult),
                    wait=(ev_avmm, e_rz, prev_oT_used), inc=True)
                if h2 == 1:
                    kv_last[0] = ev_avmm
        ev_msk_prev_use = psum_last[0]
        e_cb = p.op("dve", lambda E: E.tensor_copy(sb_oqb[:, :], sb_oq[:, :]), wait=(prev_oq_written,), inc=True)
        ev_oT = []
        prev_tr = None
        for dc in range(8):
            e_tr = p.op("pe", lambda E, dc=dc: E.transpose(ps_tr[:, 128:256], sb_oqb[:, 128 * dc:128 * (dc + 1)], sb_id16[:, :]),
                        wait=(e_cb,) if dc == 0 else (prev_tr,), inc=True)
            prev_tr = p.op("dve", lambda E, dc=dc: E.tensor_copy(sb_oT[:, 128 * dc:128 * (dc + 1)], ps_tr[:, 128:256]),
                           wait=(e_tr,), inc=True)
            ev_oT.append(prev_tr)
        e_lx = p.dma("sp", sb_xo[:, :], xb[128 * qt:128 * (qt + 1), :], d_x2, wait=(prev_store,))
        ev_res = []
        for half in range(2):
            psx = ps_a if half == 0 else ps_b
            ev_mm = None
            for dc in range(8):
                wl2 = []
                if dc == 0:
                    wl2 = [ev_oT[-1], e_wo, psum_last[half]]
                ev_mm = p.op("pe", lambda E, dc=dc, half=half, psx=psx: E.matmul(
                    psx[:, :], sb_oT[:, 128 * dc:128 * (dc + 1)], sb_wo[:, dc * D + 512 * half: dc * D + 512 * (half + 1)],
                    start=(dc == 0), stop=(dc == 7)),
                    wait=tuple([w for w in wl2 if w]), inc=(dc == 7))
            e_r1 = p.op("dve", lambda E, half=half, psx=psx: E.tensor_add(
                sb_res[:, half * 512:(half + 1) * 512], psx[:, :], sb_bo[:, 512 * half:512 * (half + 1)]),
                wait=tuple([w for w in (ev_mm, res_last[half]) if w]), inc=True)
            e_r2 = p.op("dve", lambda E, half=half: E.tensor_add(
                sb_res[:, half * 512:(half + 1) * 512], sb_res[:, half * 512:(half + 1) * 512], sb_xo[:, 512 * half:512 * (half + 1)]),
                wait=(e_r1, e_lx), inc=True)
            psum_last[half] = e_r1
            ev_res.append(e_r2)
        e_st1 = p.dma("act", x_out[128 * qt:128 * (qt + 1), 0:512], sb_res[:, 0:512], p.sem("d_ro0"), wait=(ev_res[0], ev_res[1]))
        e_st2 = p.dma("act", x_out[128 * qt:128 * (qt + 1), 512:1024], sb_res[:, 512:1024], p.sem("d_ro1"))
        res_last[0] = e_st1
        res_last[1] = e_st2
        prev_store = e_st2
        prev_oT_used = ev_mm

    p.wait("act", p.ev("d_ro0"))
    p.wait("act", p.ev("d_ro1"))
    return fin()



def build_l2_attn(fz=None, xio=None, pre="a2_"):
    p, nc, pin, dramt, sbuf, psum, fin = _env(fz, pre)
    NR, NQ = 4096, 2048
    NTK = NR // 128
    NTQ = NQ // 128

    xb = xio["xb"] if xio else pin("xb", [NR, D], F32)
    w_cat = pin("w_cat", [D, 5 * D], F32)
    ln_g = pin("ln_g", [D], F32)
    ln_b = pin("ln_b", [D], F32)
    lamb = pin("lamb", [1], F32)
    ident = pin("ident", [128, 128], F32)
    x_out = xio["x_out"] if xio else nc.dram_tensor("x_out", [NQ, D], F32, kind="ExternalOutput")

    wcat_bf = dramt("wcat_bf", [D, 5 * D], BF16)
    kT_sp = dramt("kT_sp", [2 * D, NR], BF16)
    vv_sp = dramt("vv_sp", [NR, D], BF16)
    qT_sp = dramt("qT_sp", [2 * D, NQ], BF16)

    slab = sbuf("slab", [128, 8 * NR], BF16)
    sb_S = sbuf("sb_S", [128, NR], F32)
    sb_P1 = sbuf("sb_P1", [128, NR], BF16)
    sb_P2 = sbuf("sb_P2", [128, NR], BF16)
    sb_PT = sbuf("sb_PT", [128, 2 * 512], BF16)
    sb_kT = sbuf("sb_kT", [128, NR], BF16)
    sb_qTs = sbuf("sb_qTs", [128, 128], BF16)
    sb_oq = sbuf("sb_oq", [128, D], F32)
    sb_wnat = sbuf("sb_wnat", [128, 2 * 4096], BF16)
    sb_stg = sbuf("sb_stg", [128, 2 * 512], BF16)
    sb_res = sbuf("sb_res", [128, D], F32)
    sb_gB = sbuf("sb_gB", [128, D], F32)
    sb_bB = sbuf("sb_bB", [128, D], F32)
    sb_lam = sbuf("sb_lam", [128, 4], F32)
    sb_xo = sbuf("sb_xo", [128, D], F32)
    sb_st = sbuf("sb_st", [128, 8], F32)
    sb_scr = sbuf("sb_scr2", [128, D], F32)
    sb_id16 = sbuf("sb_id16", [128, 128], BF16)

    ps_a = psum("ps_a", [128, 512], F32)
    ps_b = psum("ps_b", [128, 512], F32)
    ps_av = psum("ps_av", [128, 512], F32)
    ps_tr = psum("ps_tr", [128, 512], BF16)
    ps_tr2 = psum("ps_tr2", [128, 512], BF16)

    d_c = p.sem("d_c")
    d_cn = p.sem("d_cn")
    d_x2 = p.sem("d_x2")

    p.dma("pool", wcat_bf.ap(), w_cat.ap(), d_c)
    cast_done = p.ev("d_c")

    s = emit_prelude(p, nc, xb, alloc=(sbuf, psum, dramt), router=False, store_ln=False, ident_dram=ident,
                     n_rows=NR, xnT_buf=slab)

    p.dma("act", sb_gB[:, :], bass.AP(ln_g, 0, [[0, 128], [1, D]]), d_cn)
    p.dma("act", sb_bB[:, :], bass.AP(ln_b, 0, [[0, 128], [1, D]]), d_cn)
    p.dma("act", sb_lam[:, 0:1], bass.AP(lamb, 0, [[0, 128], [1, 1]]), d_cn)
    e_k0 = p.ev("d_cn")
    e_id16 = p.op("dve", lambda E: E.tensor_copy(sb_id16[:, :], s.sb_id32[:, :]), wait=(s.e_in0,), inc=True)
    e_l1 = p.op("dve", lambda E: E.tensor_scalar(sb_lam[:, 1:2], sb_lam[:, 0:1], -1.0, None, ALU.mult),
                wait=(e_k0,), inc=True)
    e_l2 = p.op("dve", lambda E: E.tensor_scalar(sb_lam[:, 2:3], sb_lam[:, 0:1], -1.0, 1.0, ALU.mult, ALU.add),
                wait=(e_l1,), inc=True)
    e_g = p.op("dve", lambda E: E.tensor_scalar(sb_gB[:, :], sb_gB[:, :], sb_lam[:, 2:3], None, ALU.mult),
               wait=(e_l2,), inc=True)
    e_b = p.op("dve", lambda E: E.tensor_scalar(sb_bB[:, :], sb_bB[:, :], sb_lam[:, 2:3], None, ALU.mult),
               wait=(e_g,), inc=True)
    ev_const = e_b

    psum_last = {0: None, 1: None}
    stg_last = {0: None, 1: None}
    wnat_use = {0: None, 1: None}
    p.wait("sp", cast_done)
    for oc in range(32):
        par = oc % 2
        ev_wl = p.dma("sp", sb_wnat[:, par * 4096:par * 4096 + 1024].rearrange("p (i c) -> p i c", c=128),
                      wcat_bf.ap().rearrange("(i p) c -> p i c", p=128)[:, :, 128 * oc:128 * (oc + 1)],
                      p.sem("d_w%d" % par), wait=(wnat_use[par],))
        ntf = 4 if oc < 16 else 8
        for tf in range(ntf):
            psx = ps_a if tf % 2 == 0 else ps_b
            ev_mm = None
            for dc in range(8):
                wl2 = []
                if dc == 0:
                    wl2 = [ev_wl, psum_last[tf % 2]]
                    if oc == 0 and tf == 0:
                        wl2 += [s.ev_dve_done]
                ev_mm = p.op("pe", lambda E, dc=dc, tf=tf, par=par, psx=psx: E.matmul(
                    psx[:, :], sb_wnat[:, par * 4096 + dc * 128:par * 4096 + dc * 128 + 128],
                    slab[:, dc * NR + 512 * tf: dc * NR + 512 * (tf + 1)],
                    start=(dc == 0), stop=(dc == 7)),
                    wait=tuple([w for w in wl2 if w]), inc=(dc == 7))
            spar = tf % 2
            e_ev = p.op("act", lambda E, spar=spar, psx=psx: E.activation(
                sb_stg[:, spar * 512:(spar + 1) * 512], psx[:, :], AF.Copy),
                wait=tuple([w for w in (ev_mm, stg_last[spar]) if w]), inc=True)
            psum_last[spar] = e_ev
            if oc < 8:
                dst = qT_sp[128 * oc:128 * (oc + 1), 512 * tf:512 * (tf + 1)]
            elif oc < 16:
                dst = qT_sp[1024 + 128 * (oc - 8):1024 + 128 * (oc - 7), 512 * tf:512 * (tf + 1)]
            elif oc < 24:
                dst = kT_sp[128 * (oc - 16):128 * (oc - 15), 512 * tf:512 * (tf + 1)]
            else:
                dst = kT_sp[1024 + 128 * (oc - 24):1024 + 128 * (oc - 23), 512 * tf:512 * (tf + 1)]
            e_st = p.dma("act", dst, sb_stg[:, spar * 512:(spar + 1) * 512], p.sem("d_sp%d" % spar), wait=(e_ev,))
            stg_last[spar] = e_st
        wnat_use[par] = ev_mm

    ev_vw = {}
    for vh in range(2):
        ev_vw[vh] = p.dma("sp", sb_wnat[:, vh * 4096:(vh + 1) * 4096].rearrange("p (i c) -> p i c", c=512),
                          wcat_bf.ap().rearrange("(i p) c -> p i c", p=128)[:, :, 4096 + 512 * vh:4096 + 512 * (vh + 1)],
                          p.sem("d_w%d" % vh), wait=(wnat_use[vh],))
    for tc in range(NTK):
        for vh in range(2):
            psx = ps_a if vh == 0 else ps_b
            ev_mm = None
            for dc in range(8):
                wl2 = []
                if dc == 0:
                    wl2 = [ev_vw[vh], psum_last[vh]]
                ev_mm = p.op("pe", lambda E, dc=dc, tc=tc, vh=vh, psx=psx: E.matmul(
                    psx[:, :], slab[:, dc * NR + 128 * tc: dc * NR + 128 * (tc + 1)],
                    sb_wnat[:, vh * 4096 + dc * 512: vh * 4096 + dc * 512 + 512],
                    start=(dc == 0), stop=(dc == 7)),
                    wait=tuple([w for w in wl2 if w]), inc=(dc == 7))
            e_ev = p.op("act", lambda E, vh=vh, psx=psx: E.activation(
                sb_stg[:, vh * 512:(vh + 1) * 512], psx[:, :], AF.Copy),
                wait=tuple([w for w in (ev_mm, stg_last[vh]) if w]), inc=True)
            psum_last[vh] = e_ev
            e_st = p.dma("act", vv_sp[128 * tc:128 * (tc + 1), 512 * vh:512 * (vh + 1)],
                         sb_stg[:, vh * 512:(vh + 1) * 512], p.sem("d_sp%d" % vh), wait=(e_ev,))
            stg_last[vh] = e_st
    ev_proj = (p.ev("d_sp0"), p.ev("d_sp1"))

    p.wait("sp", ev_proj[0])
    p.wait("sp", ev_proj[1])
    p.wait("sp", s.ev_dve_done)
    for tc in range(NTK):
        p.dma("sp", slab[:, 1024 * tc:1024 * (tc + 1)], vv_sp[128 * tc:128 * (tc + 1), :], p.sem("d_vl"))
    e_vload = p.ev("d_vl")

    kv_last = None
    res_last = {0: None, 1: None}
    prev_store = None
    prev_oq_written = None
    prev_ep_used = None
    prev_P1_use0 = None
    prev_P2_use0 = None
    e_ex1 = e_ex2 = None
    for qt in range(NTQ):
        for h in range(16):
            e_kl = p.dma("sp", sb_kT[0:64, :], kT_sp[64 * h:64 * (h + 1), :], p.sem("d_kv0"),
                         wait=(kv_last,))
            p.dma("sp", sb_kT[64:128, :], kT_sp[1024 + 64 * h:1024 + 64 * (h + 1), :], p.sem("d_kv0"))
            p.dma("sp", sb_qTs[0:64, :], qT_sp[64 * h:64 * (h + 1), 128 * qt:128 * (qt + 1)], p.sem("d_kv0"))
            p.dma("sp", sb_qTs[64:128, :], qT_sp[1024 + 64 * h:1024 + 64 * (h + 1), 128 * qt:128 * (qt + 1)], p.sem("d_kv0"))
            e_kv = p.ev("d_kv0")
            for w12 in range(2):
                po = 64 * w12
                sc_evac = {0: psum_last[0], 1: psum_last[1]}
                prev_eex = e_ex2 if w12 == 0 else e_ex1
                for c in range(8):
                    cpar = c % 2
                    psx = ps_a if cpar == 0 else ps_b
                    wl2 = [e_kv] if c == 0 else []
                    if c == 0 and qt == 0 and h == 0 and w12 == 0:
                        wl2.append(e_vload)
                    wl2.append(sc_evac[cpar])
                    ev_mm = p.op("pe", lambda E, c=c, po=po, psx=psx: E.matmul(
                        psx[:, :], sb_qTs[po:po + 64, :], sb_kT[po:po + 64, 512 * c: 512 * (c + 1)],
                        start=True, stop=True),
                        wait=tuple([w for w in wl2 if w]), inc=True)
                    e_sv = p.op("dve", lambda E, c=c, psx=psx: E.tensor_scalar(
                        sb_S[:, 512 * c:512 * (c + 1)], psx[:, :], 0.125, None, ALU.mult),
                        wait=(ev_mm,) + ((prev_eex,) if c == 0 and prev_eex else ()), inc=True)
                    sc_evac[cpar] = e_sv
                psum_last[0], psum_last[1] = sc_evac[0], sc_evac[1]
                e_nm = p.op("dve", lambda E: E.reduce_max(sb_st[:, 1:2], sb_S[:, :], axis=AX.X, negate=True),
                            wait=(sc_evac[0], sc_evac[1]), inc=True)
                if w12 == 0:
                    e_ex1 = p.op("act", lambda E: E.activation(sb_P1[:, :], sb_S[:, :], AF.Exp, bias=sb_st[:, 1:2], accum_out=sb_st[:, 2:3]),
                                 wait=(e_nm, prev_P1_use0), inc=True)
                    e_rz1 = p.op("dve", lambda E: E.reciprocal(sb_st[:, 3:4], sb_st[:, 2:3]), wait=(e_ex1,), inc=True)
                else:
                    e_ex2 = p.op("act", lambda E: E.activation(sb_P2[:, :], sb_S[:, :], AF.Exp, bias=sb_st[:, 1:2], accum_out=sb_st[:, 4:5]),
                                 wait=(e_nm, prev_P2_use0), inc=True)
                    e_rz2 = p.op("dve", lambda E: E.reciprocal(sb_st[:, 5:6], sb_st[:, 4:5]), wait=(e_ex2,), inc=True)
            e_nl = p.op("dve", lambda E: E.tensor_mul(sb_st[:, 6:7], sb_st[:, 5:6], sb_lam[:, 1:2]),
                        wait=(e_rz2, ev_const), inc=True)
            e_cm1 = p.op("dve", lambda E: E.tensor_scalar(sb_P1[:, :], sb_P1[:, :], sb_st[:, 3:4], None, ALU.mult),
                         wait=(e_rz1,), inc=True)
            e_cm2 = p.op("dve", lambda E: E.scalar_tensor_tensor(sb_P1[:, :], sb_P2[:, :], sb_st[:, 6:7], sb_P1[:, :], ALU.mult, ALU.add),
                         wait=(e_nl, e_cm1), inc=True)
            prev_P2_use0 = e_cm2
            ev_avmm = None
            pt_evac = {0: None, 1: None}
            for g in range(8):
                gpar = g % 2
                pstr = ps_tr if gpar == 0 else ps_tr2
                for a in range(4):
                    wl2 = []
                    if a == 0:
                        wl2 = [e_cm2, pt_evac[gpar]]
                        if qt == 0 and h == 0 and g == 0:
                            wl2.append(e_id16)
                    ev_tr = p.op("pe", lambda E, g=g, a=a, pstr=pstr: E.transpose(
                        pstr[:, 128 * a:128 * (a + 1)], sb_P1[:, 512 * g + 128 * a: 512 * g + 128 * (a + 1)], sb_id16[:, :]),
                        wait=tuple([w for w in wl2 if w]), inc=(a == 3))
                e_pt = p.op("dve", lambda E, gpar=gpar, pstr=pstr: E.tensor_copy(sb_PT[:, gpar * 512:(gpar + 1) * 512], pstr[:, :]),
                            wait=(ev_tr, pt_evac[gpar]), inc=True)
                pt_evac[gpar] = e_pt
                for a in range(4):
                    tc4 = 4 * g + a
                    wl2 = [e_pt] if a == 0 else []
                    if g == 0 and a == 0:
                        wl2.append(prev_oq_written)
                    ev_avmm = p.op("pe", lambda E, g=g, a=a, tc4=tc4, h=h: E.matmul(
                        ps_av[:, 0:64], sb_PT[:, (g % 2) * 512 + 128 * a: (g % 2) * 512 + 128 * (a + 1)],
                        slab[:, 1024 * tc4 + 64 * h: 1024 * tc4 + 64 * h + 64],
                        start=(g == 0 and a == 0), stop=(g == 7 and a == 3)),
                        wait=tuple([w for w in wl2 if w]), inc=(g == 7 and a == 3))
            prev_P1_use0 = ev_tr
            prev_oq_written = p.op("dve", lambda E, h=h: E.tensor_copy(sb_oq[:, 64 * h:64 * (h + 1)], ps_av[:, 0:64]),
                                   wait=(ev_avmm, prev_ep_used), inc=True)
            kv_last = ev_avmm
        e1 = p.op("dve", lambda E: E.reduce_sum(sb_st[:, 0:1], sb_oq[:, :], axis=AX.X),
                  wait=(prev_oq_written,), inc=True)
        e2 = p.op("dve", lambda E: E.tensor_scalar(sb_st[:, 0:1], sb_st[:, 0:1], -1.0 / D, None, ALU.mult),
                  wait=(e1,), inc=True)
        e_xc = p.op("act", lambda E: E.activation(sb_scr[:, :], sb_oq[:, :], AF.Identity, bias=sb_st[:, 0:1]),
                    wait=(e2,), inc=True)
        e_v2 = p.op("act", lambda E: E.activation(sb_res[:, :], sb_scr[:, :], AF.Square, accum_out=sb_st[:, 1:2]),
                    wait=(e_xc, res_last[0], res_last[1]), inc=True)
        e5 = p.op("dve", lambda E: E.tensor_scalar(sb_st[:, 1:2], sb_st[:, 1:2], 1.0 / D, 1e-5, ALU.mult, ALU.add),
                  wait=(e_v2,), inc=True)
        e6 = p.op("act", lambda E: E.activation(sb_st[:, 2:3], sb_st[:, 1:2], AF.Sqrt), wait=(e5,), inc=True)
        e7 = p.op("dve", lambda E: E.reciprocal(sb_st[:, 3:4], sb_st[:, 2:3]), wait=(e6,), inc=True)
        e_xn = p.op("dve", lambda E: E.tensor_scalar(sb_scr[:, :], sb_scr[:, :], sb_st[:, 3:4], None, ALU.mult),
                    wait=(e7,), inc=True)
        e_lx = p.dma("sp", sb_xo[:, :], xb[128 * qt:128 * (qt + 1), :], d_x2, wait=(prev_store,))
        e_o1 = p.op("dve", lambda E: E.tensor_mul(sb_res[:, :], sb_scr[:, :], sb_gB[:, :]),
                    wait=(e_xn, ev_const), inc=True)
        e_o2 = p.op("dve", lambda E: E.tensor_add(sb_res[:, :], sb_res[:, :], sb_bB[:, :]), wait=(e_o1,), inc=True)
        e_o3 = p.op("dve", lambda E: E.tensor_add(sb_res[:, :], sb_res[:, :], sb_xo[:, :]), wait=(e_o2, e_lx), inc=True)
        e_st1 = p.dma("act", x_out[128 * qt:128 * (qt + 1), 0:512], sb_res[:, 0:512], p.sem("d_ro0"), wait=(e_o3,))
        e_st2 = p.dma("act", x_out[128 * qt:128 * (qt + 1), 512:1024], sb_res[:, 512:1024], p.sem("d_ro1"))
        res_last[0] = e_st1
        res_last[1] = e_st2
        prev_store = e_st2
        prev_ep_used = e_v2

    p.wait("act", p.ev("d_ro0"))
    p.wait("act", p.ev("d_ro1"))
    return fin()



DEBUG_L1 = False


def build_l1_attn(fz=None, xio=None, pre="a1_"):
    p, nc, pin, dramt, sbuf, psum, fin = _env(fz, pre)
    NR, NQ, H = 4096, 2048, 16
    NTK = NR // 128
    NTQ = NQ // 128
    GAMMAS = [1.0 - 2.0 ** (-5.0 - h) for h in range(H)]

    xb = xio["xb"] if xio else pin("xb", [NR, D], F32)
    kpos = pin("kpos", [NR], F32)
    w_cat = pin("w_cat", [D, 3 * D], F32)
    pkv = pin("pkv", [H, 64, 64], F32)
    gnw_e = pin("gnw_e", [D], F32)
    gnb_e = pin("gnb_e", [D], F32)
    tabU = pin("tabU", [H, 128, 128], BF16)
    gpcol = pin("gpcol", [128, H], BF16)
    gdrow = pin("gdrow", [H, 128], BF16)
    cvec = pin("cvec", [H, NQ], BF16)
    ident = pin("ident", [128, 128], F32)
    x_out = xio["x_out"] if xio else nc.dram_tensor("x_out", [NQ, D], F32, kind="ExternalOutput")
    dbg_ret = nc.dram_tensor("dbg_ret", [NQ, D], F32, kind="ExternalOutput") if DEBUG_L1 else None

    wcat_bf = dramt("wcat_bf", [D, 3 * D], BF16)
    kT_sp = dramt("kT_sp", [D, NR], BF16)
    qT_sp = dramt("qT_sp", [D, NR], BF16)
    vv_sp = dramt("vv_sp", [NR, D], BF16)
    ret_sp = dramt("ret_sp", [NQ, D], F32)
    cc_in = dramt("cc_in", [2], F32)
    cc_out = dramt("cc_out", [2], F32)

    slab = sbuf("slab", [128, 8 * NR], BF16)
    sb_D = sbuf("sb_D", [128, NR], F32)
    sb_M = sbuf("sb_M", [128, NR], BF16)
    sb_P = sbuf("sb_P", [128, NR], BF16)
    sb_PT = sbuf("sb_PT", [128, 2 * 512], BF16)
    sb_k1a = sbuf("sb_k1a", [64, NR], BF16)
    sb_q1a = sbuf("sb_q1a", [64, NR], BF16)
    sb_pp = sbuf("sb_pp", [128, NTK * 64], BF16)
    sb_kpB = sbuf("sb_kpB", [128, NR], F32)
    sb_cut = sbuf("sb_cut", [128, NTQ], F32)
    sb_wnat = sbuf("sb_wnat", [128, 2 * 4096], BF16)
    sb_stg = sbuf("sb_stg", [128, 2 * 512], BF16)
    sb_res = sbuf("sb_res", [128, D], F32)
    sb_ret = sbuf("sb_ret", [128, D], F32)
    sb_gnw = sbuf("sb_gnw", [128, D], F32)
    sb_gnb = sbuf("sb_gnb", [128, D], F32)
    sb_tabU = sbuf("sb_tabU", [128, 128], BF16)
    sb_gp = sbuf("sb_gp", [128, H], BF16)
    sb_gps = sbuf("sb_gps", [128, 1], BF16)
    sb_gdr = sbuf("sb_gdr", [1, 128], BF16)
    sb_cvr = sbuf("sb_cvr", [1, NQ], BF16)
    sb_pkv = sbuf("sb_pkv", [64, 64], F32)
    sb_pkvb = sbuf("sb_pkvb", [64, 64], BF16)
    sb_SstF = sbuf("sb_SstF", [1, 64], F32)
    sb_Sst = sbuf("sb_Sst", [1, 64], BF16)
    sb_OW = sbuf("sb_OW", [1, 64], BF16)
    sb_xo = sbuf("sb_xo", [128, D], F32)
    sb_st = sbuf("sb_st", [128, 8], F32)
    sb_acc2 = sbuf("sb_acc2", [128, 2], F32)
    sb_sc2 = sbuf("sb_sc2", [2, 1], F32)
    sb_mr = sbuf("sb_mr", [1, 2], F32)
    sb_mrB = sbuf("sb_mrB", [128, 2], F32)
    sb_one = sbuf("sb_one", [1, 128], F32)
    sb_onec = sbuf("sb_onec", [128, 1], F32)
    sb_id16 = sbuf("sb_id16", [128, 128], BF16)
    sb_dump = sbuf("sb_dump2", [128, D], F32)

    ps_a = psum("ps_a", [128, 512], F32)
    ps_b = psum("ps_b", [128, 512], F32)
    ps_av = psum("ps_av", [128, 512], F32)
    ps_tr = psum("ps_tr", [128, 512], BF16)
    ps_tr2 = psum("ps_tr2", [128, 512], BF16)

    d_c = p.sem("d_c")
    d_cn = p.sem("d_cn")
    d_x2 = p.sem("d_x2")
    d_cc = p.sem("d_cc")

    p.dma("pool", wcat_bf.ap(), w_cat.ap(), d_c)
    cast_done = p.ev("d_c")

    s = emit_prelude(p, nc, xb, alloc=(sbuf, psum, dramt), router=False, store_ln=False, ident_dram=ident,
                     n_rows=NR, xnT_buf=slab)

    p.dma("act", sb_kpB[:, :], bass.AP(kpos, 0, [[0, 128], [1, NR]]), d_cn)
    p.dma("act", sb_cut[:, :], kpos.ap()[0:NQ].rearrange("(i p) -> p i", p=128), d_cn,
          allow_slow_non_contiguous=True)
    p.dma("act", sb_gnw[:, :], bass.AP(gnw_e, 0, [[0, 128], [1, D]]), d_cn)
    p.dma("act", sb_gnb[:, :], bass.AP(gnb_e, 0, [[0, 128], [1, D]]), d_cn)
    p.dma("act", sb_gp[:, :], gpcol.ap(), d_cn)
    e_k0 = p.ev("d_cn")
    e_id16 = p.op("dve", lambda E: E.tensor_copy(sb_id16[:, :], s.sb_id32[:, :]), wait=(s.e_in0,), inc=True)
    e_one = p.op("dve", lambda E: E.memset(sb_one[:, :], 1.0), inc=True)
    e_onec = p.op("dve", lambda E: E.memset(sb_onec[:, :], 1.0), inc=True)

    psum_last = {0: None, 1: None}
    stg_last = {0: None, 1: None}
    wnat_use = {0: None, 1: None}
    p.wait("sp", cast_done)
    for oc in range(16):
        par = oc % 2
        ev_wl = p.dma("sp", sb_wnat[:, par * 4096:par * 4096 + 1024].rearrange("p (i c) -> p i c", c=128),
                      wcat_bf.ap().rearrange("(i p) c -> p i c", p=128)[:, :, 128 * oc:128 * (oc + 1)],
                      p.sem("d_w%d" % par), wait=(wnat_use[par],))
        for tf in range(8):
            psx = ps_a if tf % 2 == 0 else ps_b
            ev_mm = None
            for dc in range(8):
                wl2 = []
                if dc == 0:
                    wl2 = [ev_wl, psum_last[tf % 2]]
                    if oc == 0 and tf == 0:
                        wl2 += [s.ev_dve_done]
                ev_mm = p.op("pe", lambda E, dc=dc, tf=tf, par=par, psx=psx: E.matmul(
                    psx[:, :], sb_wnat[:, par * 4096 + dc * 128:par * 4096 + dc * 128 + 128],
                    slab[:, dc * NR + 512 * tf: dc * NR + 512 * (tf + 1)],
                    start=(dc == 0), stop=(dc == 7)),
                    wait=tuple([w for w in wl2 if w]), inc=(dc == 7))
            spar = tf % 2
            e_ev = p.op("act", lambda E, spar=spar, psx=psx: E.activation(
                sb_stg[:, spar * 512:(spar + 1) * 512], psx[:, :], AF.Copy),
                wait=tuple([w for w in (ev_mm, stg_last[spar]) if w]), inc=True)
            psum_last[spar] = e_ev
            if oc < 8:
                dst = qT_sp[128 * oc:128 * (oc + 1), 512 * tf:512 * (tf + 1)]
            else:
                dst = kT_sp[128 * (oc - 8):128 * (oc - 7), 512 * tf:512 * (tf + 1)]
            e_st = p.dma("act", dst, sb_stg[:, spar * 512:(spar + 1) * 512], p.sem("d_sp%d" % spar), wait=(e_ev,))
            stg_last[spar] = e_st
        wnat_use[par] = ev_mm

    ev_vw = {}
    for vh in range(2):
        ev_vw[vh] = p.dma("sp", sb_wnat[:, vh * 4096:(vh + 1) * 4096].rearrange("p (i c) -> p i c", c=512),
                          wcat_bf.ap().rearrange("(i p) c -> p i c", p=128)[:, :, 2048 + 512 * vh:2048 + 512 * (vh + 1)],
                          p.sem("d_w%d" % vh), wait=(wnat_use[vh],))
    for tc in range(NTK):
        for vh in range(2):
            psx = ps_a if vh == 0 else ps_b
            ev_mm = None
            for dc in range(8):
                wl2 = []
                if dc == 0:
                    wl2 = [ev_vw[vh], psum_last[vh]]
                ev_mm = p.op("pe", lambda E, dc=dc, tc=tc, vh=vh, psx=psx: E.matmul(
                    psx[:, :], slab[:, dc * NR + 128 * tc: dc * NR + 128 * (tc + 1)],
                    sb_wnat[:, vh * 4096 + dc * 512: vh * 4096 + dc * 512 + 512],
                    start=(dc == 0), stop=(dc == 7)),
                    wait=tuple([w for w in wl2 if w]), inc=(dc == 7))
            e_ev = p.op("act", lambda E, vh=vh, psx=psx: E.activation(
                sb_stg[:, vh * 512:(vh + 1) * 512], psx[:, :], AF.Copy),
                wait=tuple([w for w in (ev_mm, stg_last[vh]) if w]), inc=True)
            psum_last[vh] = e_ev
            e_st = p.dma("act", vv_sp[128 * tc:128 * (tc + 1), 512 * vh:512 * (vh + 1)],
                         sb_stg[:, vh * 512:(vh + 1) * 512], p.sem("d_sp%d" % vh), wait=(e_ev,))
            stg_last[vh] = e_st
    ev_proj = (p.ev("d_sp0"), p.ev("d_sp1"))

    p.wait("sp", ev_proj[0])
    p.wait("sp", ev_proj[1])
    p.wait("sp", s.ev_dve_done)
    for tc in range(NTK):
        p.dma("sp", slab[:, 1024 * tc:1024 * (tc + 1)], vv_sp[128 * tc:128 * (tc + 1), :], p.sem("d_vl"))
    e_vload = p.ev("d_vl")

    kv_last = None
    ret_last = None
    pp_last = None
    prev_Duse = None
    prev_Puse = None
    ev_st_upd = None
    ev_OW_use = None
    ev_gd_use = None
    ev_tabU_use = None
    for h in range(H):
        lng = float(np.log(GAMMAS[h]))
        g128 = float(GAMMAS[h] ** 128)
        e_kl = p.dma("sp", sb_k1a[:, :], kT_sp[64 * h:64 * (h + 1), :], p.sem("d_kv0"), wait=(kv_last,))
        p.dma("sp", sb_q1a[:, :], qT_sp[64 * h:64 * (h + 1), :], p.sem("d_kv0"))
        p.dma("sp", sb_tabU[:, :], tabU[h], p.sem("d_kv0"), wait=(ev_tabU_use,))
        p.dma("sp", sb_gdr[:, :], gdrow.ap()[h:h + 1, :], p.sem("d_kv0"), wait=(ev_gd_use,))
        p.dma("sp", sb_cvr[:, :], cvec.ap()[h:h + 1, :], p.sem("d_kv0"), wait=(ev_OW_use,))
        p.dma("sp", sb_pkv[:, :], pkv[h], p.sem("d_kv0"), wait=(pp_last,))
        e_kv = p.ev("d_kv0")
        e_pkvb = p.op("dve", lambda E: E.tensor_copy(sb_pkvb[:, :], sb_pkv[:, :]), wait=(e_kv,), inc=True)
        ev_ppmm = None
        e_ppe = pp_last
        for tc in range(NTK):
            wl2 = [e_pkvb, psum_last[0]] if tc == 0 else []
            if h == 0 and tc == 0:
                wl2 += [e_vload, psum_last[1]]
            if tc == 0 and pp_last is not None:
                wl2.append(pp_last)
            ev_ppmm = p.op("pe", lambda E, tc=tc: E.matmul(
                ps_a[:, 0:64], sb_q1a[:, 128 * tc:128 * (tc + 1)], sb_pkvb[:, :],
                start=True, stop=True),
                wait=tuple([w for w in wl2 if w]) if wl2 else ((e_ppe,) if e_ppe else ()), inc=True)
            e_ppe = p.op("dve", lambda E, tc=tc: E.tensor_copy(sb_pp[:, 64 * tc:64 * (tc + 1)], ps_a[:, 0:64]),
                         wait=(ev_ppmm,), inc=True)
            p.wait("pe", e_ppe)
        pp_done = e_ppe
        psum_last[0] = pp_done
        ev_owmm = None
        for a in range(16):
            e_gs = p.op("dve", lambda E, a=a, h=h: E.tensor_scalar(
                sb_gps[:, :], sb_gp[:, h:h + 1], float(GAMMAS[h] ** (128 * a)), None, ALU.mult),
                wait=(pp_done, e_k0, psum_last[1]) if a == 0 else (ev_owmm,), inc=True)
            ev_owmm = p.op("pe", lambda E, a=a: E.matmul(
                ps_b[0:1, 0:64], sb_gps[:, :], sb_pp[:, 64 * (16 + a):64 * (17 + a)],
                start=(a == 0), stop=(a == 15)),
                wait=(e_gs,), inc=True)
        e_OW = p.op("dve", lambda E: E.tensor_copy(sb_OW[:, :], ps_b[0:1, 0:64]), wait=(ev_owmm,), inc=True)
        psum_last[1] = e_OW
        e_S0 = p.op("dve", lambda E: E.memset(sb_SstF[:, :], 0.0), wait=(ev_st_upd,), inc=True)
        e_Sb = p.op("dve", lambda E: E.memset(sb_Sst[:, :], 0.0), wait=(e_S0,), inc=True)
        ev_st_upd = e_Sb

        for qi in range(NTQ - 1, -1, -1):
            e_d1 = p.op("dve", lambda E, qi=qi, lng=lng: E.tensor_scalar(
                sb_D[:, :], sb_kpB[:, :], sb_cut[:, qi:qi + 1], -lng, ALU.subtract, ALU.mult),
                wait=tuple([w for w in (e_k0, prev_Duse, e_OW) if w]), inc=True)
            e_d2 = p.op("dve", lambda E: E.tensor_scalar_min(sb_D[:, :], sb_D[:, :], 0.0), wait=(e_d1,), inc=True)
            e_d3 = p.op("act", lambda E: E.activation(sb_D[:, :], sb_D[:, :], AF.Exp), wait=(e_d2,), inc=True)
            e_m1 = p.op("dve", lambda E, qi=qi: E.tensor_scalar(
                sb_M[:, :], sb_kpB[:, :], sb_cut[:, qi:qi + 1], None, ALU.is_le), wait=(e_d3,), inc=True)
            e_d4 = p.op("dve", lambda E: E.tensor_mul(sb_D[:, :], sb_D[:, :], sb_M[:, :]), wait=(e_m1,), inc=True)
            sc_evac = {0: psum_last[0], 1: psum_last[1]}
            for c in range(8):
                cpar = c % 2
                psx = ps_a if cpar == 0 else ps_b
                wl2 = [e_kv, e_d4] if c == 0 else []
                if c == 0 and prev_Puse is not None:
                    wl2.append(prev_Puse)
                wl2.append(sc_evac[cpar])
                ev_mm = p.op("pe", lambda E, c=c, qi=qi, psx=psx: E.matmul(
                    psx[:, :], sb_q1a[:, 128 * qi:128 * (qi + 1)], sb_k1a[:, 512 * c:512 * (c + 1)],
                    start=True, stop=True),
                    wait=tuple([w for w in wl2 if w]), inc=True)
                e_sv = p.op("dve", lambda E, c=c, psx=psx: E.tensor_tensor(
                    sb_P[:, 512 * c:512 * (c + 1)], psx[:, :], sb_D[:, 512 * c:512 * (c + 1)], ALU.mult),
                    wait=(ev_mm,), inc=True)
                sc_evac[cpar] = e_sv
            psum_last[0], psum_last[1] = sc_evac[0], sc_evac[1]
            prev_Duse = sc_evac[0]
            ev_avmm = None
            pt_evac = {0: None, 1: None}
            for g in range(8):
                gpar = g % 2
                pstr = ps_tr if gpar == 0 else ps_tr2
                for a in range(4):
                    wl2 = []
                    if a == 0:
                        wl2 = [sc_evac[0], sc_evac[1], pt_evac[gpar]]
                        if h == 0 and qi == NTQ - 1 and g == 0:
                            wl2.append(e_id16)
                    ev_tr = p.op("pe", lambda E, g=g, a=a, pstr=pstr: E.transpose(
                        pstr[:, 128 * a:128 * (a + 1)], sb_P[:, 512 * g + 128 * a: 512 * g + 128 * (a + 1)], sb_id16[:, :]),
                        wait=tuple([w for w in wl2 if w]), inc=(a == 3))
                e_pt = p.op("dve", lambda E, gpar=gpar, pstr=pstr: E.tensor_copy(sb_PT[:, gpar * 512:(gpar + 1) * 512], pstr[:, :]),
                            wait=(ev_tr, pt_evac[gpar]), inc=True)
                pt_evac[gpar] = e_pt
                for a in range(4):
                    tc4 = 4 * g + a
                    wl2 = [e_pt] if a == 0 else []
                    if g == 0 and a == 0:
                        wl2.append(ret_last)
                    ev_avmm = p.op("pe", lambda E, g=g, a=a, tc4=tc4, h=h: E.matmul(
                        ps_av[:, 0:64], sb_PT[:, (g % 2) * 512 + 128 * a: (g % 2) * 512 + 128 * (a + 1)],
                        slab[:, 1024 * tc4 + 64 * h: 1024 * tc4 + 64 * h + 64],
                        start=(g == 0 and a == 0), stop=False),
                        wait=tuple([w for w in wl2 if w]), inc=False)
            prev_Puse = ev_avmm
            e_x1 = p.op("pe", lambda E, qi=qi: E.matmul(
                ps_av[:, 0:64], sb_tabU[:, :], sb_pp[:, 64 * qi:64 * (qi + 1)],
                start=False, stop=False), wait=(e_kv, pp_done), inc=False)
            e_x2 = p.op("pe", lambda E: E.matmul(
                ps_av[:, 0:64], sb_gdr[:, :], sb_Sst[:, :],
                start=False, stop=False), wait=(ev_st_upd,), inc=False)
            e_x3 = p.op("pe", lambda E, qi=qi: E.matmul(
                ps_av[:, 0:64], sb_cvr[:, 128 * qi:128 * (qi + 1)], sb_OW[:, :],
                start=False, stop=True), wait=(e_OW,), inc=True)
            e_rc = p.op("dve", lambda E: E.tensor_copy(sb_ret[:, 0:64], ps_av[:, 0:64]),
                        wait=(e_x3, ret_last), inc=True)
            ret_last = p.dma("act", ret_sp[128 * qi:128 * (qi + 1), 64 * h:64 * (h + 1)], sb_ret[:, 0:64],
                             p.sem("d_rt"), wait=(e_rc,))
            e_w1 = p.op("pe", lambda E, qi=qi, h=h: E.matmul(
                ps_b[0:1, 64:128], sb_gp[:, h:h + 1], sb_pp[:, 64 * qi:64 * (qi + 1)],
                start=True, stop=True), wait=(e_x3, psum_last[1]), inc=True)
            e_w2 = p.op("dve", lambda E, g128=g128: E.scalar_tensor_tensor(
                sb_SstF[:, :], sb_SstF[:, :], g128, ps_b[0:1, 64:128], ALU.mult, ALU.add),
                wait=(e_w1, ev_st_upd), inc=True)
            ev_st_upd = p.op("dve", lambda E: E.tensor_copy(sb_Sst[:, :], sb_SstF[:, :]), wait=(e_w2,), inc=True)
            psum_last[1] = e_w2
        kv_last = e_w1
        pp_last = e_w1
        ev_OW_use = e_w1
        ev_gd_use = e_w1
        ev_tabU_use = e_w1

    if DEBUG_L1:
        p.dma("pool", dbg_ret.ap(), ret_sp.ap(), p.sem("d_dbg"), wait=(p.ev("d_rt"),))
    e_z = p.op("dve", lambda E: E.memset(sb_acc2[:, :], 0.0), inc=True)
    acc_last = e_z
    e_lr_prev = None
    ret_all = p.ev("d_rt")
    for i in range(NTQ):
        e_lr = p.dma("sp", sb_ret[:, :], ret_sp[128 * i:128 * (i + 1), :], p.sem("d_r2"),
                     wait=(ret_all if i == 0 else None, e_lr_prev))
        e_r1 = p.op("dve", lambda E: E.reduce_sum(sb_st[:, 0:1], sb_ret[:, :], axis=AX.X),
                    wait=(e_lr,), inc=True)
        e_r2 = p.op("act", lambda E: E.activation(sb_dump[:, :], sb_ret[:, :], AF.Square, accum_out=sb_st[:, 1:2]),
                    wait=(e_lr, e_lr_prev2 if i > 0 else None) if i > 0 else (e_lr,), inc=True)
        e_a1 = p.op("dve", lambda E: E.tensor_add(sb_acc2[:, 0:1], sb_acc2[:, 0:1], sb_st[:, 0:1]),
                    wait=(e_r1, acc_last), inc=True)
        e_a2 = p.op("dve", lambda E: E.tensor_add(sb_acc2[:, 1:2], sb_acc2[:, 1:2], sb_st[:, 1:2]),
                    wait=(e_r2, e_a1), inc=True)
        acc_last = e_a2
        e_lr_prev = e_a2
        e_lr_prev2 = e_r2
    e_pr = p.op("pe", lambda E: E.matmul(ps_b[0:2, 0:1], sb_acc2[:, :], sb_onec[:, :], start=True, stop=True),
                wait=(acc_last, e_onec, ev_st_upd, psum_last[0], psum_last[1]), inc=True)
    e_pc = p.op("dve", lambda E: E.tensor_copy(sb_sc2[:, :], ps_b[0:2, 0:1]), wait=(e_pr,), inc=True)
    e_cw = p.dma("sp", cc_in.ap(), sb_sc2.ap(), p.sem("d_ci"), wait=(e_pc,))
    e_cc = p.op("pool", lambda E: E.collective_compute(
        "AllReduce", ALU.add, replica_groups=[[0, 1], [2, 3], [4, 5], [6, 7]],
        ins=[cc_in.ap()], outs=[cc_out.ap()]),
        wait=(e_cw,), inc="d_cc", inc_amt=1)
    e_cl = p.dma("sp", sb_mr[:, :], cc_out.ap().rearrange("(o k) -> o k", o=1), p.sem("d_cl"), wait=(e_cc,))
    NTOT = float(NR * D)
    e_s1 = p.op("dve", lambda E: E.tensor_scalar(sb_mr[:, 0:1], sb_mr[:, 0:1], 1.0 / NTOT, None, ALU.mult),
                wait=(e_cl,), inc=True)
    e_s2 = p.op("dve", lambda E: E.tensor_scalar(sb_mr[:, 1:2], sb_mr[:, 1:2], 1.0 / NTOT, None, ALU.mult),
                wait=(e_s1,), inc=True)
    e_s3 = p.op("dve", lambda E: E.scalar_tensor_tensor(sb_st[0:1, 4:5], sb_mr[:, 0:1], -1.0, sb_mr[:, 0:1], ALU.mult, ALU.mult),
                wait=(e_s2,), inc=True)
    e_s4 = p.op("dve", lambda E: E.tensor_add(sb_mr[:, 1:2], sb_mr[:, 1:2], sb_st[0:1, 4:5]), wait=(e_s3,), inc=True)
    e_s4b = p.op("dve", lambda E: E.tensor_scalar(sb_mr[:, 1:2], sb_mr[:, 1:2], 1e-5, None, ALU.add), wait=(e_s4,), inc=True)
    e_s5 = p.op("act", lambda E: E.activation(sb_mr[:, 1:2], sb_mr[:, 1:2], AF.Sqrt), wait=(e_s4b,), inc=True)
    e_s6 = p.op("dve", lambda E: E.reciprocal(sb_mr[:, 1:2], sb_mr[:, 1:2]), wait=(e_s5,), inc=True)
    e_s7 = p.op("dve", lambda E: E.tensor_scalar(sb_mr[:, 0:1], sb_mr[:, 0:1], -1.0, None, ALU.mult),
                wait=(e_s6,), inc=True)
    e_bm = p.op("pe", lambda E: E.matmul(ps_b[:, 0:2], sb_one[:, :], sb_mr[:, :], start=True, stop=True),
                wait=(e_s7, e_one, e_pc), inc=True)
    e_bc = p.op("dve", lambda E: E.tensor_copy(sb_mrB[:, :], ps_b[:, 0:2]), wait=(e_bm,), inc=True)

    prev_store = None
    e_n5 = None
    for i in range(NTQ):
        e_lr = p.dma("sp", sb_ret[:, :], ret_sp[128 * i:128 * (i + 1), :], p.sem("d_r2"),
                     wait=(e_lr_prev if i == 0 else None, e_n5))
        e_lx = p.dma("sp", sb_xo[:, :], xb[128 * i:128 * (i + 1), :], d_x2, wait=(prev_store,))
        e_n1 = p.op("act", lambda E: E.activation(sb_res[:, :], sb_ret[:, :], AF.Identity, bias=sb_mrB[:, 0:1]),
                    wait=(e_lr, e_bc, prev_store), inc=True)
        e_n2 = p.op("dve", lambda E: E.tensor_scalar(sb_res[:, :], sb_res[:, :], sb_mrB[:, 1:2], None, ALU.mult),
                    wait=(e_n1,), inc=True)
        e_n3 = p.op("dve", lambda E: E.tensor_mul(sb_res[:, :], sb_res[:, :], sb_gnw[:, :]), wait=(e_n2,), inc=True)
        e_n4 = p.op("dve", lambda E: E.tensor_add(sb_res[:, :], sb_res[:, :], sb_gnb[:, :]), wait=(e_n3,), inc=True)
        e_n5 = p.op("dve", lambda E: E.tensor_add(sb_res[:, :], sb_res[:, :], sb_xo[:, :]), wait=(e_n4, e_lx), inc=True)
        prev_store = p.dma("act", x_out[128 * i:128 * (i + 1), :], sb_res[:, :], p.sem("d_ro0"), wait=(e_n5,))

    p.wait("act", p.ev("d_ro0"))
    return fin()



def build_l2_inner_attn(fz=None, xio=None, pre="ia_"):
    p, nc, pin, dramt, sbuf, psum, fin = _env(fz, pre)
    HALO = 384
    NRA = HALO + 2048
    NTA = NRA // 128
    NTQ = 16

    xh = xio["xh"] if xio else pin("xh", [NRA, D], F32)
    hpos = pin("hpos", [NRA], F32)
    w_qkv = pin("w_qkv", [D, 3 * D], F32)
    w_o = pin("w_o", [D, D], F32)
    q_scale = pin("q_scale", [64], F32)
    k_scale = pin("k_scale", [64], F32)
    csq = pin("csq", [2048, 64], F32)
    ssq = pin("ssq", [2048, 64], F32)
    csk = pin("csk", [NRA, 64], F32)
    ssk = pin("ssk", [NRA, 64], F32)
    ident = pin("ident", [128, 128], F32)
    x_out = xio["x_out"] if xio else nc.dram_tensor("x_out", [2048, D], F32, kind="ExternalOutput")

    wqkv_bf = dramt("wqkv_bf", [D, 3 * D], BF16)
    wo_bf = dramt("wo_bf", [D, D], BF16)
    v_sp = dramt("v_sp", [NRA, D], BF16)

    sb_gen = sbuf("sb_gen", [128, 2 * D], F32)
    sb_xt = sb_gen
    sb_sh = sbuf("sb_sh", [128, D], F32)
    sb_sh16 = sbuf("sb_sh16", [128, D], BF16)
    sb_xshT = sbuf("sb_xshT", [128, 8 * NRA], BF16)
    sb_qT = sbuf("sb_qT", [128, 8 * 2048], BF16)
    sb_kT = sbuf("sb_kT", [128, 8 * NRA], BF16)
    sb_wq = sbuf("sb_wq", [128, 8 * D], BF16)
    sb_wo = sb_wq
    sb_vt = sbuf("sb_vt", [128, 2 * 3 * D], BF16)
    sb_qs = sbuf("sb_qs", [128, 64], F32)
    sb_ks = sbuf("sb_ks", [128, 64], F32)
    sb_tabq = sbuf("sb_tabq", [128, 16 * 128], F32)
    sb_tabk = sbuf("sb_tabk", [128, NTA * 128], F32)
    sb_qkf = sbuf("sb_qkf", [128, D], F32)
    sb_rot = sbuf("sb_rot", [128, D], F32)
    sb_nrm = sbuf("sb_nrm", [128, 16], F32)
    sb_out16 = sbuf("sb_out16", [128, D], BF16)
    sb_hpB = sbuf("sb_hpB", [128, NRA], F32)
    sb_cut = sbuf("sb_cut", [128, NTQ], F32)
    sb_msk = sbuf("sb_msk", [128, 896], F32)
    sb_S = sbuf("sb_S", [128, 512], F32)
    sb_P = sbuf("sb_P", [128, 512], BF16)
    sb_PT = sbuf("sb_PT", [128, 2 * 128], BF16)
    sb_oq = sb_gen[:, 0:D]
    sb_oT = sbuf("sb_oT", [128, 8 * 128], BF16)
    sb_res = sb_gen[:, D:2 * D]
    sb_xo = sbuf("sb_xo", [128, D], F32)
    sb_st = sbuf("sb_st", [128, 8], F32)
    sb_id16 = sbuf("sb_id16", [128, 128], BF16)
    sb_id32v = sbuf("sb_id32v", [128, 128], F32)

    ps_a = psum("ps_a", [128, 512], F32)
    ps_b = psum("ps_b", [128, 512], F32)
    ps_av = psum("ps_av", [128, 512], F32)
    ps_tr = psum("ps_tr", [128, 512], BF16)
    ps_tr2 = psum("ps_tr2", [128, 512], BF16)

    d_c = p.sem("d_c")
    d_cn = p.sem("d_cn")
    d_x = p.sem("d_x")
    d_sh = p.sem("d_sh")
    d_x2 = p.sem("d_x2")

    p.dma("pool", wqkv_bf.ap(), w_qkv.ap(), d_c)
    p.dma("pool", wo_bf.ap(), w_o.ap(), d_c)
    cast_done = p.ev("d_c")

    p.dma("act", sb_hpB[:, :], bass.AP(hpos, 0, [[0, 128], [1, NRA]]), d_cn)
    p.dma("act", sb_cut[:, :], hpos.ap()[HALO:HALO + 2048].rearrange("(i p) -> p i", p=128), d_cn,
          allow_slow_non_contiguous=True)
    p.dma("act", sb_qs[:, :], bass.AP(q_scale, 0, [[0, 128], [1, 64]]), d_cn)
    p.dma("act", sb_ks[:, :], bass.AP(k_scale, 0, [[0, 128], [1, 64]]), d_cn)
    p.dma("act", sb_id32v[:, :], ident.ap(), d_cn)
    p.dma("act", sb_tabq.ap().rearrange("p (i c) -> p i c", c=128)[:, :, 0:64],
          csq.ap().rearrange("(i p) c -> p i c", p=128), d_cn)
    p.dma("act", sb_tabq.ap().rearrange("p (i c) -> p i c", c=128)[:, :, 64:128],
          ssq.ap().rearrange("(i p) c -> p i c", p=128), d_cn)
    p.dma("act", sb_tabk.ap().rearrange("p (i c) -> p i c", c=128)[:, :, 0:64],
          csk.ap().rearrange("(i p) c -> p i c", p=128), d_cn)
    p.dma("act", sb_tabk.ap().rearrange("p (i c) -> p i c", c=128)[:, :, 64:128],
          ssk.ap().rearrange("(i p) c -> p i c", p=128), d_cn)
    e_cn = p.ev("d_cn")
    e_id16 = p.op("dve", lambda E: E.tensor_copy(sb_id16[:, :], sb_id32v[:, :]), wait=(e_cn,), inc=True)

    pv_sh = None
    pv_tr = None
    prev_ev = None
    for i in range(NTA):
        par = i % 2
        e_lx = p.dma("sp", sb_xt[:, par * D:(par + 1) * D], xh[128 * i:128 * (i + 1), :], d_x,
                     wait=(pv_tr,))
        e_c1 = p.op("dve", lambda E, par=par: E.tensor_copy(sb_sh[:, 0:512], sb_xt[:, par * D:par * D + 512]),
                    wait=(e_lx, pv_sh), inc=True)
        e_d1 = p.dma("sp", sb_sh[1:128, 512:1024], sb_xt[0:127, par * D + 512:par * D + 1024], d_sh,
                     wait=(e_c1,))
        if i == 0:
            e_d2 = p.op("dve", lambda E: E.memset(sb_sh[0:1, 512:1024], 0.0), wait=(e_c1,), inc=True)
            e_row = (e_d1, e_d2)
        else:
            e_d2 = p.dma("sp", sb_sh[0:1, 512:1024], sb_xt[127:128, (1 - par) * D + 512:(1 - par) * D + 1024], d_sh)
            e_row = (e_d2,)
        e_cast = p.op("dve", lambda E: E.tensor_copy(sb_sh16[:, :], sb_sh[:, :]), wait=(e_d1,) + e_row, inc=True)
        pv_sh = e_cast
        for dc in range(8):
            wl = [e_cast] if dc == 0 else [prev_ev]
            if i == 0 and dc == 0:
                wl.append(e_id16)
            et = p.op("pe", lambda E, dc=dc: E.transpose(ps_tr[:, 0:128], sb_sh16[:, 128 * dc:128 * (dc + 1)], sb_id16[:, :]),
                      wait=tuple(wl), inc=True)
            prev_ev = p.op("dve", lambda E, dc=dc, i=i: E.tensor_copy(
                sb_xshT[:, dc * NRA + 128 * i: dc * NRA + 128 * (i + 1)], ps_tr[:, 0:128]),
                wait=(et,), inc=True)
        pv_tr = prev_ev
    ev_phA = p.ev("c_dve")

    psum_last = {0: None, 1: None}
    vst_last = {0: None, 1: None}
    e_wuse = None
    rot_busy = None
    prev_trr = None

    def rotary_pass(kind):
        nonlocal psum_last, vst_last, e_wuse, rot_busy, prev_trr
        e_wl = p.dma("sp", sb_wq.ap().rearrange("p (i c) -> p i c", c=D),
                     wqkv_bf.ap().rearrange("(i p) c -> p i c", p=128)[:, :, D * kind:D * (kind + 1)],
                     p.sem("d_wk%d" % kind), wait=(e_wuse, cast_done if kind == 0 else None))
        tiles = range(3, NTA) if kind == 0 else range(NTA)
        last_mm = None
        for ti in tiles:
            ev_mm2 = {}
            for half in range(2):
                psx = ps_a if half == 0 else ps_b
                ev_mm = None
                for dc in range(8):
                    wl = []
                    if dc == 0:
                        wl = [e_wl, psum_last[half]]
                        if kind == 0 and ti == 3 and half == 0:
                            wl.append(ev_phA)
                    ev_mm = p.op("pe", lambda E, dc=dc, ti=ti, half=half, psx=psx: E.matmul(
                        psx[:, :], sb_xshT[:, dc * NRA + 128 * ti: dc * NRA + 128 * (ti + 1)],
                        sb_wq[:, dc * D + 512 * half:dc * D + 512 * half + 512],
                        start=(dc == 0), stop=(dc == 7)),
                        wait=tuple([w for w in wl if w]), inc=(dc == 7))
                ev_mm2[half] = ev_mm
            last_mm = ev_mm2[1]
            if kind == 2:
                for half in range(2):
                    psx = ps_a if half == 0 else ps_b
                    e_ev = p.op("act", lambda E, half=half, psx=psx: E.activation(
                        sb_out16[:, half * 512:(half + 1) * 512], psx[:, :], AF.Copy),
                        wait=(ev_mm2[half], vst_last[half], rot_busy, prev_trr), inc=True)
                    psum_last[half] = e_ev
                    vst_last[half] = p.dma("act", v_sp[128 * ti:128 * (ti + 1), 512 * half:512 * (half + 1)],
                                           sb_out16[:, half * 512:(half + 1) * 512], p.sem("d_v%d" % half), wait=(e_ev,))
                continue
            e_e0 = p.op("dve", lambda E: E.tensor_copy(sb_qkf[:, 0:512], ps_a[:, :]),
                        wait=(ev_mm2[0], rot_busy), inc=True)
            e_e1 = p.op("dve", lambda E: E.tensor_copy(sb_qkf[:, 512:1024], ps_b[:, :]), wait=(ev_mm2[1],), inc=True)
            psum_last[0] = e_e0
            psum_last[1] = e_e1
            e_sq = p.op("act", lambda E: E.activation(sb_rot[:, :], sb_qkf[:, :], AF.Square),
                        wait=(e_e1, prev_trr), inc=True)
            e_ss = p.op("dve", lambda E: E.reduce_sum(sb_nrm[:, :], sb_rot.ap().rearrange("p (h d) -> p h d", d=64), axis=AX.X),
                        wait=(e_sq,), inc=True)
            e_sr = p.op("act", lambda E: E.activation(sb_nrm[:, :], sb_nrm[:, :], AF.Sqrt), wait=(e_ss,), inc=True)
            e_mx = p.op("dve", lambda E: E.tensor_scalar_max(sb_nrm[:, :], sb_nrm[:, :], 1e-12), wait=(e_sr,), inc=True)
            e_rc = p.op("dve", lambda E: E.reciprocal(sb_nrm[:, :], sb_nrm[:, :]), wait=(e_mx,), inc=True)
            nb = bass.AP(sb_nrm, 0, [[16, 128], [1, 16], [0, 64]])
            e_n1 = p.op("dve", lambda E, nb=nb: E.tensor_tensor(
                sb_qkf.ap().rearrange("p (h d) -> p h d", d=64),
                sb_qkf.ap().rearrange("p (h d) -> p h d", d=64), nb, ALU.mult),
                wait=(e_rc,), inc=True)
            scl = sb_qs if kind == 0 else sb_ks
            sb_ = bass.AP(scl, 0, [[64, 128], [0, 16], [1, 64]])
            e_n2 = p.op("dve", lambda E, sb_=sb_: E.tensor_tensor(
                sb_qkf.ap().rearrange("p (h d) -> p h d", d=64),
                sb_qkf.ap().rearrange("p (h d) -> p h d", d=64), sb_, ALU.mult),
                wait=(e_n1, e_cn), inc=True)
            e_r1 = p.op("dve", lambda E: E.tensor_scalar(
                sb_rot.ap().rearrange("p (h d) -> p h d", d=64)[:, :, 0:32],
                sb_qkf.ap().rearrange("p (h d) -> p h d", d=64)[:, :, 32:64], -1.0, None, ALU.mult),
                wait=(e_n2,), inc=True)
            e_r2 = p.op("dve", lambda E: E.tensor_copy(
                sb_rot.ap().rearrange("p (h d) -> p h d", d=64)[:, :, 32:64],
                sb_qkf.ap().rearrange("p (h d) -> p h d", d=64)[:, :, 0:32]),
                wait=(e_r1,), inc=True)
            tab = sb_tabq if kind == 0 else sb_tabk
            toff = (ti - 3) * 128 if kind == 0 else ti * 128
            cs = bass.AP(tab, toff, [[tab.ap().ap[0][0], 128], [0, 16], [1, 64]])
            ss = bass.AP(tab, toff + 64, [[tab.ap().ap[0][0], 128], [0, 16], [1, 64]])
            e_m1 = p.op("dve", lambda E, cs=cs: E.tensor_tensor(
                sb_qkf.ap().rearrange("p (h d) -> p h d", d=64),
                sb_qkf.ap().rearrange("p (h d) -> p h d", d=64), cs, ALU.mult),
                wait=(e_r2,), inc=True)
            e_m2 = p.op("dve", lambda E, ss=ss: E.tensor_tensor(
                sb_rot.ap().rearrange("p (h d) -> p h d", d=64),
                sb_rot.ap().rearrange("p (h d) -> p h d", d=64), ss, ALU.mult),
                wait=(e_m1,), inc=True)
            e_add = p.op("dve", lambda E: E.tensor_add(sb_out16[:, :], sb_qkf[:, :], sb_rot[:, :]),
                         wait=(e_m2, vst_last[0], vst_last[1]), inc=True)
            rot_busy = e_add
            for dc in range(8):
                wl = [e_add] if dc == 0 else [prev_trr]
                et = p.op("pe", lambda E, dc=dc: E.transpose(ps_tr2[:, 0:128], sb_out16[:, 128 * dc:128 * (dc + 1)], sb_id16[:, :]),
                          wait=tuple(wl), inc=True)
                if kind == 0:
                    dst = sb_qT[:, dc * 2048 + 128 * (ti - 3): dc * 2048 + 128 * (ti - 2)]
                else:
                    dst = sb_kT[:, dc * NRA + 128 * ti: dc * NRA + 128 * (ti + 1)]
                prev_trr = p.op("dve", lambda E, dst=dst: E.tensor_copy(dst, ps_tr2[:, 0:128]), wait=(et,), inc=True)
        e_wuse = last_mm

    rotary_pass(0)
    rotary_pass(1)
    rotary_pass(2)
    ev_phB = p.ev("c_dve")

    p.wait("sp", cast_done)
    p.wait("sp", e_wuse)
    for dc in range(8):
        p.dma("sp", sb_wo[:, dc * D:(dc + 1) * D], wo_bf[dc * 128:(dc + 1) * 128, :], p.sem("d_wo"))
    e_wo = p.ev("d_wo")

    vld_last = {0: None, 1: None}
    res_last = {0: None, 1: None}
    prev_store = None
    prev_oq_written = None
    prev_oT_used = None
    e_ex = None
    ev_msk = None
    for qt in range(NTQ):
        par = qt % 2
        e_vl = p.dma("sp", sb_vt[:, par * 3 * D:(par + 1) * 3 * D].rearrange("p (i c) -> p i c", c=D),
                     v_sp.ap().rearrange("(i p) c -> p i c", p=128)[:, qt + 1:qt + 4, :],
                     p.sem("d_vt%d" % par), wait=(vld_last[par], (p.ev("d_v0") if qt == 0 else None),
                                                  (p.ev("d_v1") if qt == 0 else None)))
        ev_k1 = p.op("dve", lambda E, qt=qt: E.tensor_scalar(
            sb_msk[:, 0:384], sb_hpB[:, 128 * (qt + 1):128 * (qt + 4)], sb_cut[:, qt:qt + 1], None, ALU.subtract),
            wait=tuple([w for w in (e_cn, ev_msk) if w]), inc=True)
        ev_k2 = p.op("dve", lambda E: E.tensor_scalar(sb_msk[:, 512:896], sb_msk[:, 0:384], 0.0, None, ALU.is_le),
                     wait=(ev_k1,), inc=True)
        ev_k3 = p.op("dve", lambda E: E.scalar_tensor_tensor(sb_msk[:, 512:896], sb_msk[:, 0:384], -256.0, sb_msk[:, 512:896], ALU.is_ge, ALU.mult),
                     wait=(ev_k2,), inc=True)
        ev_msk = p.op("dve", lambda E: E.tensor_scalar(sb_msk[:, 512:896], sb_msk[:, 512:896], 1e9, -1e9, ALU.mult, ALU.add),
                     wait=(ev_k3,), inc=True)
        for h in range(16):
            po = 64 * (h % 2)
            dcq = h // 2
            wl = [ev_phB, e_vl] if h == 0 else []
            wl.append(psum_last[0])
            ev_mm = p.op("pe", lambda E, qt=qt, h=h, po=po, dcq=dcq: E.matmul(
                ps_a[:, 0:384], sb_qT[po:po + 64, dcq * 2048 + 128 * qt: dcq * 2048 + 128 * (qt + 1)],
                sb_kT[po:po + 64, dcq * NRA + 128 * (qt + 1): dcq * NRA + 128 * (qt + 4)],
                start=True, stop=True),
                wait=tuple([w for w in wl if w]), inc=True)
            e_sv = p.op("dve", lambda E: E.scalar_tensor_tensor(
                sb_S[:, 0:384], ps_a[:, 0:384], 8.0, sb_msk[:, 512:896], ALU.mult, ALU.add),
                wait=(ev_mm, ev_msk, e_ex), inc=True)
            psum_last[0] = e_sv
            e_nm = p.op("dve", lambda E: E.reduce_max(sb_st[:, 1:2], sb_S[:, 0:384], axis=AX.X, negate=True),
                        wait=(e_sv,), inc=True)
            e_ex = p.op("act", lambda E: E.activation(sb_P[:, 0:384], sb_S[:, 0:384], AF.Exp, bias=sb_st[:, 1:2], accum_out=sb_st[:, 0:1]),
                        wait=(e_nm,), inc=True)
            e_rz = p.op("dve", lambda E: E.reciprocal(sb_st[:, 2:3], sb_st[:, 0:1]), wait=(e_ex,), inc=True)
            ev_avmm = None
            pt_evac = {0: None, 1: None}
            for a in range(3):
                gpar = a % 2
                pstr = ps_tr if gpar == 0 else ps_tr2
                et = p.op("pe", lambda E, a=a, pstr=pstr: E.transpose(
                    pstr[:, 0:128], sb_P[:, 128 * a:128 * (a + 1)], sb_id16[:, :]),
                    wait=tuple([w for w in (e_ex if a == 0 else None, pt_evac[gpar]) if w]), inc=True)
                e_pt = p.op("dve", lambda E, gpar=gpar, pstr=pstr: E.tensor_copy(sb_PT[:, gpar * 128:(gpar + 1) * 128], pstr[:, 0:128]),
                            wait=(et, pt_evac[gpar]), inc=True)
                pt_evac[gpar] = e_pt
                wl = [e_pt]
                if a == 0:
                    wl.append(prev_oq_written)
                ev_avmm = p.op("pe", lambda E, a=a, gpar=gpar, h=h, qt=qt, par=par: E.matmul(
                    ps_av[:, 0:64], sb_PT[:, gpar * 128:(gpar + 1) * 128],
                    sb_vt[:, par * 3 * D + a * D + 64 * h: par * 3 * D + a * D + 64 * h + 64],
                    start=(a == 0), stop=(a == 2)),
                    wait=tuple([w for w in wl if w]), inc=(a == 2))
            prev_oq_written = p.op("dve", lambda E, h=h: E.tensor_scalar(
                sb_oq[:, 64 * h:64 * (h + 1)], ps_av[:, 0:64], sb_st[:, 2:3], None, ALU.mult),
                wait=(ev_avmm, e_rz, prev_oT_used), inc=True)
        vld_last[par] = ev_avmm
        e_cb = p.op("dve", lambda E: E.tensor_copy(sb_out16[:, :], sb_oq[:, :]), wait=(prev_oq_written, psum_last[1], vst_last[0], vst_last[1]), inc=True)
        prev_tr = None
        for dc in range(8):
            et = p.op("pe", lambda E, dc=dc: E.transpose(ps_tr[:, 128:256], sb_out16[:, 128 * dc:128 * (dc + 1)], sb_id16[:, :]),
                      wait=(e_cb, psum_last[0]) if dc == 0 else (prev_tr,), inc=True)
            prev_tr = p.op("dve", lambda E, dc=dc: E.tensor_copy(sb_oT[:, 128 * dc:128 * (dc + 1)], ps_tr[:, 128:256]),
                           wait=(et,), inc=True)
        e_lx = p.dma("sp", sb_xo[:, :], xh[HALO + 128 * qt:HALO + 128 * (qt + 1), :], d_x2, wait=(prev_store,))
        ev_res = []
        ev_mm = None
        for half in range(2):
            psx = ps_a if half == 0 else ps_b
            for dc in range(8):
                wl = []
                if dc == 0:
                    wl = [prev_tr, e_wo, psum_last[half]]
                ev_mm = p.op("pe", lambda E, dc=dc, half=half, psx=psx: E.matmul(
                    psx[:, :], sb_oT[:, 128 * dc:128 * (dc + 1)], sb_wo[:, dc * D + 512 * half: dc * D + 512 * (half + 1)],
                    start=(dc == 0), stop=(dc == 7)),
                    wait=tuple([w for w in wl if w]), inc=(dc == 7))
            e_r2 = p.op("dve", lambda E, half=half, psx=psx: E.tensor_add(
                sb_res[:, half * 512:(half + 1) * 512], psx[:, :], sb_xo[:, 512 * half:512 * (half + 1)]),
                wait=(ev_mm, e_lx, res_last[half]), inc=True)
            psum_last[half] = e_r2
            ev_res.append(e_r2)
        e_st1 = p.dma("act", x_out[128 * qt:128 * (qt + 1), 0:512], sb_res[:, 0:512], p.sem("d_ro0"), wait=(ev_res[0], ev_res[1]))
        e_st2 = p.dma("act", x_out[128 * qt:128 * (qt + 1), 512:1024], sb_res[:, 512:1024], p.sem("d_ro1"))
        res_last[0] = e_st1
        res_last[1] = e_st2
        prev_store = e_st2
        prev_oT_used = ev_mm
        vst_last = {0: None, 1: None}

    p.wait("act", p.ev("d_ro0"))
    p.wait("act", p.ev("d_ro1"))
    return fin()


def build_l2_inner_ffn(fz=None, xio=None, pre="if_"):
    p, nc, pin, dramt, sbuf, psum, fin = _env(fz, pre)
    HALO = 128
    NRF = HALO + 2048
    NTF = NRF // 128
    DI = 2730
    NJ = 22
    NTB = 4
    TB = 512

    xh = xio["xh"] if xio else pin("xh", [NRF, D], F32)
    w1 = pin("w1", [D, 2 * DI], F32)
    b1 = pin("b1", [2 * DI], F32)
    w2 = pin("w2", [DI, D], F32)
    b2 = pin("b2", [D], F32)
    ln_g = pin("ln_g", [D], F32)
    ln_b = pin("ln_b", [D], F32)
    ident = pin("ident", [128, 128], F32)
    x_out = xio["x_out"] if xio else nc.dram_tensor("x_out", [2048, D], F32, kind="ExternalOutput")

    w1_bf = dramt("w1_bf", [D, 2 * DI], BF16)
    w2_bf = dramt("w2_bf", [DI, D], BF16)

    sb_xt = sbuf("sb_xt", [128, 2 * D], F32)
    sb_sh = sbuf("sb_sh", [128, D], F32)
    sb_scr = sbuf("sb_scr", [128, D], F32)
    sb_dump = sbuf("sb_dump", [128, D], F32)
    sb_ln16 = sbuf("sb_ln16", [128, D], BF16)
    sb_xnT = sbuf("sb_xnT", [128, 8 * 2048], BF16)
    sb_w1s = sbuf("sb_w1s", [128, 2 * 2 * 8 * 128], BF16)
    sb_w2 = sbuf("sb_w2", [128, NJ * D], BF16)
    sb_b1a = sbuf("sb_b1a", [128, NJ], F32)
    sb_b1g = sbuf("sb_b1g", [128, NJ], F32)
    sb_b2B = sbuf("sb_b2B", [128, D], F32)
    sb_gB = sbuf("sb_gB", [128, D], F32)
    sb_bB = sbuf("sb_bB", [128, D], F32)
    sb_ab = sbuf("sb_ab", [128, TB], BF16)
    sb_hT = sbuf("sb_hT", [128, NJ * TB], BF16)
    sb_res = sbuf("sb_res", [128, D], F32)
    sb_xo = sbuf("sb_xo", [128, D], F32)
    sb_st = sbuf("sb_st", [128, 8], F32)
    sb_id16 = sbuf("sb_id16", [128, 128], BF16)
    sb_id32v = sbuf("sb_id32v", [128, 128], F32)

    ps_a = psum("ps_a", [128, 512], F32)
    ps_b = psum("ps_b", [128, 512], F32)
    ps_c = psum("ps_c", [128, 512], F32)
    ps_tr = psum("ps_tr", [128, 512], BF16)

    d_c = p.sem("d_c")
    d_cn = p.sem("d_cn")
    d_x = p.sem("d_x")
    d_sh = p.sem("d_sh")
    d_x2 = p.sem("d_x2")

    p.dma("pool", w1_bf.ap(), w1.ap(), d_c)
    p.dma("pool", w2_bf.ap(), w2.ap(), d_c)
    cast_done = p.ev("d_c")

    p.dma("act", sb_b2B[:, :], bass.AP(b2, 0, [[0, 128], [1, D]]), d_cn)
    p.dma("act", sb_gB[:, :], bass.AP(ln_g, 0, [[0, 128], [1, D]]), d_cn)
    p.dma("act", sb_bB[:, :], bass.AP(ln_b, 0, [[0, 128], [1, D]]), d_cn)
    p.dma("act", sb_id32v[:, :], ident.ap(), d_cn)
    for j in range(NJ):
        cw = min(128, DI - 128 * j)
        p.dma("act", sb_b1a[0:cw, j:j + 1], b1.ap()[128 * j:128 * j + cw].rearrange("(k o) -> k o", o=1), d_cn)
        p.dma("act", sb_b1g[0:cw, j:j + 1], b1.ap()[DI + 128 * j:DI + 128 * j + cw].rearrange("(k o) -> k o", o=1), d_cn)
    e_cn = p.ev("d_cn")
    e_id16 = p.op("dve", lambda E: E.tensor_copy(sb_id16[:, :], sb_id32v[:, :]), wait=(e_cn,), inc=True)

    p.wait("sp", cast_done)
    for j in range(NJ):
        cw = min(128, DI - 128 * j)
        p.dma("sp", sb_w2[0:cw, j * D:(j + 1) * D], w2_bf[128 * j:128 * j + cw, :], p.sem("d_w2"))
    e_w2l = p.ev("d_w2")

    pv_sh = None
    pv_tr = None
    prev_ev = None
    for i in range(16):
        ii = i + 1
        par = i % 2
        e_lx = p.dma("sp", sb_xt[:, par * D:(par + 1) * D], xh[128 * ii:128 * (ii + 1), :], d_x,
                     wait=(pv_tr,))
        if i == 0:
            p.dma("sp", sb_xt[:, D:2 * D], xh[0:128, :], d_x)
        e_jx = p.ev("d_x")
        e_c1 = p.op("dve", lambda E, par=par: E.tensor_copy(sb_sh[:, 0:512], sb_xt[:, par * D:par * D + 512]),
                    wait=(e_jx, pv_sh), inc=True)
        e_d1 = p.dma("sp", sb_sh[1:128, 512:1024], sb_xt[0:127, par * D + 512:par * D + 1024], d_sh,
                     wait=(e_c1,))
        e_d2 = p.dma("sp", sb_sh[0:1, 512:1024], sb_xt[127:128, (1 - par) * D + 512:(1 - par) * D + 1024], d_sh)
        e1 = p.op("dve", lambda E: E.reduce_sum(sb_st[:, 0:1], sb_sh[:, :], axis=AX.X),
                  wait=(e_d1, e_d2), inc=True)
        e2 = p.op("dve", lambda E: E.tensor_scalar(sb_st[:, 0:1], sb_st[:, 0:1], -1.0 / D, None, ALU.mult),
                  wait=(e1,), inc=True)
        e_xc = p.op("act", lambda E: E.activation(sb_scr[:, :], sb_sh[:, :], AF.Identity, bias=sb_st[:, 0:1]),
                    wait=(e2,), inc=True)
        pv_sh = e_xc
        e_v = p.op("act", lambda E: E.activation(sb_dump[:, :], sb_scr[:, :], AF.Square, accum_out=sb_st[:, 1:2]),
                   wait=(e_xc,), inc=True)
        e5 = p.op("dve", lambda E: E.tensor_scalar(sb_st[:, 1:2], sb_st[:, 1:2], 1.0 / D, 1e-5, ALU.mult, ALU.add),
                  wait=(e_v,), inc=True)
        e6 = p.op("act", lambda E: E.activation(sb_st[:, 2:3], sb_st[:, 1:2], AF.Sqrt), wait=(e5,), inc=True)
        e7 = p.op("dve", lambda E: E.reciprocal(sb_st[:, 3:4], sb_st[:, 2:3]), wait=(e6,), inc=True)
        e_ln = p.op("dve", lambda E: E.tensor_scalar(sb_scr[:, :], sb_scr[:, :], sb_st[:, 3:4], None, ALU.mult),
                    wait=(e7,), inc=True)
        e_g1 = p.op("dve", lambda E: E.tensor_mul(sb_scr[:, :], sb_scr[:, :], sb_gB[:, :]), wait=(e_ln, e_cn), inc=True)
        e_g2 = p.op("dve", lambda E: E.tensor_add(sb_scr[:, :], sb_scr[:, :], sb_bB[:, :]), wait=(e_g1,), inc=True)
        e_cast = p.op("dve", lambda E: E.tensor_copy(sb_ln16[:, :], sb_scr[:, :]), wait=(e_g2,), inc=True)
        for dc in range(8):
            wl = [e_cast] if dc == 0 else [prev_ev]
            if i == 0 and dc == 0:
                wl.append(e_id16)
            et = p.op("pe", lambda E, dc=dc: E.transpose(ps_tr[:, 0:128], sb_ln16[:, 128 * dc:128 * (dc + 1)], sb_id16[:, :]),
                      wait=tuple(wl), inc=True)
            prev_ev = p.op("dve", lambda E, dc=dc, i=i: E.tensor_copy(
                sb_xnT[:, dc * 2048 + 128 * i: dc * 2048 + 128 * (i + 1)], ps_tr[:, 0:128]),
                wait=(et,), inc=True)
        pv_tr = prev_ev
    ev_phA = p.ev("c_dve")

    psum_last = {0: None, 1: None, 2: None}
    w1_use = {0: None, 1: None}
    e_habuse = None
    res_last = {0: None, 1: None}
    prev_store = None
    e_h = None
    for tb in range(NTB):
        for j in range(NJ):
            cw = min(128, DI - 128 * j)
            par = j % 2
            if True:
                e_wl = p.dma("sp", sb_w1s[:, par * 2048:par * 2048 + 8 * 128].rearrange("p (i c) -> p i c", c=128)[:, :, 0:cw],
                             w1_bf.ap().rearrange("(i p) c -> p i c", p=128)[:, :, 128 * j:128 * j + cw],
                             p.sem("d_w1%d" % par), wait=(w1_use[par],))
                e_wl2 = p.dma("sp", sb_w1s[:, par * 2048 + 1024:par * 2048 + 1024 + 8 * 128].rearrange("p (i c) -> p i c", c=128)[:, :, 0:cw],
                              w1_bf.ap().rearrange("(i p) c -> p i c", p=128)[:, :, DI + 128 * j:DI + 128 * j + cw],
                              p.sem("d_w1%d" % par))
                e_wld = p.ev("d_w1%d" % par)
            ev_a = ev_g = None
            for dc in range(8):
                wl = []
                if dc == 0:
                    wl = [e_wld, psum_last[0]]
                    if tb == 0 and j == 0:
                        wl.append(ev_phA)
                        wl.append(cast_done)
                ev_a = p.op("pe", lambda E, dc=dc, j=j, cw=cw, tb=tb, par=par: E.matmul(
                    ps_a[0:cw, :], sb_w1s[:, par * 2048 + dc * 128:par * 2048 + dc * 128 + cw],
                    sb_xnT[:, dc * 2048 + TB * tb: dc * 2048 + TB * (tb + 1)],
                    start=(dc == 0), stop=(dc == 7)),
                    wait=tuple([w for w in wl if w]), inc=(dc == 7))
            for dc in range(8):
                wl = [psum_last[1]] if dc == 0 else []
                ev_g = p.op("pe", lambda E, dc=dc, j=j, cw=cw, tb=tb, par=par: E.matmul(
                    ps_b[0:cw, :], sb_w1s[:, par * 2048 + 1024 + dc * 128:par * 2048 + 1024 + dc * 128 + cw],
                    sb_xnT[:, dc * 2048 + TB * tb: dc * 2048 + TB * (tb + 1)],
                    start=(dc == 0), stop=(dc == 7)),
                    wait=tuple([w for w in wl if w]), inc=(dc == 7))
            w1_use[par] = ev_g
            e_ab = p.op("act", lambda E, j=j, cw=cw: E.activation(
                sb_ab[0:cw, :], ps_a[0:cw, :], AF.Identity, bias=sb_b1a[0:cw, j:j + 1]),
                wait=(ev_a, e_cn, e_habuse), inc=True)
            psum_last[0] = e_ab
            e_gl = p.op("act", lambda E, j=j, cw=cw: E.activation(
                sb_hT[0:cw, TB * j:TB * j + TB], ps_b[0:cw, :], AF.Gelu, bias=sb_b1g[0:cw, j:j + 1]),
                wait=(ev_g, e_h), inc=True)
            psum_last[1] = e_gl
            e_h = p.op("dve", lambda E, j=j, cw=cw: E.tensor_tensor(
                sb_hT[0:cw, TB * j:TB * j + TB], sb_hT[0:cw, TB * j:TB * j + TB], sb_ab[0:cw, :], ALU.mult),
                wait=(e_gl, e_ab), inc=True)
            e_habuse = e_h
        e_hT_done = e_h
        for tc in range(4):
            ti = 4 * tb + tc
            e_lx = p.dma("sp", sb_xo[:, :], xh[HALO + 128 * ti:HALO + 128 * (ti + 1), :], d_x2, wait=(prev_store,))
            ev_res = []
            for half in range(2):
                ev_d = None
                for j in range(NJ):
                    cw = min(128, DI - 128 * j)
                    wl = []
                    if j == 0:
                        wl = [e_hT_done, psum_last[2]]
                        if tb == 0 and tc == 0 and half == 0:
                            wl.append(e_w2l)
                    ev_d = p.op("pe", lambda E, j=j, cw=cw, tc=tc, half=half: E.matmul(
                        ps_c[:, :], sb_hT[0:cw, TB * j + 128 * tc: TB * j + 128 * (tc + 1)],
                        sb_w2[0:cw, j * D + 512 * half: j * D + 512 * half + 512],
                        start=(j == 0), stop=(j == NJ - 1)),
                        wait=tuple([w for w in wl if w]), inc=(j == NJ - 1))
                e_r1 = p.op("dve", lambda E, half=half: E.tensor_add(
                    sb_res[:, half * 512:(half + 1) * 512], ps_c[:, :], sb_b2B[:, 512 * half:512 * (half + 1)]),
                    wait=(ev_d, res_last[half]), inc=True)
                e_r2 = p.op("dve", lambda E, half=half: E.tensor_add(
                    sb_res[:, half * 512:(half + 1) * 512], sb_res[:, half * 512:(half + 1) * 512], sb_xo[:, 512 * half:512 * (half + 1)]),
                    wait=(e_r1, e_lx), inc=True)
                psum_last[2] = e_r2
                ev_res.append(e_r2)
            e_st1 = p.dma("act", x_out[128 * ti:128 * (ti + 1), 0:512], sb_res[:, 0:512], p.sem("d_ro0"), wait=(ev_res[0], ev_res[1]))
            e_st2 = p.dma("act", x_out[128 * ti:128 * (ti + 1), 512:1024], sb_res[:, 512:1024], p.sem("d_ro1"))
            res_last[0] = e_st1
            res_last[1] = e_st2
            prev_store = e_st2

    p.wait("act", p.ev("d_ro0"))
    p.wait("act", p.ev("d_ro1"))
    return fin()



def build_l3_attn(fz=None, xio=None, pre="a3_"):
    p, nc, pin, dramt, sbuf, psum, fin = _env(fz, pre)
    NB = 4

    xb4 = xio["xb4"] if xio else pin("xb4", [NB * TOK_PER_CORE, D], F32)
    w_in = pin("w_in", [3 * D, D], F32)
    b_in = pin("b_in", [3 * D], F32)
    gamma = pin("gamma", [3 * D], F32)
    alpha = pin("alpha", [D], F32)
    lgam = pin("lgam", [D], F32)
    w_out = pin("w_out", [D, D], F32)
    b_out = pin("b_out", [D], F32)
    maskvec = pin("maskvec", [64], F32)
    ident = pin("ident", [128, 128], F32)
    x_out = xio["x_out"] if xio else nc.dram_tensor("x_out", [TOK_PER_CORE, D], F32, kind="ExternalOutput")

    win_bf = dramt("win_bf", [3 * D, D], BF16)
    wout_bf = dramt("wout_bf", [D, D], BF16)
    bq_sp = dramt("bq_sp", [3 * D], F32)
    q_sp = dramt("q_sp", [TOK_PER_CORE, D], BF16)
    kv_sp = dramt("kv_sp", [NB, TOK_PER_CORE, 2 * D], BF16)

    sb_wprep = sbuf("sb_wprep", [128, D], F32)
    sb_wprep16 = sbuf("sb_wprep16", [128, D], BF16)
    sb_gcols = sbuf("sb_gcols", [128, 24], F32)
    sb_bcols = sbuf("sb_bcols", [128, 24], F32)
    sb_bqB = sbuf("sb_bqB", [128, 3 * D], F32)
    sb_aB = sbuf("sb_aB", [128, D], F32)
    sb_lgB = sbuf("sb_lgB", [128, D], F32)
    sb_boB = sbuf("sb_boB", [128, D], F32)
    sb_mvB = sbuf("sb_mvB", [128, 64], F32)
    sb_wT = sbuf("sb_wT", [128, 2 * 8 * 512], BF16)
    sb_stg = sbuf("sb_stg", [128, 2 * 512], BF16)
    sb_q = sbuf("sb_q", [128, D], BF16)
    sb_kj = sbuf("sb_kj", [128, NB * D], BF16)
    sb_vj = sbuf("sb_vj", [128, NB * D], BF16)
    sb_mul = sbuf("sb_mul", [128, D], F32)
    sb_sc = sbuf("sb_sc", [128, 64], F32)
    sb_e = sbuf("sb_e", [128, 64], F32)
    sb_zs = sbuf("sb_zs", [128, 48], F32)
    sb_oac = sbuf("sb_oac", [128, D], F32)
    sb_o16 = sbuf("sb_o16", [128, D], BF16)
    sb_oT = sbuf("sb_oT", [128, 8 * 128], BF16)
    sb_res = sbuf("sb_res", [128, D], F32)
    sb_xo = sbuf("sb_xo", [128, D], F32)
    sb_id16 = sbuf("sb_id16", [128, 128], BF16)

    ps_a = psum("ps_a", [128, 512], F32)
    ps_b = psum("ps_b", [128, 512], F32)
    ps_tr = psum("ps_tr", [128, 512], BF16)

    d_c = p.sem("d_c")
    d_cn = p.sem("d_cn")
    d_pr = p.sem("d_pr")
    d_x2 = p.sem("d_x2")

    p.dma("pool", wout_bf.ap(), w_out.ap(), d_c)
    cast_done = p.ev("d_c")

    p.dma("act", sb_gcols[:, :], gamma.ap().rearrange("(i p) -> p i", p=128), d_cn,
          allow_slow_non_contiguous=True)
    p.dma("act", sb_bcols[:, :], b_in.ap().rearrange("(i p) -> p i", p=128), d_cn,
          allow_slow_non_contiguous=True)
    p.dma("act", sb_aB[:, :], bass.AP(alpha, 0, [[0, 128], [1, D]]), d_cn)
    p.dma("act", sb_lgB[:, :], bass.AP(lgam, 0, [[0, 128], [1, D]]), d_cn)
    p.dma("act", sb_boB[:, :], bass.AP(b_out, 0, [[0, 128], [1, D]]), d_cn)
    p.dma("act", sb_mvB[:, :], bass.AP(maskvec, 0, [[0, 128], [1, 64]]), d_cn)
    e_cn = p.ev("d_cn")

    e_g1 = p.op("dve", lambda E: E.tensor_scalar(sb_gcols[:, 0:8], sb_gcols[:, 0:8], 0.125, None, ALU.mult),
                wait=(e_cn,), inc=True)
    e_b1 = p.op("dve", lambda E: E.tensor_mul(sb_bcols[:, :], sb_bcols[:, :], sb_gcols[:, :]),
                wait=(e_g1,), inc=True)
    e_bw = p.dma("sp", bq_sp.ap().rearrange("(i p) -> p i", p=128), sb_bcols[:, :], d_pr,
                 wait=(e_b1,), allow_slow_non_contiguous=True)
    e_bB = p.dma("act", sb_bqB[:, :], bass.AP(bq_sp, 0, [[0, 128], [1, 3 * D]]), p.sem("d_bb"), wait=(e_bw,))

    prep_st = None
    for oc in range(24):
        e_l = p.dma("sp", sb_wprep[:, :], w_in[128 * oc:128 * (oc + 1), :], d_pr, wait=(prep_st,))
        e_s = p.op("act", lambda E, oc=oc: E.activation(sb_wprep16[:, :], sb_wprep[:, :], AF.Copy, scale=sb_gcols[:, oc:oc + 1]),
                   wait=(e_l, e_g1), inc=True)
        prep_st = p.dma("act", win_bf[128 * oc:128 * (oc + 1), :], sb_wprep16[:, :], p.sem("d_pw"), wait=(e_s,))
    e_wprep = p.ev("d_pw")

    psum_last = {0: None, 1: None}
    stg_last = {0: None, 1: None}
    wT_use = {0: None, 1: None}
    s = None
    e_id16 = None
    last_mm_j = None
    for j in range(NB):
        s = emit_prelude(p, nc, xb4, alloc=(sbuf, psum, dramt), router=False, store_ln=False, ident_dram=ident,
                         n_rows=TOK_PER_CORE, row_base=TOK_PER_CORE * j,
                         reuse=s, first_dep=last_mm_j)
        if j == 0:
            e_id16 = p.op("dve", lambda E: E.tensor_copy(sb_id16[:, :], s.sb_id32[:, :]), wait=(s.e_in0,), inc=True)
        ots = range(6) if j == 0 else range(2, 6)
        ev_mm = None
        for ot in ots:
            par = ot % 2
            for dc in range(8):
                p.dma("sp", sb_wT[:, par * 4096 + dc * 512:par * 4096 + dc * 512 + 512],
                      win_bf[512 * ot:512 * (ot + 1), 128 * dc:128 * (dc + 1)], p.sem("d_wt%d" % par),
                      wait=(wT_use[par] if dc == 0 else None,
                            e_wprep if (j == 0 and ot <= 1 and dc == 0) else None), transpose=True)
            e_wl = p.ev("d_wt%d" % par)
            for tc in range(NT):
                psx = ps_a if tc % 2 == 0 else ps_b
                for dc in range(8):
                    wl = []
                    if dc == 0:
                        wl = [e_wl, psum_last[tc % 2]]
                        if tc == 0:
                            wl.append(s.ev_dve_done)
                    ev_mm = p.op("pe", lambda E, dc=dc, tc=tc, par=par, psx=psx, s=s: E.matmul(
                        psx[:, :], s.sb_xnT[:, dc * TOK_PER_CORE + 128 * tc: dc * TOK_PER_CORE + 128 * (tc + 1)],
                        sb_wT[:, par * 4096 + dc * 512:par * 4096 + dc * 512 + 512],
                        start=(dc == 0), stop=(dc == 7)),
                        wait=tuple([w for w in wl if w]), inc=(dc == 7))
                spar = tc % 2
                e_ev = p.op("dve", lambda E, ot=ot, spar=spar, psx=psx: E.tensor_add(
                    sb_stg[:, spar * 512:(spar + 1) * 512], psx[:, :], sb_bqB[:, 512 * ot:512 * (ot + 1)]),
                    wait=tuple([w for w in (ev_mm, e_bB, stg_last[spar]) if w]), inc=True)
                psum_last[spar] = e_ev
                if ot < 2:
                    dst = q_sp[128 * tc:128 * (tc + 1), 512 * ot:512 * (ot + 1)]
                else:
                    dst = kv_sp[j][128 * tc:128 * (tc + 1), 512 * (ot - 2):512 * (ot - 1)]
                e_st = p.dma("act", dst, sb_stg[:, spar * 512:(spar + 1) * 512], p.sem("d_sp%d" % spar), wait=(e_ev,))
                stg_last[spar] = e_st
            wT_use[par] = ev_mm
        last_mm_j = ev_mm
    ev_spills = (p.ev("d_sp0"), p.ev("d_sp1"))

    p.wait("sp", wT_use[0])
    p.wait("sp", wT_use[1])
    p.wait("sp", cast_done)
    for dc in range(8):
        p.dma("sp", sb_wT[:, dc * 1024:dc * 1024 + 1024], wout_bf[:, 128 * dc:128 * (dc + 1)],
              p.sem("d_wo"), transpose=True)
    e_wo = p.ev("d_wo")

    ld_last = None
    prev_write = None
    prev_store = None
    prev_oT_used = None
    e_pexp = None
    e_m2 = None
    res_last = {0: None, 1: None}
    for i in range(NT):
        e_lq = p.dma("sp", sb_q[:, :], q_sp[128 * i:128 * (i + 1), :], p.sem("d_q"),
                     wait=(ld_last, ev_spills[0] if i == 0 else None, ev_spills[1] if i == 0 else None))
        for j in range(NB):
            p.dma("sp", sb_kj[:, j * D:(j + 1) * D], kv_sp[j][128 * i:128 * (i + 1), 0:D], p.sem("d_q"))
            p.dma("sp", sb_vj[:, j * D:(j + 1) * D], kv_sp[j][128 * i:128 * (i + 1), D:2 * D], p.sem("d_q"))
        e_lkv = p.ev("d_q")
        ev_sc = None
        for j in range(NB):
            e_m = p.op("dve", lambda E, j=j: E.tensor_mul(sb_mul[:, :], sb_q[:, :], sb_kj[:, j * D:(j + 1) * D]),
                       wait=(e_lkv, prev_write) if j == 0 else (ev_sc,), inc=True)
            ev_sc = p.op("dve", lambda E, j=j: E.reduce_sum(
                bass.AP(sb_sc, j, [[64, 128], [4, 16]]),
                sb_mul.ap().rearrange("p (h d) -> p h d", d=64), axis=AX.X),
                wait=(e_m,), inc=True)
        e_am = p.op("dve", lambda E: E.tensor_add(sb_sc[:, :], sb_sc[:, :], sb_mvB[:, :]), wait=(ev_sc, e_cn), inc=True)
        e_mx = p.op("dve", lambda E: E.tensor_reduce(sb_zs[:, 0:16], sb_sc.ap().rearrange("p (h j) -> p h j", j=4), AX.X, ALU.max),
                    wait=(e_am,), inc=True)
        e_sb = p.op("dve", lambda E: E.tensor_tensor(
            sb_sc.ap().rearrange("p (h j) -> p h j", j=4), sb_sc.ap().rearrange("p (h j) -> p h j", j=4),
            bass.AP(sb_zs, 0, [[48, 128], [1, 16], [0, 4]]), ALU.subtract),
            wait=(e_mx,), inc=True)
        e_ex = p.op("act", lambda E: E.activation(sb_e[:, :], sb_sc[:, :], AF.Exp), wait=(e_sb, e_pexp), inc=True)
        e_z = p.op("dve", lambda E: E.reduce_sum(sb_zs[:, 16:32], sb_e.ap().rearrange("p (h j) -> p h j", j=4), axis=AX.X),
                   wait=(e_ex,), inc=True)
        e_rz = p.op("dve", lambda E: E.reciprocal(sb_zs[:, 32:48], sb_zs[:, 16:32]), wait=(e_z,), inc=True)
        e_pp = p.op("dve", lambda E: E.tensor_tensor(
            sb_e.ap().rearrange("p (h j) -> p h j", j=4), sb_e.ap().rearrange("p (h j) -> p h j", j=4),
            bass.AP(sb_zs, 32, [[48, 128], [1, 16], [0, 4]]), ALU.mult),
            wait=(e_rz,), inc=True)
        e_pexp = e_pp
        e_acc = None
        for j in range(NB):
            e_m2 = p.op("dve", lambda E, j=j: E.tensor_tensor(
                sb_mul.ap().rearrange("p (h d) -> p h d", d=64),
                sb_vj[:, j * D:(j + 1) * D].rearrange("p (h d) -> p h d", d=64),
                bass.AP(sb_e, j, [[64, 128], [4, 16], [0, 64]]), ALU.mult),
                wait=(e_pp,) if j == 0 else (e_acc,), inc=True)
            if j == 0:
                e_acc = p.op("dve", lambda E: E.tensor_copy(sb_oac[:, :], sb_mul[:, :]), wait=(e_m2,), inc=True)
            else:
                e_acc = p.op("dve", lambda E: E.tensor_add(sb_oac[:, :], sb_oac[:, :], sb_mul[:, :]), wait=(e_m2,), inc=True)
        prev_write = e_acc
        e_al = p.op("dve", lambda E: E.tensor_mul(sb_oac[:, :], sb_oac[:, :], sb_aB[:, :]), wait=(e_acc,), inc=True)
        e_cb = p.op("dve", lambda E: E.tensor_copy(sb_o16[:, :], sb_oac[:, :]), wait=(e_al,), inc=True)
        prev_tr = None
        for dc in range(8):
            wl = [e_cb, e_id16, prev_oT_used] if dc == 0 else [prev_tr]
            et = p.op("pe", lambda E, dc=dc: E.transpose(ps_tr[:, 0:128], sb_o16[:, 128 * dc:128 * (dc + 1)], sb_id16[:, :]),
                      wait=tuple([w for w in wl if w]), inc=True)
            prev_tr = p.op("dve", lambda E, dc=dc: E.tensor_copy(sb_oT[:, 128 * dc:128 * (dc + 1)], ps_tr[:, 0:128]),
                           wait=(et,), inc=True)
        e_lx = p.dma("sp", sb_xo[:, :], xb4[128 * i:128 * (i + 1), :], d_x2, wait=(prev_store,))
        ev_mm = None
        ev_res = []
        for half in range(2):
            psx = ps_a if half == 0 else ps_b
            for dc in range(8):
                wl = []
                if dc == 0:
                    wl = [prev_tr, e_wo, psum_last[half]]
                ev_mm = p.op("pe", lambda E, dc=dc, half=half, psx=psx: E.matmul(
                    psx[:, :], sb_oT[:, 128 * dc:128 * (dc + 1)], sb_wT[:, dc * 1024 + 512 * half: dc * 1024 + 512 * (half + 1)],
                    start=(dc == 0), stop=(dc == 7)),
                    wait=tuple([w for w in wl if w]), inc=(dc == 7))
            e_r1 = p.op("dve", lambda E, half=half, psx=psx: E.tensor_add(
                sb_res[:, half * 512:(half + 1) * 512], psx[:, :], sb_boB[:, 512 * half:512 * (half + 1)]),
                wait=tuple([w for w in (ev_mm, res_last[half]) if w]), inc=True)
            e_r2 = p.op("dve", lambda E, half=half: E.tensor_mul(
                sb_res[:, half * 512:(half + 1) * 512], sb_res[:, half * 512:(half + 1) * 512], sb_lgB[:, 512 * half:512 * (half + 1)]),
                wait=(e_r1,), inc=True)
            e_r3 = p.op("dve", lambda E, half=half: E.tensor_add(
                sb_res[:, half * 512:(half + 1) * 512], sb_res[:, half * 512:(half + 1) * 512], sb_xo[:, 512 * half:512 * (half + 1)]),
                wait=(e_r2, e_lx), inc=True)
            psum_last[half] = e_r1
            ev_res.append(e_r3)
        e_st1 = p.dma("act", x_out[128 * i:128 * (i + 1), 0:512], sb_res[:, 0:512], p.sem("d_ro0"), wait=(ev_res[0], ev_res[1]))
        e_st2 = p.dma("act", x_out[128 * i:128 * (i + 1), 512:1024], sb_res[:, 512:1024], p.sem("d_ro1"))
        res_last = {0: e_st1, 1: e_st2}
        prev_store = e_st2
        prev_oT_used = ev_mm
        ld_last = e_m2
    p.wait("act", p.ev("d_ro0"))
    p.wait("act", p.ev("d_ro1"))
    return fin()


_IDENT = np.eye(128, dtype=np.float32)
_NC_CACHE = {}


def _get_nc(key, builder, *args):
    if key not in _NC_CACHE:
        _NC_CACHE[key] = builder(*args)
    return _NC_CACHE[key]


def _run_sharded(nc, x, extra):
    shards = x.reshape(N_CORES, TOK_PER_CORE, D)
    in_maps = [dict(extra, x=np.ascontiguousarray(shards[c])) for c in range(N_CORES)]
    res = run_bass_kernel_spmd(nc, in_maps, core_ids=list(range(N_CORES)))
    return np.concatenate([r["x_out"] for r in res.results], axis=0)


def _attn_in_maps(x, extra):
    x4 = x.reshape(B, S, D)
    in_maps = []
    for c in range(N_CORES):
        b, h = c // 2, c % 2
        own = x4[b, 2048 * h:2048 * (h + 1)]
        oth = x4[b, 2048 * (1 - h):2048 * (2 - h)]
        xb = np.ascontiguousarray(np.concatenate([own, oth], axis=0))
        kpos = np.concatenate([np.arange(2048 * h, 2048 * (h + 1)),
                               np.arange(2048 * (1 - h), 2048 * (2 - h))]).astype(np.float32)
        in_maps.append(dict(extra, xb=xb, kpos=kpos))
    return in_maps


def _attn_gather(res, x):
    out = x.copy().reshape(B, S, D)
    for c in range(N_CORES):
        b, h = c // 2, c % 2
        out[b, 2048 * h:2048 * (h + 1)] = res.results[c]["x_out"]
    return out.reshape(B * S, D)


def _run_l0_attn(x, inputs):
    nc = _get_nc("l0_attn", build_l0_attn)
    in_maps = _attn_in_maps(x, {
        "w_qkv": inputs["L0_attn_W_qkv"],
        "b_qkv": inputs["L0_attn_b_qkv"],
        "w_o": inputs["L0_attn_W_o"],
        "b_o": inputs["L0_attn_b_o"],
        "ident": _IDENT,
    })
    res = run_bass_kernel_spmd(nc, in_maps, core_ids=list(range(N_CORES)))
    return _attn_gather(res, x)


def _run_l2_attn(x, inputs):
    nc = _get_nc("l2_attn", build_l2_attn)
    w_cat = np.concatenate([
        inputs["L2_attn_wq1"].transpose(1, 0, 2).reshape(D, D),
        inputs["L2_attn_wq2"].transpose(1, 0, 2).reshape(D, D),
        inputs["L2_attn_wk1"].transpose(1, 0, 2).reshape(D, D),
        inputs["L2_attn_wk2"].transpose(1, 0, 2).reshape(D, D),
        inputs["L2_attn_wv"].transpose(1, 0, 2).reshape(D, D)], axis=1)
    in_maps = _attn_in_maps(x, {
        "w_cat": np.ascontiguousarray(w_cat),
        "ln_g": inputs["L2_attn_ln_gamma"],
        "ln_b": inputs["L2_attn_ln_beta"],
        "lamb": np.asarray(inputs["L2_attn_lamb"], dtype=np.float32).reshape(1),
        "ident": _IDENT,
    })
    res = run_bass_kernel_spmd(nc, in_maps, core_ids=list(range(N_CORES)))
    return _attn_gather(res, x)


def _run_l1_attn(x, inputs):
    nc = _get_nc("l1_attn", build_l1_attn)
    H = 16
    gam = 1.0 - 2.0 ** (-5.0 - np.arange(H, dtype=np.float64))
    idx = np.arange(128)
    dmat = idx[:, None] - idx[None, :]
    tabU = np.where(dmat[None] >= 0, gam[:, None, None] ** np.maximum(dmat, 0)[None], 0.0)
    tabU = tabU.astype(ml_dtypes.bfloat16)
    gpcol = np.ascontiguousarray((gam[None, :] ** idx[:, None]).astype(ml_dtypes.bfloat16))
    gdrow = np.ascontiguousarray((gam[:, None] ** (128 - idx[None, :])).astype(ml_dtypes.bfloat16))
    w_cat = np.concatenate([inputs["L1_attn_Wq"], inputs["L1_attn_Wk"], inputs["L1_attn_Wv"]], axis=1)
    base = {
        "w_cat": np.ascontiguousarray(w_cat),
        "pkv": inputs["L1_attn_past_kv"],
        "gnw_e": np.ascontiguousarray(np.repeat(inputs["L1_attn_gn_w"], 64)),
        "gnb_e": np.ascontiguousarray(np.repeat(inputs["L1_attn_gn_b"], 64)),
        "tabU": np.ascontiguousarray(tabU),
        "gpcol": gpcol,
        "gdrow": gdrow,
        "ident": _IDENT,
    }
    in_maps = _attn_in_maps(x, base)
    for c in range(N_CORES):
        h = c % 2
        qpos = np.arange(2048 * h, 2048 * (h + 1), dtype=np.float64)
        if h == 0:
            cv = gam[:, None] ** (2048.0 - qpos[None, :])
        else:
            cv = np.zeros((H, 2048))
        in_maps[c]["cvec"] = np.ascontiguousarray(cv.astype(ml_dtypes.bfloat16))
    res = run_bass_kernel_spmd(nc, in_maps, core_ids=list(range(N_CORES)))
    return _attn_gather(res, x)




def build_ln_only(fz=None, xio=None, pre="ln_"):
    p, nc, pin, dramt, sbuf, psum, fin = _env(fz, pre)
    x_in = xio["x"] if xio else pin("x", [TOK_PER_CORE, D], F32)
    x_out = xio["x_out"] if xio else nc.dram_tensor("x_out", [TOK_PER_CORE, D], F32, kind="ExternalOutput")
    sb_x = sbuf("sb_x", [128, D], F32)
    sb_scr = sbuf("sb_scr", [128, D], F32)
    sb_dump = sbuf("sb_dump", [128, D], F32)
    sb_ln = sbuf("sb_ln", [128, D], F32)
    sb_st = sbuf("sb_st", [128, 8], F32)
    d_x = p.sem("d_x")
    d_o = p.sem("d_o")
    pv_xc = None
    pv_st = None
    for i in range(NT):
        e_lx = p.dma("sp", sb_x[:, :], x_in[128 * i:128 * (i + 1), :], d_x, wait=(pv_xc,))
        e1 = p.op("dve", lambda E: E.reduce_sum(sb_st[:, 0:1], sb_x[:, :], axis=AX.X), wait=(e_lx,), inc=True)
        e2 = p.op("dve", lambda E: E.tensor_scalar(sb_st[:, 0:1], sb_st[:, 0:1], -1.0 / D, None, ALU.mult),
                  wait=(e1,), inc=True)
        e_xc = p.op("act", lambda E: E.activation(sb_scr[:, :], sb_x[:, :], AF.Identity, bias=sb_st[:, 0:1]),
                    wait=(e2,), inc=True)
        pv_xc = e_xc
        e_v = p.op("act", lambda E: E.activation(sb_dump[:, :], sb_scr[:, :], AF.Square, accum_out=sb_st[:, 1:2]),
                   wait=(e_xc,), inc=True)
        e5 = p.op("dve", lambda E: E.tensor_scalar(sb_st[:, 1:2], sb_st[:, 1:2], 1.0 / D, 1e-5, ALU.mult, ALU.add),
                  wait=(e_v,), inc=True)
        e6 = p.op("act", lambda E: E.activation(sb_st[:, 2:3], sb_st[:, 1:2], AF.Sqrt), wait=(e5,), inc=True)
        e7 = p.op("dve", lambda E: E.reciprocal(sb_st[:, 3:4], sb_st[:, 2:3]), wait=(e6,), inc=True)
        e_ln = p.op("dve", lambda E: E.tensor_scalar(sb_ln[:, :], sb_scr[:, :], sb_st[:, 3:4], None, ALU.mult),
                    wait=(e7, pv_st), inc=True)
        pv_st = p.dma("act", x_out[128 * i:128 * (i + 1), :], sb_ln[:, :], d_o, wait=(e_ln,))
    p.wait("act", p.ev("d_o"))
    return fin()


def build_axpy(fz=None, xio=None, pre="ax_"):
    p, nc, pin, dramt, sbuf, psum, fin = _env(fz, pre)
    x_in = xio["x"] if xio else pin("x", [TOK_PER_CORE, D], F32)
    y_in = xio["y"] if xio else pin("y", [TOK_PER_CORE, D], F32)
    x_out = xio["x_out"] if xio else nc.dram_tensor("x_out", [TOK_PER_CORE, D], F32, kind="ExternalOutput")
    sb_x = sbuf("sb_x", [128, D], F32)
    sb_y = sbuf("sb_y", [128, D], F32)
    sb_o = sbuf("sb_o", [128, D], F32)
    d_x = p.sem("d_x")
    d_o = p.sem("d_o")
    pv = None
    for i in range(NT):
        e_l = p.dma("sp", sb_x[:, :], x_in[128 * i:128 * (i + 1), :], d_x, wait=(pv,))
        e_l2 = p.dma("sp", sb_y[:, :], y_in[128 * i:128 * (i + 1), :], d_x)
        e_c = p.op("dve", lambda E: E.scalar_tensor_tensor(sb_o[:, :], sb_y[:, :], 0.5, sb_x[:, :], ALU.mult, ALU.add),
                   wait=(e_l2,), inc=True)
        pv = p.dma("act", x_out[128 * i:128 * (i + 1), :], sb_o[:, :], d_o, wait=(e_c,))
    p.wait("act", p.ev("d_o"))
    return fin()


def _rotary_tables(pos):
    half = np.arange(0, 64, 2, dtype=np.float64)
    inv_freq = 1.0 / 10000.0 ** (half / 64)
    t = pos.astype(np.float64)
    freqs = t[:, None] * inv_freq[None, :]
    freqs = np.concatenate([freqs, freqs], -1)
    base_scale = (half + 0.4 * 64) / (1.4 * 64)
    power = (t - 4096 // 2) / (256 // 2)
    scale = base_scale[None, :] ** power[:, None]
    scale = np.concatenate([scale, scale], -1)
    cos, sin = np.cos(freqs), np.sin(freqs)
    return (np.ascontiguousarray((cos * scale).astype(np.float32)),
            np.ascontiguousarray((sin * scale).astype(np.float32)),
            np.ascontiguousarray((cos / scale).astype(np.float32)),
            np.ascontiguousarray((sin / scale).astype(np.float32)))


def _run_simple(nc, x, extra=None):
    shards = x.reshape(N_CORES, TOK_PER_CORE, D)
    in_maps = [dict(extra or {}, x=np.ascontiguousarray(shards[c])) for c in range(N_CORES)]
    res = run_bass_kernel_spmd(nc, in_maps, core_ids=list(range(N_CORES)))
    return np.concatenate([r["x_out"] for r in res.results], axis=0)


def _halo_in_maps(x, halo, extra_fn):
    x4 = x.reshape(B, S, D)
    in_maps = []
    for c in range(N_CORES):
        b, h = c // 2, c % 2
        start = 2048 * h
        lo = start - halo
        pad = max(0, -lo)
        xs = x4[b, max(0, lo):start + 2048]
        if pad:
            xs = np.concatenate([np.zeros((pad, D), np.float32), xs], axis=0)
        pos = np.arange(lo, start + 2048).astype(np.float32)
        if pad:
            pos[:pad] = -1e6
        m = {"xh": np.ascontiguousarray(xs), }
        m.update(extra_fn(b, h, pos))
        in_maps.append(m)
    return in_maps


def _run_l2_inner_attn(x, Wqkv, Wo, qs, ks):
    nc = _get_nc("l2_inner_attn", build_l2_inner_attn)
    HALO = 384

    def extra(b, h, pos):
        start = 2048 * h
        cpos = np.clip(np.arange(start - HALO, start + 2048), 0, 4095)
        csk, ssk, _, _ = None, None, None, None
        cq, sq, ci, si = _rotary_tables(np.arange(start, start + 2048))
        ck_all = _rotary_tables(cpos)
        return {"hpos": pos, "w_qkv": Wqkv, "w_o": Wo,
                "q_scale": qs, "k_scale": ks,
                "csq": cq, "ssq": sq,
                "csk": ck_all[2], "ssk": ck_all[3],
                "ident": _IDENT}
    in_maps = _halo_in_maps(x, HALO, extra)
    res = run_bass_kernel_spmd(nc, in_maps, core_ids=list(range(N_CORES)))
    return np.concatenate([r["x_out"] for r in res.results], axis=0)


def _run_l2_inner_ffn(x, lg, lb, W1, b1, W2, b2):
    nc = _get_nc("l2_inner_ffn", build_l2_inner_ffn)

    def extra(b, h, pos):
        return {"w1": W1, "b1": b1, "w2": W2, "b2": b2, "ln_g": lg, "ln_b": lb, "ident": _IDENT}
    in_maps = _halo_in_maps(x, 128, extra)
    res = run_bass_kernel_spmd(nc, in_maps, core_ids=list(range(N_CORES)))
    return np.concatenate([r["x_out"] for r in res.results], axis=0)


def _run_l2_ffn(x, inputs):
    xi = _run_simple(_get_nc("ln_only", build_ln_only), x)
    for i in range(6):
        xi = _run_l2_inner_attn(xi, inputs["L2_ffn_Wqkv"][i], inputs["L2_ffn_Wo"][i],
                                inputs["L2_ffn_q_scale"][i], inputs["L2_ffn_k_scale"][i])
        xi = _run_l2_inner_ffn(xi, inputs["L2_ffn_ln_g"][i], inputs["L2_ffn_ln_b"][i],
                               inputs["L2_ffn_W1"][i], inputs["L2_ffn_b1"][i],
                               inputs["L2_ffn_W2"][i], inputs["L2_ffn_b2"][i])
    nc = _get_nc("axpy", build_axpy)
    shards_x = x.reshape(N_CORES, TOK_PER_CORE, D)
    shards_y = xi.reshape(N_CORES, TOK_PER_CORE, D)
    in_maps = [{"x": np.ascontiguousarray(shards_x[c]), "y": np.ascontiguousarray(shards_y[c])}
               for c in range(N_CORES)]
    res = run_bass_kernel_spmd(nc, in_maps, core_ids=list(range(N_CORES)))
    return np.concatenate([r["x_out"] for r in res.results], axis=0)


def _run_l3_attn(x, inputs):
    nc = _get_nc("l3_attn", build_l3_attn)
    x4 = x.reshape(B, S, D)
    in_maps = []
    base = {
        "w_in": inputs["L3_attn_in_proj_weight"],
        "b_in": inputs["L3_attn_in_proj_bias"],
        "gamma": np.ascontiguousarray(inputs["L3_attn_gamma"].reshape(-1)),
        "alpha": np.ascontiguousarray(inputs["L3_attn_linear_alpha"].reshape(-1)),
        "lgam": np.ascontiguousarray(inputs["L3_attn_linear_gamma"].reshape(-1)),
        "w_out": inputs["L3_attn_out_proj_weight"],
        "b_out": inputs["L3_attn_out_proj_bias"],
        "ident": _IDENT,
    }
    for c in range(N_CORES):
        b, h = c // 2, c % 2
        sl = slice(2048 * h, 2048 * (h + 1))
        order = [b] + [bb for bb in range(B) if bb != b]
        xb4 = np.concatenate([x4[bb, sl] for bb in order], axis=0)
        mv = np.zeros((16, 4), np.float32)
        for jj, bb in enumerate(order):
            if bb > b:
                mv[:, jj] = -1e9
        in_maps.append(dict(base, xb4=np.ascontiguousarray(xb4),
                            maskvec=np.ascontiguousarray(mv.reshape(-1))))
    res = run_bass_kernel_spmd(nc, in_maps, core_ids=list(range(N_CORES)))
    return _attn_gather(res, x)


def _run_l0_ffn(x, inputs):
    nc = _get_nc("l0_ffn", build_moe_ffn, True, True, False)
    return _run_sharded(nc, x, {
        "norm_scale": inputs["L0_ffn_norm_scale"],
        "w_router": inputs["L0_ffn_w_router"],
        "w_up": inputs["L0_ffn_w_up"],
        "w_down": inputs["L0_ffn_w_down"],
        "ident": _IDENT,
    })


def _run_l3_ffn(x, inputs):
    nc = _get_nc("l3_ffn", build_moe_ffn, False, False, True)
    return _run_sharded(nc, x, {
        "w_router": inputs["L3_ffn_Wg"],
        "w_a": inputs["L3_ffn_W1"],
        "w_b": inputs["L3_ffn_W2"],
        "w_down": inputs["L3_ffn_W3"],
        "ident": _IDENT,
    })


def _run_l1_ffn(x, inputs):
    nc = _get_nc("l1_ffn", build_l1_ffn)
    return _run_sharded(nc, x, {
        "base_w": inputs["L1_ffn_base_w"],
        "base_b": inputs["L1_ffn_base_b"],
        "lora_A": inputs["L1_ffn_lora_A"],
        "lora_B": inputs["L1_ffn_lora_B"],
        "w_router": inputs["L1_ffn_router_w"],
        "router_b": inputs["L1_ffn_router_b"],
        "ident": _IDENT,
    })





PAIR_GROUPS = [[0, 1], [2, 3], [4, 5], [6, 7]]
L3_GROUPS = [[0, 2, 4, 6], [1, 3, 5, 7]]


def _flat(t, off_el, n_el):
    return bass.AP(t, off_el, [[1, n_el]])


def _emit_selsum(p, sbuf, tag, dst, dst_row0, src, src_row0s, sv, rows, dep):
    nb = len(src_row0s)
    sb_s = sbuf(tag + "_s", [128, nb], F32)
    sb_b = sbuf(tag + "_b", [128, nb * D], F32)
    sb_o = sbuf(tag + "_o", [128, D], F32)
    d_g = p.sem("d_g")
    d_go = p.sem("d_go")
    e_s = p.dma("sp", sb_s[:, :], bass.AP(sv, 0, [[0, 128], [1, nb]]), d_g)
    prev_c = None
    prev_st = None
    for t in range(rows // 128):
        evl = []
        for j in range(nb):
            r0 = src_row0s[j] + 128 * t
            evl.append(p.dma("sp", sb_b[:, j * D:(j + 1) * D], src[r0:r0 + 128, :], d_g,
                             wait=(dep, prev_c)))
        e_c = p.op("dve", lambda E: E.tensor_scalar(sb_o[:, :], sb_b[:, 0:D], sb_s[:, 0:1], None, ALU.mult),
                   wait=tuple(evl) + (e_s, prev_st), inc=True)
        for j in range(1, nb):
            e_c = p.op("dve", lambda E, j=j: E.scalar_tensor_tensor(
                sb_o[:, :], sb_b[:, j * D:(j + 1) * D], sb_s[:, j:j + 1], sb_o[:, :], ALU.mult, ALU.add),
                wait=(e_c,), inc=True)
        r0 = dst_row0 + 128 * t
        prev_st = p.dma("sp", dst[r0:r0 + 128, :], sb_o[:, :], d_go, wait=(e_c,))
        prev_c = e_c


def emit_glue_pair_full(fz, xc, xb, g2, sv, pre):
    p, nc, pin, dramt, sbuf, psum, fin = _env(fz, pre)
    e_cc = p.op("pool", lambda E: E.collective_compute(
        "AllGather", ALU.bypass, replica_groups=PAIR_GROUPS,
        ins=[_flat(xc, 0, TOK_PER_CORE * D)], outs=[_flat(g2, 0, 2 * TOK_PER_CORE * D)]),
        inc="d_cc", inc_amt=1)
    p.dma("sp", xb[0:TOK_PER_CORE, :], xc[:, :], p.sem("d_cp"))
    _emit_selsum(p, sbuf, "ss", xb, TOK_PER_CORE, g2, [0, TOK_PER_CORE], sv, TOK_PER_CORE, e_cc)
    fin()


def emit_glue_halo(fz, xl, xh, gt, halo, sv, pre):
    p, nc, pin, dramt, sbuf, psum, fin = _env(fz, pre)
    e_cc = p.op("pool", lambda E: E.collective_compute(
        "AllGather", ALU.bypass, replica_groups=PAIR_GROUPS,
        ins=[_flat(xl, (TOK_PER_CORE - halo) * D, halo * D)], outs=[_flat(gt, 0, 2 * halo * D)]),
        inc="d_cc", inc_amt=1)
    p.dma("sp", xh[halo:halo + TOK_PER_CORE, :], xl[:, :], p.sem("d_cp"))
    _emit_selsum(p, sbuf, "ss", xh, 0, gt, [0, halo], sv, halo, e_cc)
    fin()


def emit_glue_l3(fz, xc, xb4, g4, svs, pre):
    p, nc, pin, dramt, sbuf, psum, fin = _env(fz, pre)
    e_cc = p.op("pool", lambda E: E.collective_compute(
        "AllGather", ALU.bypass, replica_groups=L3_GROUPS,
        ins=[_flat(xc, 0, TOK_PER_CORE * D)], outs=[_flat(g4, 0, 4 * TOK_PER_CORE * D)]),
        inc="d_cc", inc_amt=1)
    p.dma("sp", xb4[0:TOK_PER_CORE, :], xc[:, :], p.sem("d_cp"))
    src0 = [0, TOK_PER_CORE, 2 * TOK_PER_CORE, 3 * TOK_PER_CORE]
    for k in (1, 2, 3):
        _emit_selsum(p, sbuf, f"ss{k}", xb4, k * TOK_PER_CORE, g4, src0, svs[k - 1], TOK_PER_CORE, e_cc)
    fin()


def emit_final_copy(fz, xc, out, pre):
    p, nc, pin, dramt, sbuf, psum, fin = _env(fz, pre)
    p.dma("sp", out[:, :], xc[:, :], p.sem("d_fc"))
    fin()


def _redir(fz, dst_pre, src_pre, names):
    for n in names:
        fz.redirect[dst_pre + n] = src_pre + n


def build_fused(upto=12):
    nc = bass.Bass("TRN2", target_bir_lowering=False)
    p = Prog(nc)
    fz = FZ(p, nc)

    _, _, gpin, _, _, _, _ = fz.env("g_")
    ident = gpin("ident", [128, 128], F32)
    sv_pair = gpin("sv_pair", [2], F32)
    sv_halo = gpin("sv_halo", [2], F32)
    sv_l3 = [gpin(f"sv_l3_{k}", [4], F32) for k in (1, 2, 3)]

    x0 = nc.dram_tensor("x0", [TOK_PER_CORE, D], F32, kind="ExternalInput")
    fz.input_names.append("x0")
    out = nc.dram_tensor("out", [TOK_PER_CORE, D], F32, kind="ExternalOutput")
    xc = nc.dram_tensor("xc", [TOK_PER_CORE, D], F32)
    xl = nc.dram_tensor("xl", [TOK_PER_CORE, D], F32)
    xb = nc.dram_tensor("xbuf", [2 * TOK_PER_CORE, D], F32)
    xb4 = nc.dram_tensor("xb4buf", [4 * TOK_PER_CORE, D], F32)
    g2 = nc.dram_tensor("g2buf", [2 * TOK_PER_CORE, D], F32)
    g4 = nc.dram_tensor("g4buf", [4 * TOK_PER_CORE, D], F32)
    gta = nc.dram_tensor("gtabuf", [2 * 384, D], F32)
    gtf = nc.dram_tensor("gtfbuf", [2 * 128, D], F32)
    xha = nc.dram_tensor("xhabuf", [384 + TOK_PER_CORE, D], F32)
    xhf = nc.dram_tensor("xhfbuf", [128 + TOK_PER_CORE, D], F32)

    def all_ident(pre):
        fz.redirect[pre + "ident"] = "g_ident"

    MOE_L0 = ["w_router", "w_up", "w_down", "norm_scale"]
    MOE_L3 = ["w_router", "w_a", "w_b", "w_down"]
    L1FFN = ["base_w", "base_b", "lora_A", "lora_B", "w_router", "router_b"]
    IA_W = ["w_qkv", "w_o", "q_scale", "k_scale"]
    IA_SH = ["hpos", "csq", "ssq", "csk", "ssk"]
    IF_W = ["w1", "b1", "w2", "b2", "ln_g", "ln_b"]

    k = 0

    def emit_l2_ffn(r):
        pre = f"c{r}"
        all_ident(pre + "ln_")
        build_ln_only(fz=fz, xio={"x": xc, "x_out": xl}, pre=pre + "ln_")
        for i in range(6):
            pa = f"c{r}a{i}_"
            pf = f"c{r}f{i}_"
            all_ident(pa)
            all_ident(pf)
            if r == 1:
                _redir(fz, pa, f"c0a{i}_", IA_W)
                _redir(fz, pf, f"c0f{i}_", IF_W)
            if not (r == 0 and i == 0):
                _redir(fz, pa, "c0a0_", IA_SH)
            emit_glue_halo(fz, xl, xha, gta, 384, sv_halo, pa + "g_")
            build_l2_inner_attn(fz=fz, xio={"xh": xha, "x_out": xl}, pre=pa)
            emit_glue_halo(fz, xl, xhf, gtf, 128, sv_halo, pf + "g_")
            build_l2_inner_ffn(fz=fz, xio={"xh": xhf, "x_out": xl}, pre=pf)
        all_ident(pre + "ax_")
        build_axpy(fz=fz, xio={"x": xc, "y": xl, "x_out": xc}, pre=pre + "ax_")

    if upto > 0:
        all_ident("b0f1_")
        build_moe_ffn(True, True, False, fz=fz, xio={"x": x0, "x_out": xc}, pre="b0f1_")
    k = 1
    if upto > 1:
        emit_glue_pair_full(fz, xc, xb, g2, sv_pair, "ga0_")
        all_ident("b0a_")
        build_l0_attn(fz=fz, xio={"xb": xb, "x_out": xc}, pre="b0a_")
    if upto > 2:
        all_ident("b0f2_")
        _redir(fz, "b0f2_", "b0f1_", MOE_L0)
        build_moe_ffn(True, True, False, fz=fz, xio={"x": xc, "x_out": xc}, pre="b0f2_")
    if upto > 3:
        all_ident("b1f1_")
        build_l1_ffn(fz=fz, xio={"x": xc, "x_out": xc}, pre="b1f1_")
    if upto > 4:
        emit_glue_pair_full(fz, xc, xb, g2, sv_pair, "ga1_")
        all_ident("b1a_")
        fz.redirect["b1a_kpos"] = "b0a_kpos" if upto > 1 else None
        if fz.redirect["b1a_kpos"] is None:
            del fz.redirect["b1a_kpos"]
        build_l1_attn(fz=fz, xio={"xb": xb, "x_out": xc}, pre="b1a_")
    if upto > 5:
        all_ident("b1f2_")
        _redir(fz, "b1f2_", "b1f1_", L1FFN)
        build_l1_ffn(fz=fz, xio={"x": xc, "x_out": xc}, pre="b1f2_")
    if upto > 6:
        emit_l2_ffn(0)
    if upto > 7:
        emit_glue_pair_full(fz, xc, xb, g2, sv_pair, "ga2_")
        all_ident("b2a_")
        build_l2_attn(fz=fz, xio={"xb": xb, "x_out": xc}, pre="b2a_")
    if upto > 8:
        emit_l2_ffn(1)
    if upto > 9:
        all_ident("b3f1_")
        build_moe_ffn(False, False, True, fz=fz, xio={"x": xc, "x_out": xc}, pre="b3f1_")
    if upto > 10:
        emit_glue_l3(fz, xc, xb4, g4, sv_l3, "gl3_")
        all_ident("b3a_")
        build_l3_attn(fz=fz, xio={"xb4": xb4, "x_out": xc}, pre="b3a_")
    if upto > 11:
        all_ident("b3f2_")
        _redir(fz, "b3f2_", "b3f1_", MOE_L3)
        build_moe_ffn(False, False, True, fz=fz, xio={"x": xc, "x_out": xc}, pre="b3f2_")

    emit_final_copy(fz, xc, out, "fc_")
    p.finalize()
    return nc, sorted(set(fz.input_names))


def _fused_in_maps(inputs, declared):
    x = np.ascontiguousarray(np.asarray(inputs["L0_attn_x"], dtype=np.float32).reshape(B * S, D))
    shards = x.reshape(N_CORES, TOK_PER_CORE, D)
    H = 16
    gam = 1.0 - 2.0 ** (-5.0 - np.arange(H, dtype=np.float64))
    idx = np.arange(128)
    dmat = idx[:, None] - idx[None, :]
    tabU = np.ascontiguousarray(np.where(dmat[None] >= 0, gam[:, None, None] ** np.maximum(dmat, 0)[None], 0.0).astype(ml_dtypes.bfloat16))
    gpcol = np.ascontiguousarray((gam[None, :] ** idx[:, None]).astype(ml_dtypes.bfloat16))
    gdrow = np.ascontiguousarray((gam[:, None] ** (128 - idx[None, :])).astype(ml_dtypes.bfloat16))
    w_cat_l1 = np.ascontiguousarray(np.concatenate(
        [inputs["L1_attn_Wq"], inputs["L1_attn_Wk"], inputs["L1_attn_Wv"]], axis=1))
    w_cat_l2 = np.ascontiguousarray(np.concatenate([
        inputs["L2_attn_wq1"].transpose(1, 0, 2).reshape(D, D),
        inputs["L2_attn_wq2"].transpose(1, 0, 2).reshape(D, D),
        inputs["L2_attn_wk1"].transpose(1, 0, 2).reshape(D, D),
        inputs["L2_attn_wk2"].transpose(1, 0, 2).reshape(D, D),
        inputs["L2_attn_wv"].transpose(1, 0, 2).reshape(D, D)], axis=1))

    maps = []
    for c in range(N_CORES):
        b, h = c // 2, c % 2
        m = {"x0": np.ascontiguousarray(shards[c]), "g_ident": _IDENT}
        m["g_sv_pair"] = np.array([1.0, 0.0] if h == 1 else [0.0, 1.0], np.float32)
        m["g_sv_halo"] = np.array([1.0, 0.0] if h == 1 else [0.0, 0.0], np.float32)
        order = [b] + [bb for bb in range(B) if bb != b]
        for kk in (1, 2, 3):
            sv = np.zeros(4, np.float32)
            sv[order[kk]] = 1.0
            m[f"g_sv_l3_{kk}"] = sv
        m["b0f1_norm_scale"] = inputs["L0_ffn_norm_scale"]
        m["b0f1_w_router"] = inputs["L0_ffn_w_router"]
        m["b0f1_w_up"] = inputs["L0_ffn_w_up"]
        m["b0f1_w_down"] = inputs["L0_ffn_w_down"]
        kpos = np.concatenate([np.arange(2048 * h, 2048 * (h + 1)),
                               np.arange(2048 * (1 - h), 2048 * (2 - h))]).astype(np.float32)
        m["b0a_kpos"] = kpos
        m["b0a_w_qkv"] = inputs["L0_attn_W_qkv"]
        m["b0a_b_qkv"] = inputs["L0_attn_b_qkv"]
        m["b0a_w_o"] = inputs["L0_attn_W_o"]
        m["b0a_b_o"] = inputs["L0_attn_b_o"]
        m["b1f1_base_w"] = inputs["L1_ffn_base_w"]
        m["b1f1_base_b"] = inputs["L1_ffn_base_b"]
        m["b1f1_lora_A"] = inputs["L1_ffn_lora_A"]
        m["b1f1_lora_B"] = inputs["L1_ffn_lora_B"]
        m["b1f1_w_router"] = inputs["L1_ffn_router_w"]
        m["b1f1_router_b"] = inputs["L1_ffn_router_b"]
        m["b1a_w_cat"] = w_cat_l1
        m["b1a_pkv"] = inputs["L1_attn_past_kv"]
        m["b1a_gnw_e"] = np.ascontiguousarray(np.repeat(inputs["L1_attn_gn_w"], 64))
        m["b1a_gnb_e"] = np.ascontiguousarray(np.repeat(inputs["L1_attn_gn_b"], 64))
        m["b1a_tabU"] = tabU
        m["b1a_gpcol"] = gpcol
        m["b1a_gdrow"] = gdrow
        qpos = np.arange(2048 * h, 2048 * (h + 1), dtype=np.float64)
        if h == 0:
            cv = gam[:, None] ** (2048.0 - qpos[None, :])
        else:
            cv = np.zeros((H, 2048))
        m["b1a_cvec"] = np.ascontiguousarray(cv.astype(ml_dtypes.bfloat16))
        HALO = 384
        start = 2048 * h
        lo = start - HALO
        pos = np.arange(lo, start + 2048).astype(np.float32)
        if h == 0:
            pos[:HALO] = -1e6
        cpos = np.clip(np.arange(start - HALO, start + 2048), 0, 4095)
        cq, sq, _, _ = _rotary_tables(np.arange(start, start + 2048))
        _, _, ck, sk = None, None, None, None
        ck_all = _rotary_tables(cpos)
        m["c0a0_hpos"] = pos
        m["c0a0_csq"] = cq
        m["c0a0_ssq"] = sq
        m["c0a0_csk"] = ck_all[2]
        m["c0a0_ssk"] = ck_all[3]
        for i in range(6):
            m[f"c0a{i}_w_qkv"] = inputs["L2_ffn_Wqkv"][i]
            m[f"c0a{i}_w_o"] = inputs["L2_ffn_Wo"][i]
            m[f"c0a{i}_q_scale"] = inputs["L2_ffn_q_scale"][i]
            m[f"c0a{i}_k_scale"] = inputs["L2_ffn_k_scale"][i]
            m[f"c0f{i}_w1"] = inputs["L2_ffn_W1"][i]
            m[f"c0f{i}_b1"] = inputs["L2_ffn_b1"][i]
            m[f"c0f{i}_w2"] = inputs["L2_ffn_W2"][i]
            m[f"c0f{i}_b2"] = inputs["L2_ffn_b2"][i]
            m[f"c0f{i}_ln_g"] = inputs["L2_ffn_ln_g"][i]
            m[f"c0f{i}_ln_b"] = inputs["L2_ffn_ln_b"][i]
        m["b2a_w_cat"] = w_cat_l2
        m["b2a_ln_g"] = inputs["L2_attn_ln_gamma"]
        m["b2a_ln_b"] = inputs["L2_attn_ln_beta"]
        m["b2a_lamb"] = np.asarray(inputs["L2_attn_lamb"], dtype=np.float32).reshape(1)
        m["b3f1_w_router"] = inputs["L3_ffn_Wg"]
        m["b3f1_w_a"] = inputs["L3_ffn_W1"]
        m["b3f1_w_b"] = inputs["L3_ffn_W2"]
        m["b3f1_w_down"] = inputs["L3_ffn_W3"]
        m["b3a_w_in"] = inputs["L3_attn_in_proj_weight"]
        m["b3a_b_in"] = inputs["L3_attn_in_proj_bias"]
        m["b3a_gamma"] = np.ascontiguousarray(inputs["L3_attn_gamma"].reshape(-1))
        m["b3a_alpha"] = np.ascontiguousarray(inputs["L3_attn_linear_alpha"].reshape(-1))
        m["b3a_lgam"] = np.ascontiguousarray(inputs["L3_attn_linear_gamma"].reshape(-1))
        m["b3a_w_out"] = inputs["L3_attn_out_proj_weight"]
        m["b3a_b_out"] = inputs["L3_attn_out_proj_bias"]
        mv = np.zeros((16, 4), np.float32)
        for jj, bb in enumerate(order):
            if bb > b:
                mv[:, jj] = -1e9
        m["b3a_maskvec"] = np.ascontiguousarray(mv.reshape(-1))
        missing = [kk for kk in declared if kk not in m]
        assert not missing, f"missing fused inputs: {missing[:8]}"
        maps.append({kk: m[kk] for kk in declared})
    return maps


_FUSED_CACHE = {}


def kernel_fused(upto=12, **inputs):
    key = upto
    if key not in _FUSED_CACHE:
        _FUSED_CACHE[key] = build_fused(upto)
    nc, declared = _FUSED_CACHE[key]
    maps = _fused_in_maps(inputs, declared)
    res = run_bass_kernel_spmd(nc, maps, core_ids=list(range(N_CORES)))
    outs = np.stack([res.results[c]["out"] for c in range(N_CORES)], axis=0)
    x4 = outs.reshape(B, 2, 2048, D).reshape(B, S, D)
    return x4


def kernel(**inputs):
    return kernel_fused(upto=12, **inputs)


def kernel_unfused(**inputs):
    x = np.ascontiguousarray(np.asarray(inputs["L0_attn_x"], dtype=np.float32).reshape(B * S, D))
    x = _run_l0_ffn(x, inputs)
    x = _run_l0_attn(x, inputs)
    x = _run_l0_ffn(x, inputs)
    x = _run_l1_ffn(x, inputs)
    x = _run_l1_attn(x, inputs)
    x = _run_l1_ffn(x, inputs)
    x = _run_l2_ffn(x, inputs)
    x = _run_l2_attn(x, inputs)
    x = _run_l2_ffn(x, inputs)
    x = _run_l3_ffn(x, inputs)
    x = _run_l3_attn(x, inputs)
    x = _run_l3_ffn(x, inputs)
    return x.reshape(B, S, D)


# version 05
# speedup vs baseline: 1.1328x; 1.1328x over previous
import os
os.environ.setdefault("GAUGE_NTFF_IGNORE_DMA", "1")
os.environ.setdefault("JAX_COMPILATION_CACHE_DIR", "/tmp/jaxcache")
os.environ.setdefault("JAX_PERSISTENT_CACHE_MIN_ENTRY_SIZE_BYTES", "-1")
os.environ.setdefault("JAX_PERSISTENT_CACHE_MIN_COMPILE_TIME_SECS", "0")

import numpy as np
import ml_dtypes
import concourse.bass as bass
import concourse.mybir as mybir
from concourse.bass_utils import run_bass_kernel_spmd

try:
    import jax as _jax
    _jax.config.update("jax_compilation_cache_dir", "/tmp/jaxcache")
except Exception:
    pass

F32 = mybir.dt.float32
BF16 = mybir.dt.bfloat16
AF = mybir.ActivationFunctionType
ALU = mybir.AluOpType
AX = mybir.AxisListType

N_CORES = 8
B, S, D = 4, 4096, 1024
TOK_PER_CORE = B * S // N_CORES
NT = TOK_PER_CORE // 128

ENG2BLK = {"pe": "tensor", "dve": "vector", "act": "scalar", "pool": "gpsimd", "sp": "sync"}


class Prog:

    def __init__(self, nc):
        self.nc = nc
        self.ops = {e: [] for e in ENG2BLK}
        self.sems = {}
        self.counts = {}
        self.seen = {e: {} for e in ENG2BLK}
        for e in ENG2BLK:
            self.sem("c_" + e)

    def sem(self, name):
        if name not in self.sems:
            self.sems[name] = self.nc.alloc_semaphore(name)
            self.counts[name] = 0
        return name

    def _filter(self, eng, wait):
        seen = self.seen[eng]
        out = []
        for ev in wait:
            if ev is None:
                continue
            s, v = ev
            if seen.get(s, -1) >= v:
                continue
            seen[s] = v
            out.append(ev)
        return out

    def wait(self, eng, ev):
        for s, v in self._filter(eng, (ev,)):
            h = self.sems[s]
            self.ops[eng].append(lambda E, h=h, v=v: E.wait_ge(h, v))

    def op(self, eng, build, wait=(), inc=None, inc_amt=1):
        evs = self._filter(eng, wait)
        for s, v in evs[:-1]:
            h = self.sems[s]
            self.ops[eng].append(lambda E, h=h, v=v: E.wait_ge(h, v))
        att = None
        if evs:
            s, v = evs[-1]
            att = (self.sems[s], v)
        if inc is True:
            inc = "c_" + eng
        if inc:
            self.sem(inc)
            self.counts[inc] += inc_amt
            h, v = self.sems[inc], inc_amt
            self.ops[eng].append(lambda E, b=build, h=h, a=v, att=att:
                                 (b(E).then_inc(h, a)._wait_ge(*att) if att else b(E).then_inc(h, a)))
            return (inc, self.counts[inc])
        else:
            self.ops[eng].append(lambda E, b=build, att=att:
                                 (b(E)._wait_ge(*att) if att else b(E)))
            return None

    def dma(self, eng, out, in_, sem, wait=(), **kw):
        return self.op(eng, lambda E: E.dma_start(out=out, in_=in_, **kw),
                       wait=wait, inc=sem, inc_amt=16)

    def ev(self, sname):
        return (sname, self.counts[sname])

    def barrier(self):
        for eng in ENG2BLK:
            for sname in list(self.counts):
                if self.counts[sname] > 0:
                    self.wait(eng, (sname, self.counts[sname]))

    def finalize(self):
        with self.nc.Block() as blk:
            for eng, attr in ENG2BLK.items():
                ops = self.ops[eng]
                getattr(blk, attr)(lambda E, ops=ops: [f(E) for f in ops])
        return self.nc


class NS:
    def __init__(self, **kw):
        self.__dict__.update(kw)


class FZ:

    def __init__(self, p, nc):
        self.p = p
        self.nc = nc
        self.input_names = []
        self.tensors = {}
        self.redirect = {}

    def env(self, pre):
        fz = self
        sb0 = self.nc.sbuf_base
        ps0 = self.nc.psum_base

        def pin(name, shape, dt=F32):
            full = pre + name
            tgt = fz.redirect.get(full)
            if tgt is not None:
                return fz.tensors[tgt]
            fz.input_names.append(full)
            h = self.nc.dram_tensor(full, shape, dt, kind="ExternalInput")
            fz.tensors[full] = h
            return h

        def dramt(name, shape, dt):
            return self.nc.dram_tensor(pre + name, shape, dt)

        def sbuf(name, shape, dt):
            return self.nc.alloc_sbuf_tensor(pre + name, shape, dt)

        def psum(name, shape, dt=F32):
            return self.nc.alloc_psum_tensor(pre + name, shape, dt)

        def fin():
            self.p.barrier()
            self.nc.sbuf_base = sb0
            self.nc.psum_base = ps0
            return None
        return self.p, self.nc, pin, dramt, sbuf, psum, fin


def _env(fz, pre):
    if fz is not None:
        return fz.env(pre)
    nc = bass.Bass("TRN2", target_bir_lowering=False)
    p = Prog(nc)

    def pin(name, shape, dt=F32):
        return nc.dram_tensor(name, shape, dt, kind="ExternalInput")

    def dramt(name, shape, dt):
        return nc.dram_tensor(name, shape, dt)

    def sbuf(name, shape, dt):
        return nc.alloc_sbuf_tensor(name, shape, dt)

    def psum(name, shape, dt=F32):
        return nc.alloc_psum_tensor(name, shape, dt)

    def fin():
        return p.finalize()
    return p, nc, pin, dramt, sbuf, psum, fin



def emit_prelude(p, nc, x_in, *, alloc=None, with_rms=False, router=False, router_bias=False,
                 store_ln=False, w8_scale=1.0, w_scale_dram=None, w_router_dram=None,
                 router_b_dram=None, ident_dram=None, n_rows=TOK_PER_CORE, xnT_buf=None,
                 row_base=0, reuse=None, first_dep=None):
    NTL = n_rows // 128
    if reuse is not None:
        s = reuse
        pv = dict(s._pv_end)
        s.ev_w8 = []
        s.ev_lnT = []
        for i in range(NTL):
            tile_dep = p.dma("sp", s.sb_x[:, :], x_in[row_base + 128 * i:row_base + 128 * (i + 1), :], p.sem("d_in"),
                             wait=(pv.get("e_xc"), first_dep if i == 0 else None))
            e1 = p.op("dve", lambda E: E.reduce_sum(s.sb_stat[:, 0:1], s.sb_x[:, :], axis=AX.X),
                      wait=(tile_dep, pv.get("e_xc"), first_dep if i == 0 else None), inc=True)
            e2 = p.op("dve", lambda E: E.tensor_scalar(s.sb_stat[:, 0:1], s.sb_stat[:, 0:1], -1.0 / D, None, ALU.mult),
                      wait=(e1,), inc=True)
            e_xc = p.op("act", lambda E: E.activation(s.sb_scr[:, :], s.sb_x[:, :], AF.Identity, bias=s.sb_stat[:, 0:1]),
                        wait=(e2,), inc=True)
            e_v = p.op("act", lambda E: E.activation(s.sb_dump[:, :], s.sb_scr[:, :], AF.Square, accum_out=s.sb_stat[:, 1:2]),
                       wait=(e_xc,), inc=True)
            e5 = p.op("dve", lambda E: E.tensor_scalar(s.sb_stat[:, 1:2], s.sb_stat[:, 1:2], 1.0 / D, 1e-5, ALU.mult, ALU.add),
                      wait=(e_v,), inc=True)
            e6 = p.op("act", lambda E: E.activation(s.sb_stat[:, 2:3], s.sb_stat[:, 1:2], AF.Sqrt), wait=(e5,), inc=True)
            e7 = p.op("dve", lambda E: E.reciprocal(s.sb_stat[:, 3:4], s.sb_stat[:, 2:3]), wait=(e6,), inc=True)
            e_ln = p.op("dve", lambda E: E.tensor_scalar(s.sb_ln[:, :], s.sb_scr[:, :], s.sb_stat[:, 3:4], None, ALU.mult),
                        wait=(e7, pv.get("e_lastT")), inc=True)
            e_xnp = e_ln
            evA = None
            for dc in range(8):
                wl = [e_xnp] if dc == 0 else [e_xnp, evA]
                et = p.op("pe", lambda E, dc=dc: E.transpose(s.ps_t[:, 0:128], s.sb_ln[:, dc * 128:(dc + 1) * 128], s.sb_id32[:, :]),
                          wait=tuple(wl), inc=True)
                evA = p.op("dve", lambda E, dc=dc, i=i: E.tensor_scalar(
                    s.sb_xnT[:, dc * n_rows + 128 * i: dc * n_rows + 128 * (i + 1)],
                    s.ps_t[:, 0:128], s.sb_nsc[:, dc:dc + 1], None, ALU.mult), wait=(et,), inc=True)
            e_lastT = et
            s.ev_lnT.append(evA)
            pv = dict(e_xc=e_xc, lnstore=None, e_lnsq=None, e_lastT=e_lastT, e_w8=None)
        s._pv_end = pv
        s.ev_dve_done = p.ev("c_dve")
        return s
    s = NS()
    if alloc is not None:
        _sba, _psa, _dra = alloc
    else:
        _sba, _psa, _dra = nc.alloc_sbuf_tensor, nc.alloc_psum_tensor, nc.dram_tensor
    s.sb_x = _sba("pl_x", [128, D], F32)
    s.sb_scr = _sba("pl_scr", [128, D], F32)
    s.sb_dump = _sba("pl_dump", [128, D], F32)
    s.sb_ln = _sba("pl_ln", [128, D], F32)
    s.sb_xnp = _sba("pl_xnp", [128, D], F32) if with_rms else None
    s.sb_stat = _sba("pl_stat", [128, 16], F32)
    s.sb_xnT = xnT_buf if xnT_buf is not None else _sba("pl_xnT", [128, 8 * n_rows], BF16)
    s.sb_xnpT = _sba("pl_xnpT", [128, 128], F32)
    s.sb_id32 = _sba("pl_id32", [128, 128], F32)
    s.sb_nsc = _sba("pl_nsc", [128, 8], F32)
    s.ps_t = _psa("pl_ps_t", [128, 512], F32)
    if router:
        s.sb_wrT = _sba("pl_wrT", [128, 8 * 8], F32)
        s.sb_wr = _sba("pl_wr", [8, D], F32)
        s.sb_g = _sba("pl_g", [128, 8], F32)
        s.sb_gt = _sba("pl_gt", [128, 8], F32)
        s.sb_w8 = _sba("pl_w8", [128, NT * 8], F32)
        s.ps_r = _psa("pl_ps_r", [128, 512], F32)
        if router_bias:
            s.sb_rbr = _sba("pl_rbr", [1, 8], F32)
            s.sb_rb = _sba("pl_rb", [128, 8], F32)
            s.sb_one1 = _sba("pl_one1", [1, 128], F32)
    if store_ln:
        s.ln_dram = _dra("pl_ln_dram", [n_rows, D], F32)

    d_in = p.sem("d_in")
    d_o = p.sem("d_o")

    p.dma("sp", s.sb_id32[:, :], ident_dram[:, :], d_in)
    if router:
        p.dma("sp", s.sb_wr[:, :], w_router_dram[:, :], d_in)
    if with_rms:
        p.dma("sp", s.sb_nsc[:, :], w_scale_dram.ap().rearrange("(i p) -> p i", p=128), d_in,
              allow_slow_non_contiguous=True)
        e_nsc = None
    else:
        e_nsc = p.op("dve", lambda E: E.memset(s.sb_nsc[:, :], 1.0), inc=True)
    if router and router_bias:
        p.dma("sp", s.sb_rbr[:, :], router_b_dram.ap().rearrange("(o e) -> o e", o=1), d_in)
    e_in0 = p.ev("d_in")
    s.e_in0 = e_in0
    p.wait("sp", e_in0)

    ev_rb = None
    if router:
        prev_ev = None
        for dc in range(8):
            et = p.op("pe", lambda E, dc=dc: E.transpose(s.ps_t[:, 0:8], s.sb_wr[0:8, dc * 128:(dc + 1) * 128], s.sb_id32[0:8, 0:8]),
                      wait=(e_in0,) if dc == 0 else (prev_ev,), inc=True)
            prev_ev = p.op("dve", lambda E, dc=dc: E.tensor_scalar(s.sb_wrT[:, dc * 8:(dc + 1) * 8], s.ps_t[:, 0:8], s.sb_nsc[:, dc:dc + 1], None, ALU.mult),
                           wait=(et, e_nsc) if dc == 0 else (et,), inc=True)
        s.ev_wrT = prev_ev
        if router_bias:
            e_o1 = p.op("dve", lambda E: E.memset(s.sb_one1[:, :], 1.0), inc=True)
            e_bm = p.op("pe", lambda E: E.matmul(s.ps_r[:, 0:8], s.sb_one1[:, :], s.sb_rbr[:, :], start=True, stop=True),
                        wait=(e_o1, e_in0, prev_ev), inc=True)
            ev_rb = p.op("dve", lambda E: E.tensor_copy(s.sb_rb[:, :], s.ps_r[:, 0:8]), wait=(e_bm,), inc=True)

    s.ev_w8 = []
    s.ev_lnT = []
    pv = {}
    for i in range(NTL):
        tile_dep = p.dma("sp", s.sb_x[:, :], x_in[128 * i:128 * (i + 1), :], d_in,
                         wait=(pv.get("e_xc"),))
        e1 = p.op("dve", lambda E: E.reduce_sum(s.sb_stat[:, 0:1], s.sb_x[:, :], axis=AX.X),
                  wait=(tile_dep, pv.get("e_xc")), inc=True)
        e2 = p.op("dve", lambda E: E.tensor_scalar(s.sb_stat[:, 0:1], s.sb_stat[:, 0:1], -1.0 / D, None, ALU.mult),
                  wait=(e1,), inc=True)
        e_xc = p.op("act", lambda E: E.activation(s.sb_scr[:, :], s.sb_x[:, :], AF.Identity, bias=s.sb_stat[:, 0:1]),
                    wait=(e2,), inc=True)
        e_v = p.op("act", lambda E: E.activation(s.sb_dump[:, :], s.sb_scr[:, :], AF.Square, accum_out=s.sb_stat[:, 1:2]),
                   wait=(e_xc,), inc=True)
        e5 = p.op("dve", lambda E: E.tensor_scalar(s.sb_stat[:, 1:2], s.sb_stat[:, 1:2], 1.0 / D, 1e-5, ALU.mult, ALU.add),
                  wait=(e_v,), inc=True)
        e6 = p.op("act", lambda E: E.activation(s.sb_stat[:, 2:3], s.sb_stat[:, 1:2], AF.Sqrt),
                  wait=(e5,), inc=True)
        e7 = p.op("dve", lambda E: E.reciprocal(s.sb_stat[:, 3:4], s.sb_stat[:, 2:3]), wait=(e6,), inc=True)
        e_ln = p.op("dve", lambda E: E.tensor_scalar(s.sb_ln[:, :], s.sb_scr[:, :], s.sb_stat[:, 3:4], None, ALU.mult),
                    wait=(e7, pv.get("lnstore"), pv.get("e_lnsq"), pv.get("e_lastT")), inc=True)
        lnstore = p.dma("act", s.ln_dram[128 * i:128 * (i + 1), :], s.sb_ln[:, :], d_o, wait=(e_ln,)) if store_ln else None
        if with_rms:
            e_lnsq = p.op("act", lambda E: E.activation(s.sb_dump[:, :], s.sb_ln[:, :], AF.Square, accum_out=s.sb_stat[:, 5:6]),
                          wait=(e_ln,), inc=True)
            e_ms = p.op("dve", lambda E: E.tensor_scalar(s.sb_stat[:, 5:6], s.sb_stat[:, 5:6], 1.0 / D, 1e-6, ALU.mult, ALU.add),
                        wait=(e_lnsq,), inc=True)
            e_sd = p.op("act", lambda E: E.activation(s.sb_stat[:, 6:7], s.sb_stat[:, 5:6], AF.Sqrt), wait=(e_ms,), inc=True)
            e_rr = p.op("dve", lambda E: E.reciprocal(s.sb_stat[:, 7:8], s.sb_stat[:, 6:7]), wait=(e_sd,), inc=True)
            e_xnp = p.op("dve", lambda E: E.tensor_scalar(s.sb_xnp[:, :], s.sb_ln[:, :], s.sb_stat[:, 7:8], None, ALU.mult),
                         wait=(e_rr, pv.get("e_lastT")), inc=True)
            src_xn = s.sb_xnp
        else:
            e_lnsq = None
            e_xnp = e_ln
            src_xn = s.sb_ln
        em = None
        evA = evB = None
        for dc in range(8):
            wl = [e_xnp] if dc == 0 else [e_xnp, evA, evB]
            et = p.op("pe", lambda E, dc=dc: E.transpose(s.ps_t[:, 0:128], src_xn[:, dc * 128:(dc + 1) * 128], s.sb_id32[:, :]),
                      wait=tuple(wl), inc=True)
            evA = p.op("dve", lambda E, dc=dc, i=i: E.tensor_scalar(
                s.sb_xnT[:, dc * n_rows + 128 * i: dc * n_rows + 128 * (i + 1)],
                s.ps_t[:, 0:128], s.sb_nsc[:, dc:dc + 1], None, ALU.mult), wait=(et,), inc=True)
            if router:
                evB = p.op("dve", lambda E: E.tensor_copy(s.sb_xnpT[:, :], s.ps_t[:, 0:128]), wait=(et,), inc=True)
                em = p.op("pe", lambda E, dc=dc: E.matmul(s.ps_r[:, 0:8], s.sb_xnpT[:, :], s.sb_wrT[:, dc * 8:(dc + 1) * 8],
                                                          start=(dc == 0), stop=(dc == 7)),
                          wait=(evB, s.ev_wrT, ev_rb) if dc == 0 else (evB,), inc=(dc == 7))
            else:
                evB = evA
        e_lastT = et
        s.ev_lnT.append(evA)
        if router:
            guard = (pv.get("e_w8"),)
            e_m1n = None
            if router_bias:
                e_gs = p.op("act", lambda E: E.activation(s.sb_g[:, :], s.ps_r[:, 0:8], AF.Copy), wait=(em, ev_rb) + guard, inc=True)
                e_gb = p.op("dve", lambda E: E.tensor_add(s.sb_g[:, :], s.sb_g[:, :], s.sb_rb[:, :]), wait=(e_gs,) + guard, inc=True)
                e_m1n = p.op("dve", lambda E: E.reduce_max(s.sb_stat[:, 8:9], s.sb_g[:, :], axis=AX.X, negate=True),
                             wait=(e_gb,), inc=True)
            else:
                e_m1n = p.op("dve", lambda E: E.reduce_max(s.sb_stat[:, 8:9], s.ps_r[:, 0:8], axis=AX.X, negate=True),
                             wait=(em,) + guard, inc=True)
                e_gs = p.op("act", lambda E: E.activation(s.sb_g[:, :], s.ps_r[:, 0:8], AF.Copy), wait=(em,) + guard, inc=True)
            e_m1 = p.op("dve", lambda E: E.tensor_scalar(s.sb_stat[:, 9:10], s.sb_stat[:, 8:9], -1.0, None, ALU.mult),
                        wait=(e_m1n,), inc=True)
            e_eq = p.op("dve", lambda E: E.tensor_scalar(s.sb_gt[:, :], s.sb_g[:, :], s.sb_stat[:, 9:10], None, ALU.is_ge),
                        wait=(e_m1, e_gs), inc=True)
            e_s2 = p.op("dve", lambda E: E.scalar_tensor_tensor(s.sb_gt[:, :], s.sb_gt[:, :], -1e9, s.sb_g[:, :], ALU.mult, ALU.add),
                        wait=(e_eq,), inc=True)
            e_m2 = p.op("dve", lambda E: E.reduce_max(s.sb_stat[:, 10:11], s.sb_gt[:, :], axis=AX.X), wait=(e_s2,), inc=True)
            e_msk = p.op("dve", lambda E: E.tensor_scalar(s.sb_gt[:, :], s.sb_g[:, :], s.sb_stat[:, 10:11], None, ALU.is_ge),
                         wait=(e_m2,), inc=True)
            e_e1 = p.op("act", lambda E: E.activation(s.sb_g[:, :], s.sb_g[:, :], AF.Exp, bias=s.sb_stat[:, 8:9]), wait=(e_msk,), inc=True)
            e_p = p.op("dve", lambda E: E.tensor_mul(s.sb_g[:, :], s.sb_g[:, :], s.sb_gt[:, :]), wait=(e_e1,), inc=True)
            e_z = p.op("dve", lambda E: E.reduce_sum(s.sb_stat[:, 11:12], s.sb_g[:, :], axis=AX.X), wait=(e_p,), inc=True)
            e_rz = p.op("dve", lambda E: E.reciprocal(s.sb_stat[:, 12:13], s.sb_stat[:, 11:12]), wait=(e_z,), inc=True)
            e_w8 = p.op("dve", lambda E, i=i: E.tensor_scalar(s.sb_w8[:, 8 * i:8 * (i + 1)], s.sb_g[:, :], s.sb_stat[:, 12:13], w8_scale,
                                                              ALU.mult, ALU.mult),
                        wait=(e_rz,), inc=True)
            s.ev_w8.append(e_w8)
        else:
            e_w8 = None
        pv = dict(e_xc=e_xc, lnstore=lnstore, e_lnsq=e_lnsq, e_lastT=e_lastT, e_w8=e_w8)

    s._pv_end = pv
    s.ev_dve_done = p.ev("c_dve")
    s.ln_all_stored = p.ev("d_o")
    return s



def build_moe_ffn(with_rms, with_skip, split_ab, fz=None, xio=None, pre="mo_", cast_share=None):
    p, nc, pin, dramt, sbuf, psum, fin = _env(fz, pre)

    NTB = 4
    TBT = NT // NTB
    TB = 128 * TBT
    E_, HID = 8, 2048

    x_in = xio["x"] if xio else pin("x", [TOK_PER_CORE, D], F32)
    w_router = pin("w_router", [E_, D], F32)
    if split_ab:
        w_a = pin("w_a", [E_, HID, D], F32)
        w_b = pin("w_b", [E_, HID, D], F32)
        a_bf = dramt("a_bf", [E_, HID, D], BF16)
        b_bf = dramt("b_bf", [E_, HID, D], BF16)
    else:
        w_up = pin("w_up", [E_, 2 * HID, D], F32)
        up_bf = dramt("up_bf", [E_, 2 * HID, D], BF16)
    w_down = pin("w_down", [E_, D, HID], F32)
    dn_bf = dramt("dn_bf", [E_, D, HID], BF16)
    w_scale = pin("norm_scale", [D], F32) if with_rms else None
    ident = pin("ident", [128, 128], F32)
    x_out = xio["x_out"] if xio else nc.dram_tensor("x_out", [TOK_PER_CORE, D], F32, kind="ExternalOutput")

    sb_upT = sbuf("sb_upT", [128, 8 * 2 * HID], BF16)
    sb_dnT = sbuf("sb_dnT", [128, 16 * D], BF16)
    sb_hT = sbuf("sb_hT", [128, 16 * TB], BF16)
    sb_sg = sbuf("sb_sg", [128, TB], BF16)
    sb_acc = sbuf("sb_acc", [128, TBT * D], F32)
    sb_tmp = sbuf("sb_tmp", [128, 512], F32)
    sb_out = sbuf("sb_out", [128, D], F32)
    sb_xc = sbuf("sb_xc", [128, D], F32)
    sb_lnld = sbuf("sb_lnld", [128, D], F32)

    ps_u = psum("ps_u", [128, 512], F32)
    ps_g = psum("ps_g", [128, 512], F32)
    ps_d = psum("ps_d", [128, 512], F32)

    d_in = p.sem("d_in")
    d_w = p.sem("d_w")
    d_c = p.sem("d_c")
    d_o = p.sem("d_o")

    if cast_share is not None:
        if split_ab:
            a_bf, b_bf, dn_bf = cast_share
        else:
            up_bf, dn_bf = cast_share
        cast_done = None
    else:
        for e in range(E_):
            if split_ab:
                p.dma("pool", a_bf[e], w_a[e], d_c)
                p.dma("pool", b_bf[e], w_b[e], d_c)
            else:
                p.dma("pool", up_bf[e], w_up[e], d_c)
            p.dma("pool", dn_bf[e], w_down[e], d_c)
        cast_done = p.ev("d_c")
    if fz is not None:
        fz.tensors[pre + "castbufs"] = (a_bf, b_bf, dn_bf) if split_ab else (up_bf, dn_bf)

    s = emit_prelude(p, nc, x_in, alloc=(sbuf, psum, dramt), with_rms=with_rms, router=True, store_ln=with_skip,
                     w_scale_dram=w_scale, w_router_dram=w_router, ident_dram=ident)

    ev_block_out = None
    ev_last_up_use = None
    ev_last_dn_use = None
    e_h = None
    ev_d = None
    for tb in range(NTB):
        e_z0 = p.op("dve", lambda E: E.memset(sb_acc[:, :], 0.0),
                    wait=(s.ev_dve_done,) if tb == 0 else (ev_block_out,), inc=True)
        last_acc = e_z0
        last_evac_read = None
        for e in range(E_):
            p.wait("sp", ev_last_up_use)
            p.wait("sp", ev_last_dn_use)
            if tb == 0 and e == 0:
                p.wait("sp", cast_done)
            for dc in range(8):
                if split_ab:
                    p.dma("sp", sb_upT[:, dc * 2 * HID: dc * 2 * HID + HID],
                          b_bf[e][:, dc * 128:(dc + 1) * 128], d_w, transpose=True)
                    p.dma("sp", sb_upT[:, dc * 2 * HID + HID: (dc + 1) * 2 * HID],
                          a_bf[e][:, dc * 128:(dc + 1) * 128], d_w, transpose=True)
                else:
                    p.dma("sp", sb_upT[:, dc * 2 * HID:(dc + 1) * 2 * HID],
                          up_bf[e][:, dc * 128:(dc + 1) * 128], d_w, transpose=True)
            for hc in range(16):
                p.dma("sp", sb_dnT[:, hc * D:(hc + 1) * D], dn_bf[e][:, hc * 128:(hc + 1) * 128], d_w, transpose=True)
            e_wload = p.ev("d_w")

            for j in range(16):
                ev_u = ev_gt = None
                for dc in range(8):
                    xr = s.sb_xnT[:, dc * TOK_PER_CORE + TB * tb: dc * TOK_PER_CORE + TB * (tb + 1)]
                    wl = []
                    if dc == 0:
                        wl.append(e_wload)
                        if tb == 0 and e == 0 and j == 0:
                            wl.append(s.ev_dve_done)
                        if j > 0:
                            wl.append(e_h)
                    ev_u = p.op("pe", lambda E, dc=dc, j=j, xr=xr: E.matmul(
                        ps_u[:, 0:TB], sb_upT[:, dc * 2 * HID + 128 * j: dc * 2 * HID + 128 * (j + 1)], xr,
                        start=(dc == 0), stop=(dc == 7)),
                        wait=tuple(wl), inc=(dc == 7))
                for dc in range(8):
                    xr = s.sb_xnT[:, dc * TOK_PER_CORE + TB * tb: dc * TOK_PER_CORE + TB * (tb + 1)]
                    ev_gt = p.op("pe", lambda E, dc=dc, j=j, xr=xr: E.matmul(
                        ps_g[:, 0:TB], sb_upT[:, dc * 2 * HID + HID + 128 * j: dc * 2 * HID + HID + 128 * (j + 1)], xr,
                        start=(dc == 0), stop=(dc == 7)),
                        inc=(dc == 7))
                e_sg = p.op("act", lambda E: E.activation(sb_sg[:, :], ps_g[:, 0:TB], AF.Silu),
                            wait=(ev_gt, e_h), inc=True)
                e_h = p.op("dve", lambda E, j=j: E.tensor_tensor(sb_hT[:, TB * j:TB * (j + 1)], ps_u[:, 0:TB], sb_sg[:, :], ALU.mult),
                           wait=(ev_u, e_sg), inc=True)
            e_hT_done = e_h
            ev_last_up_use = e_h

            for tc in range(TBT):
                for do in range(2):
                    for hc in range(16):
                        wl = [e_hT_done, last_evac_read] if hc == 0 else []
                        ev_d = p.op("pe", lambda E, hc=hc, tc=tc, do=do: E.matmul(
                            ps_d[:, :], sb_hT[:, TB * hc + 128 * tc: TB * hc + 128 * (tc + 1)],
                            sb_dnT[:, hc * D + 512 * do: hc * D + 512 * (do + 1)],
                            start=(hc == 0), stop=(hc == 15)),
                            wait=tuple(wl), inc=(hc == 15))
                    wcol = s.sb_w8[:, (TBT * tb + tc) * 8 + e: (TBT * tb + tc) * 8 + e + 1]
                    e_t = p.op("act", lambda E, wcol=wcol: E.activation(sb_tmp[:, :], ps_d[:, :], AF.Copy, scale=wcol),
                               wait=(ev_d, s.ev_w8[TBT * tb + tc], last_evac_read), inc=True)
                    e_a = p.op("dve", lambda E, tc=tc, do=do: E.tensor_add(
                        sb_acc[:, tc * D + 512 * do: tc * D + 512 * (do + 1)],
                        sb_acc[:, tc * D + 512 * do: tc * D + 512 * (do + 1)], sb_tmp[:, :]),
                        wait=(e_t, last_acc), inc=True)
                    last_acc = e_a
                    last_evac_read = e_a
            ev_last_dn_use = ev_d

        prev_store = None
        for tc in range(TBT):
            ti = TBT * tb + tc
            e_lx = p.dma("sp", sb_xc[:, :], x_in[128 * ti:128 * (ti + 1), :], p.sem("d_x2"),
                         wait=(last_acc, prev_store))
            if with_skip:
                e_ll = p.dma("sp", sb_lnld[:, :], s.ln_dram[128 * ti:128 * (ti + 1), :], p.sem("d_x2"),
                             wait=(s.ln_all_stored,))
                joint = e_ll
            else:
                e_ll = None
                joint = e_lx
            e_c1 = p.op("dve", lambda E, tc=tc: E.scalar_tensor_tensor(
                sb_out[:, :], sb_acc[:, tc * D:(tc + 1) * D], 0.5, sb_xc[:, :], ALU.mult, ALU.add),
                wait=(joint, last_acc), inc=True)
            if with_skip:
                e_c2 = p.op("dve", lambda E: E.scalar_tensor_tensor(sb_out[:, :], sb_lnld[:, :], 0.5, sb_out[:, :], ALU.mult, ALU.add),
                            wait=(e_ll, e_c1), inc=True)
            else:
                e_c2 = e_c1
            prev_store = p.dma("act", x_out[128 * ti:128 * (ti + 1), :], sb_out[:, :], d_o, wait=(e_c2,))
        ev_block_out = prev_store

    p.wait("act", p.ev("d_o"))
    return fin()



def build_l1_ffn(fz=None, xio=None, pre="f1_"):
    p, nc, pin, dramt, sbuf, psum, fin = _env(fz, pre)
    E_, R = 8, 16

    x_in = xio["x"] if xio else pin("x", [TOK_PER_CORE, D], F32)
    base_w = pin("base_w", [D, D], F32)
    base_b = pin("base_b", [D], F32)
    lora_A = pin("lora_A", [E_, D, R], F32)
    lora_B = pin("lora_B", [E_, R, D], F32)
    w_router = pin("w_router", [E_, D], F32)
    router_b = pin("router_b", [E_], F32)
    ident = pin("ident", [128, 128], F32)
    x_out = xio["x_out"] if xio else nc.dram_tensor("x_out", [TOK_PER_CORE, D], F32, kind="ExternalOutput")

    bw_bf = dramt("bw_bf", [D, D], BF16)
    la_bf = dramt("la_bf", [E_, D, R], BF16)
    lb_bf = dramt("lb_bf", [E_, R, D], BF16)

    sb_bwT = sbuf("sb_bwT", [128, 8 * D], BF16)
    sb_A = sbuf("sb_A", [128, 8 * E_ * R], BF16)
    sb_Bm = sbuf("sb_Bm", [16, E_ * D], BF16)
    sb_ax = sbuf("sb_ax", [128, E_ * R], BF16)
    sb_axT = sbuf("sb_axT", [16, E_ * 128], BF16)
    sb_bb = sbuf("sb_bb", [128, D], F32)
    sb_bbr = sbuf("sb_bbr", [1, D], F32)
    sb_one1 = sbuf("sb_one1b", [1, 128], F32)
    sb_xc = sbuf("sb_xc", [128, D], F32)
    sb_t1 = sbuf("sb_t1", [128, D], F32)
    sb_out = sbuf("sb_out", [128, D], F32)
    sb_id16 = sbuf("sb_id16", [128, 128], BF16)

    ps_o = psum("ps_o", [128, 512], F32)
    ps_o2 = psum("ps_o2", [128, 512], F32)
    ps_ax = psum("ps_ax", [128, 512], F32)
    ps_tr = psum("ps_tr", [128, 512], BF16)

    d_in = p.sem("d_in")
    d_w = p.sem("d_w")
    d_c = p.sem("d_c")
    d_o = p.sem("d_o")

    p.dma("pool", bw_bf[:, :], base_w[:, :], d_c)
    p.dma("pool", la_bf.ap(), lora_A.ap(), d_c)
    p.dma("pool", lb_bf.ap(), lora_B.ap(), d_c)
    cast_done = p.ev("d_c")

    s = emit_prelude(p, nc, x_in, alloc=(sbuf, psum, dramt), with_rms=False, router=True, router_bias=True,
                     store_ln=False, w8_scale=2.0,
                     w_router_dram=w_router, router_b_dram=router_b, ident_dram=ident)

    p.wait("sp", cast_done)
    for dc in range(8):
        p.dma("sp", sb_bwT[:, dc * D:(dc + 1) * D], bw_bf[:, dc * 128:(dc + 1) * 128], d_w, transpose=True)
    for dc in range(8):
        for e in range(E_):
            p.dma("sp", sb_A[:, (dc * E_ + e) * R:(dc * E_ + e + 1) * R], la_bf[e][dc * 128:(dc + 1) * 128, :], d_w)
    for e in range(E_):
        p.dma("sp", sb_Bm[:, e * D:(e + 1) * D], lb_bf[e][:, :], d_w)
    e_wload = p.ev("d_w")

    e_id = p.op("dve", lambda E: E.tensor_copy(sb_id16[:, :], s.sb_id32[:, :]), wait=(s.e_in0,), inc=True)
    p.dma("sp", sb_bbr[:, :], base_b.ap().rearrange("(o d) -> o d", o=1), p.sem("d_bb"))
    e_bbl = p.ev("d_bb")
    e_o1 = p.op("dve", lambda E: E.memset(sb_one1[:, :], 1.0), inc=True)
    e_bm = p.op("pe", lambda E: E.matmul(ps_o[:, :], sb_one1[:, :], sb_bbr[0:1, 0:512], start=True, stop=True),
                wait=(e_o1, e_bbl, s.ev_dve_done), inc=True)
    e_bm2 = p.op("pe", lambda E: E.matmul(ps_o2[:, :], sb_one1[:, :], sb_bbr[0:1, 512:1024], start=True, stop=True),
                 wait=(), inc=True)
    e_bb1 = p.op("dve", lambda E: E.tensor_copy(sb_bb[:, 0:512], ps_o[:, :]), wait=(e_bm,), inc=True)
    e_bb2 = p.op("dve", lambda E: E.tensor_copy(sb_bb[:, 512:1024], ps_o2[:, :]), wait=(e_bm2,), inc=True)

    pv_store = None
    pv_evac = None
    pv_axT_read = e_bb2
    for i in range(NT):
        lnT_ = [s.sb_xnT[:, dc * TOK_PER_CORE + 128 * i: dc * TOK_PER_CORE + 128 * (i + 1)] for dc in range(8)]
        ev_mm = None
        for e in range(E_):
            for dc in range(8):
                wl = []
                if e == 0 and dc == 0:
                    wl = [e_wload, s.ev_dve_done, pv_evac] if i == 0 else [pv_evac]
                ev_mm = p.op("pe", lambda E, dc=dc, e=e, xr=lnT_[dc]: E.matmul(
                    ps_ax[:, e * R:(e + 1) * R], xr, sb_A[:, (dc * E_ + e) * R:(dc * E_ + e + 1) * R],
                    start=(dc == 0), stop=(dc == 7)),
                    wait=tuple(wl), inc=(dc == 7 and e == E_ - 1))
        ev_axs = []
        for e in range(E_):
            wcol = s.sb_w8[:, 8 * i + e: 8 * i + e + 1]
            e_axs = p.op("act", lambda E, e=e, wcol=wcol: E.activation(
                sb_ax[:, e * R:(e + 1) * R], ps_ax[:, e * R:(e + 1) * R], AF.Copy, scale=wcol),
                wait=(ev_mm, s.ev_w8[i], pv_axT_read) if e == 0 else (), inc=True)
            ev_axs.append(e_axs)
        pv_evac = ev_axs[-1]
        ev_trs = []
        prev_tr_evac = None
        for e in range(E_):
            e_tr = p.op("pe", lambda E, e=e: E.transpose(ps_tr[0:16, 0:128], sb_ax[:, e * R:(e + 1) * R], sb_id16[:, :]),
                        wait=(ev_axs[e], e_id) if e == 0 else (ev_axs[e], prev_tr_evac), inc=True)
            prev_tr_evac = p.op("dve", lambda E, e=e: E.tensor_copy(sb_axT[:, e * 128:(e + 1) * 128], ps_tr[0:16, 0:128]),
                                wait=(e_tr,), inc=True)
            ev_trs.append(prev_tr_evac)
        ev_half = []
        for half in range(2):
            pso = ps_o if half == 0 else ps_o2
            ev_o = None
            for dc in range(8):
                wl = []
                if dc == 0 and i == 0 and half == 0:
                    wl = [e_bb1, e_bb2]
                ev_o = p.op("pe", lambda E, dc=dc, pso=pso, half=half, xr=lnT_[dc]: E.matmul(
                    pso[:, :], xr, sb_bwT[:, dc * D + 512 * half: dc * D + 512 * (half + 1)],
                    start=(dc == 0), stop=False),
                    wait=tuple(wl), inc=False)
            for e in range(E_):
                ev_o = p.op("pe", lambda E, e=e, pso=pso, half=half: E.matmul(
                    pso[:, :], sb_axT[:, e * 128:(e + 1) * 128], sb_Bm[:, e * D + 512 * half: e * D + 512 * (half + 1)],
                    start=False, stop=(e == E_ - 1)),
                    wait=(ev_trs[e],), inc=(e == E_ - 1))
            ev_half.append(ev_o)
        pv_axT_read = ev_half[1]
        e_lx = p.dma("sp", sb_xc[:, :], x_in[128 * i:128 * (i + 1), :], p.sem("d_x2"), wait=(pv_store,))
        e_ev1 = p.op("dve", lambda E: E.tensor_add(sb_t1[:, 0:512], ps_o[:, :], sb_bb[:, 0:512]),
                     wait=(ev_half[0], pv_store) if i else (ev_half[0],), inc=True)
        e_ev2 = p.op("dve", lambda E: E.tensor_add(sb_t1[:, 512:1024], ps_o2[:, :], sb_bb[:, 512:1024]),
                     wait=(ev_half[1],), inc=True)
        e_c1 = p.op("dve", lambda E: E.scalar_tensor_tensor(sb_out[:, :], sb_t1[:, :], 0.5, sb_xc[:, :], ALU.mult, ALU.add),
                    wait=(e_lx, e_ev1, e_ev2), inc=True)
        pv_store = p.dma("act", x_out[128 * i:128 * (i + 1), :], sb_out[:, :], d_o, wait=(e_c1,))
        p.wait("pe", e_ev1)
        p.wait("pe", e_ev2)

    p.wait("act", p.ev("d_o"))
    return fin()



def build_l0_attn(fz=None, xio=None, pre="a0_"):
    p, nc, pin, dramt, sbuf, psum, fin = _env(fz, pre)
    NR, NQ = 4096, 2048
    NTK = NR // 128
    NTQ = NQ // 128

    xb = xio["xb"] if xio else pin("xb", [NR, D], F32)
    kpos = pin("kpos", [NR], F32)
    w_qkv = pin("w_qkv", [D, 3 * D], F32)
    b_qkv = pin("b_qkv", [3 * D], F32)
    w_o = pin("w_o", [D, D], F32)
    b_o = pin("b_o", [D], F32)
    ident = pin("ident", [128, 128], F32)
    x_out = xio["x_out"] if xio else nc.dram_tensor("x_out", [NQ, D], F32, kind="ExternalOutput")

    wqkv_bf = dramt("wqkv_bf", [D, 3 * D], BF16)
    wo_bf = dramt("wo_bf", [D, D], BF16)
    kT_sp = dramt("kT_sp", [D, NR], BF16)
    vv_sp = dramt("vv_sp", [NR, D], BF16)
    qT_sp = dramt("qT_sp", [D, NQ], BF16)

    slab = sbuf("slab", [128, 8 * NR], BF16)
    sb_S = sbuf("sb_S", [128, NR], F32)
    sb_msk = sbuf("sb_msk", [128, NR], F32)
    sb_P = sbuf("sb_P", [128, NR], BF16)
    sb_PT = sbuf("sb_PT", [128, 2 * 512], BF16)
    sb_kT = sbuf("sb_kT", [128, NR], BF16)
    sb_qTs = sbuf("sb_qTs", [128, 128], BF16)
    sb_oq = sbuf("sb_oq", [128, D], F32)
    sb_oqb = sbuf("sb_oqb", [128, D], BF16)
    sb_oT = sbuf("sb_oT", [128, 8 * 128], BF16)
    sb_kpB = sbuf("sb_kpB", [128, NR], F32)
    sb_cut = sbuf("sb_cut", [128, NTQ], F32)
    sb_wnat = sbuf("sb_wnat", [128, 2 * 4096], BF16)
    sb_wo = sbuf("sb_wo", [128, 8 * D], BF16)
    sb_stg = sbuf("sb_stg", [128, 2 * 512], BF16)
    sb_res = sbuf("sb_res", [128, D], F32)
    sb_bq = sbuf("sb_bq", [128, 24], F32)
    sb_bv = sbuf("sb_bv", [128, D], F32)
    sb_bo = sbuf("sb_bo", [128, D], F32)
    sb_xo = sbuf("sb_xo", [128, D], F32)
    sb_st = sbuf("sb_st", [128, 8], F32)
    sb_id16 = sbuf("sb_id16", [128, 128], BF16)

    ps_a = psum("ps_a", [128, 512], F32)
    ps_b = psum("ps_b", [128, 512], F32)
    ps_av = psum("ps_av", [128, 512], F32)
    ps_tr = psum("ps_tr", [128, 512], BF16)
    ps_tr2 = psum("ps_tr2", [128, 512], BF16)

    d_c = p.sem("d_c")
    d_w = p.sem("d_w")
    d_cn = p.sem("d_cn")
    d_sp = p.sem("d_sp")
    d_kv = p.sem("d_kv")
    d_x2 = p.sem("d_x2")
    d_o = p.sem("d_o")

    p.dma("pool", wqkv_bf.ap(), w_qkv.ap(), d_c)
    p.dma("pool", wo_bf.ap(), w_o.ap(), d_c)
    cast_done = p.ev("d_c")

    s = emit_prelude(p, nc, xb, alloc=(sbuf, psum, dramt), router=False, store_ln=False, ident_dram=ident,
                     n_rows=NR, xnT_buf=slab)

    p.dma("act", sb_kpB[:, :], bass.AP(kpos, 0, [[0, 128], [1, NR]]), d_cn)
    p.dma("act", sb_cut[:, :], kpos.ap()[0:NQ].rearrange("(i p) -> p i", p=128), d_cn,
          allow_slow_non_contiguous=True)
    p.dma("act", sb_bq[:, :], b_qkv.ap().rearrange("(i p) -> p i", p=128), d_cn,
          allow_slow_non_contiguous=True)
    p.dma("act", sb_bv[:, :], bass.AP(b_qkv, 2048, [[0, 128], [1, D]]), d_cn)
    p.dma("act", sb_bo[:, :], bass.AP(b_o, 0, [[0, 128], [1, D]]), d_cn)
    e_k0 = p.ev("d_cn")
    ev_const = (e_k0, e_k0)
    psum_last = {0: None, 1: None}
    e_id16 = p.op("dve", lambda E: E.tensor_copy(sb_id16[:, :], s.sb_id32[:, :]), wait=(s.e_in0,), inc=True)

    p.wait("sp", cast_done)
    for dc in range(8):
        p.dma("sp", sb_wo[:, dc * D:(dc + 1) * D], wo_bf[dc * 128:(dc + 1) * 128, :], p.sem("d_wo"))
    e_wo = p.ev("d_wo")

    stg_last = {0: None, 1: None}
    wnat_use = {0: None, 1: None}
    ev_wl = {}
    for oc in range(16):
        par = oc % 2
        ev_wl[oc] = p.dma("sp", sb_wnat[:, par * 4096:par * 4096 + 1024].rearrange("p (i c) -> p i c", c=128),
                          wqkv_bf.ap().rearrange("(i p) c -> p i c", p=128)[:, :, 128 * oc:128 * (oc + 1)], p.sem("d_w%d" % par),
                          wait=(wnat_use[par],))
        ntf = 4 if oc < 8 else 8
        for tf in range(ntf):
            psx = ps_a if tf % 2 == 0 else ps_b
            ev_mm = None
            for dc in range(8):
                wl2 = []
                if dc == 0:
                    wl2 = [ev_wl[oc], psum_last[tf % 2]]
                    if oc == 0 and tf == 0:
                        wl2 += [s.ev_dve_done] + list(ev_const)
                ev_mm = p.op("pe", lambda E, dc=dc, oc=oc, tf=tf, par=par, psx=psx: E.matmul(
                    psx[:, :], sb_wnat[:, par * 4096 + dc * 128:par * 4096 + dc * 128 + 128],
                    slab[:, dc * NR + 512 * tf: dc * NR + 512 * (tf + 1)],
                    start=(dc == 0), stop=(dc == 7)),
                    wait=tuple([w for w in wl2 if w]), inc=(dc == 7))
            spar = tf % 2
            e_ev = p.op("act", lambda E, oc=oc, spar=spar, psx=psx: E.activation(
                sb_stg[:, spar * 512:(spar + 1) * 512], psx[:, :], AF.Identity, bias=sb_bq[:, oc:oc + 1]),
                wait=tuple([w for w in (ev_mm, e_k0, stg_last[spar]) if w]), inc=True)
            psum_last[tf % 2] = e_ev
            if oc < 8:
                dst = qT_sp[128 * oc:128 * (oc + 1), 512 * tf:512 * (tf + 1)]
            else:
                dst = kT_sp[128 * (oc - 8):128 * (oc - 7), 512 * tf:512 * (tf + 1)]
            e_st = p.dma("act", dst, sb_stg[:, spar * 512:(spar + 1) * 512], p.sem("d_sp%d" % spar), wait=(e_ev,))
            stg_last[spar] = e_st
        wnat_use[par] = ev_mm

    ev_vw = {}
    for vh in range(2):
        ev_vw[vh] = p.dma("sp", sb_wnat[:, vh * 4096:(vh + 1) * 4096].rearrange("p (i c) -> p i c", c=512),
                          wqkv_bf.ap().rearrange("(i p) c -> p i c", p=128)[:, :, 2048 + 512 * vh:2048 + 512 * (vh + 1)], p.sem("d_w%d" % vh),
                          wait=(wnat_use[vh],))
    for tc in range(NTK):
        for vh in range(2):
            psx = ps_a if vh == 0 else ps_b
            ev_mm = None
            for dc in range(8):
                wl2 = []
                if dc == 0:
                    wl2 = [ev_vw[vh], psum_last[vh]]
                ev_mm = p.op("pe", lambda E, dc=dc, tc=tc, vh=vh, psx=psx: E.matmul(
                    psx[:, :], slab[:, dc * NR + 128 * tc: dc * NR + 128 * (tc + 1)],
                    sb_wnat[:, vh * 4096 + dc * 512: vh * 4096 + dc * 512 + 512],
                    start=(dc == 0), stop=(dc == 7)),
                    wait=tuple([w for w in wl2 if w]), inc=(dc == 7))
            e_ev = p.op("dve", lambda E, vh=vh, psx=psx: E.tensor_add(sb_stg[:, vh * 512:(vh + 1) * 512], psx[:, :], sb_bv[:, vh * 512:(vh + 1) * 512]),
                        wait=tuple([w for w in (ev_mm, stg_last[vh]) if w]), inc=True)
            psum_last[vh] = e_ev
            e_st = p.dma("act", vv_sp[128 * tc:128 * (tc + 1), 512 * vh:512 * (vh + 1)],
                         sb_stg[:, vh * 512:(vh + 1) * 512], p.sem("d_sp%d" % vh), wait=(e_ev,))
            stg_last[vh] = e_st
    ev_proj_done = (p.ev("d_sp0"), p.ev("d_sp1"))

    p.wait("sp", ev_proj_done[0])
    p.wait("sp", ev_proj_done[1])
    p.wait("sp", s.ev_dve_done)
    for tc in range(NTK):
        p.dma("sp", slab[:, 1024 * tc:1024 * (tc + 1)], vv_sp[128 * tc:128 * (tc + 1), :], p.sem("d_vl"))
    e_vload = p.ev("d_vl")

    kv_last = {0: None}
    res_last = {0: None, 1: None}
    ev_msk = None
    ev_msk_prev_use = None
    prev_store = None
    prev_oq_written = None
    prev_oT_used = None
    for qt in range(NTQ):
        ev_msk = p.op("dve", lambda E, qt=qt: E.tensor_scalar(
            sb_msk[:, :], sb_kpB[:, :], sb_cut[:, qt:qt + 1], -1e9, ALU.is_gt, ALU.mult),
            wait=tuple([w for w in (ev_const[0], ev_const[1], ev_msk_prev_use) if w]), inc=True)
        for hp in range(8):
            par = 0
            e_kl = p.dma("sp", sb_kT[:, :], kT_sp[128 * hp:128 * (hp + 1), :], p.sem("d_kv0"),
                         wait=(kv_last[0],))
            e_ql = p.dma("sp", sb_qTs[:, :], qT_sp[128 * hp:128 * (hp + 1), 128 * qt:128 * (qt + 1)], p.sem("d_kv0"))
            e_kv = p.ev("d_kv0")
            for h2 in range(2):
                po = 64 * h2
                sc_evac = {0: psum_last[0], 1: psum_last[1]}
                prev_eex = locals().get("e_ex")
                for c in range(8):
                    cpar = c % 2
                    psx = ps_a if cpar == 0 else ps_b
                    wl2 = [e_kv] if c == 0 else []
                    if c == 0 and qt == 0 and hp == 0 and h2 == 0:
                        wl2.append(e_vload)
                    wl2.append(sc_evac[cpar])
                    ev_mm = p.op("pe", lambda E, c=c, po=po, psx=psx: E.matmul(
                        psx[:, :], sb_qTs[po:po + 64, :],
                        sb_kT[po:po + 64, 512 * c: 512 * (c + 1)],
                        start=True, stop=True),
                        wait=tuple([w for w in wl2 if w]), inc=True)
                    e_sv = p.op("dve", lambda E, c=c, psx=psx: E.scalar_tensor_tensor(
                        sb_S[:, 512 * c:512 * (c + 1)], psx[:, :], 0.125, sb_msk[:, 512 * c:512 * (c + 1)],
                        ALU.mult, ALU.add),
                        wait=(ev_mm, ev_msk) + ((prev_eex,) if c == 0 and prev_eex else ()), inc=True)
                    sc_evac[cpar] = e_sv
                psum_last[0], psum_last[1] = sc_evac[0], sc_evac[1]
                e_nm = p.op("dve", lambda E: E.reduce_max(sb_st[:, 1:2], sb_S[:, :], axis=AX.X, negate=True),
                            wait=(sc_evac[0], sc_evac[1]), inc=True)
                e_ex = p.op("act", lambda E: E.activation(sb_P[:, :], sb_S[:, :], AF.Exp, bias=sb_st[:, 1:2], accum_out=sb_st[:, 0:1]),
                            wait=(e_nm,), inc=True)
                e_rz = p.op("dve", lambda E: E.reciprocal(sb_st[:, 2:3], sb_st[:, 0:1]), wait=(e_ex,), inc=True)
                e_avmm = None
                pt_evac = {0: None, 1: None}
                for g in range(8):
                    gpar = g % 2
                    pstr = ps_tr if gpar == 0 else ps_tr2
                    for a in range(4):
                        wl2 = []
                        if a == 0:
                            wl2 = [e_ex, pt_evac[gpar]]
                        if qt == 0 and hp == 0 and h2 == 0 and g == 0 and a == 0:
                            wl2.append(e_id16)
                        ev_tr = p.op("pe", lambda E, g=g, a=a, pstr=pstr: E.transpose(
                            pstr[:, 128 * a:128 * (a + 1)], sb_P[:, 512 * g + 128 * a: 512 * g + 128 * (a + 1)], sb_id16[:, :]),
                            wait=tuple([w for w in wl2 if w]), inc=(a == 3))
                    e_pt = p.op("dve", lambda E, gpar=gpar, pstr=pstr: E.tensor_copy(sb_PT[:, gpar * 512:(gpar + 1) * 512], pstr[:, :]),
                                wait=(ev_tr, pt_evac[gpar]), inc=True)
                    pt_evac[gpar] = e_pt
                    for a in range(4):
                        tc4 = 4 * g + a
                        wl2 = [e_pt] if a == 0 else []
                        if g == 0 and a == 0:
                            wl2.append(prev_oq_written)
                        ev_avmm = p.op("pe", lambda E, g=g, a=a, tc4=tc4, hp=hp, h2=h2: E.matmul(
                            ps_av[:, 0:64], sb_PT[:, (g % 2) * 512 + 128 * a: (g % 2) * 512 + 128 * (a + 1)],
                            slab[:, 1024 * tc4 + 128 * hp + 64 * h2: 1024 * tc4 + 128 * hp + 64 * h2 + 64],
                            start=(g == 0 and a == 0), stop=(g == 7 and a == 3)),
                            wait=tuple([w for w in wl2 if w]), inc=(g == 7 and a == 3))
                h = 2 * hp + h2
                prev_oq_written = p.op("dve", lambda E, h=h: E.tensor_scalar(
                    sb_oq[:, 64 * h:64 * (h + 1)], ps_av[:, 0:64], sb_st[:, 2:3], None, ALU.mult),
                    wait=(ev_avmm, e_rz, prev_oT_used), inc=True)
                if h2 == 1:
                    kv_last[0] = ev_avmm
        ev_msk_prev_use = psum_last[0]
        e_cb = p.op("dve", lambda E: E.tensor_copy(sb_oqb[:, :], sb_oq[:, :]), wait=(prev_oq_written,), inc=True)
        ev_oT = []
        prev_tr = None
        for dc in range(8):
            e_tr = p.op("pe", lambda E, dc=dc: E.transpose(ps_tr[:, 128:256], sb_oqb[:, 128 * dc:128 * (dc + 1)], sb_id16[:, :]),
                        wait=(e_cb,) if dc == 0 else (prev_tr,), inc=True)
            prev_tr = p.op("dve", lambda E, dc=dc: E.tensor_copy(sb_oT[:, 128 * dc:128 * (dc + 1)], ps_tr[:, 128:256]),
                           wait=(e_tr,), inc=True)
            ev_oT.append(prev_tr)
        e_lx = p.dma("sp", sb_xo[:, :], xb[128 * qt:128 * (qt + 1), :], d_x2, wait=(prev_store,))
        ev_res = []
        for half in range(2):
            psx = ps_a if half == 0 else ps_b
            ev_mm = None
            for dc in range(8):
                wl2 = []
                if dc == 0:
                    wl2 = [ev_oT[-1], e_wo, psum_last[half]]
                ev_mm = p.op("pe", lambda E, dc=dc, half=half, psx=psx: E.matmul(
                    psx[:, :], sb_oT[:, 128 * dc:128 * (dc + 1)], sb_wo[:, dc * D + 512 * half: dc * D + 512 * (half + 1)],
                    start=(dc == 0), stop=(dc == 7)),
                    wait=tuple([w for w in wl2 if w]), inc=(dc == 7))
            e_r1 = p.op("dve", lambda E, half=half, psx=psx: E.tensor_add(
                sb_res[:, half * 512:(half + 1) * 512], psx[:, :], sb_bo[:, 512 * half:512 * (half + 1)]),
                wait=tuple([w for w in (ev_mm, res_last[half]) if w]), inc=True)
            e_r2 = p.op("dve", lambda E, half=half: E.tensor_add(
                sb_res[:, half * 512:(half + 1) * 512], sb_res[:, half * 512:(half + 1) * 512], sb_xo[:, 512 * half:512 * (half + 1)]),
                wait=(e_r1, e_lx), inc=True)
            psum_last[half] = e_r1
            ev_res.append(e_r2)
        e_st1 = p.dma("act", x_out[128 * qt:128 * (qt + 1), 0:512], sb_res[:, 0:512], p.sem("d_ro0"), wait=(ev_res[0], ev_res[1]))
        e_st2 = p.dma("act", x_out[128 * qt:128 * (qt + 1), 512:1024], sb_res[:, 512:1024], p.sem("d_ro1"))
        res_last[0] = e_st1
        res_last[1] = e_st2
        prev_store = e_st2
        prev_oT_used = ev_mm

    p.wait("act", p.ev("d_ro0"))
    p.wait("act", p.ev("d_ro1"))
    return fin()



def build_l2_attn(fz=None, xio=None, pre="a2_"):
    p, nc, pin, dramt, sbuf, psum, fin = _env(fz, pre)
    NR, NQ = 4096, 2048
    NTK = NR // 128
    NTQ = NQ // 128

    xb = xio["xb"] if xio else pin("xb", [NR, D], F32)
    w_cat = pin("w_cat", [D, 5 * D], F32)
    ln_g = pin("ln_g", [D], F32)
    ln_b = pin("ln_b", [D], F32)
    lamb = pin("lamb", [1], F32)
    ident = pin("ident", [128, 128], F32)
    x_out = xio["x_out"] if xio else nc.dram_tensor("x_out", [NQ, D], F32, kind="ExternalOutput")

    wcat_bf = dramt("wcat_bf", [D, 5 * D], BF16)
    kT_sp = dramt("kT_sp", [2 * D, NR], BF16)
    vv_sp = dramt("vv_sp", [NR, D], BF16)
    qT_sp = dramt("qT_sp", [2 * D, NQ], BF16)

    slab = sbuf("slab", [128, 8 * NR], BF16)
    sb_S = sbuf("sb_S", [128, NR], F32)
    sb_P1 = sbuf("sb_P1", [128, NR], BF16)
    sb_P2 = sbuf("sb_P2", [128, NR], BF16)
    sb_PT = sbuf("sb_PT", [128, 2 * 512], BF16)
    sb_kT = sbuf("sb_kT", [128, NR], BF16)
    sb_qTs = sbuf("sb_qTs", [128, 128], BF16)
    sb_oq = sbuf("sb_oq", [128, D], F32)
    sb_wnat = sbuf("sb_wnat", [128, 2 * 4096], BF16)
    sb_stg = sbuf("sb_stg", [128, 2 * 512], BF16)
    sb_res = sbuf("sb_res", [128, D], F32)
    sb_gB = sbuf("sb_gB", [128, D], F32)
    sb_bB = sbuf("sb_bB", [128, D], F32)
    sb_lam = sbuf("sb_lam", [128, 4], F32)
    sb_xo = sbuf("sb_xo", [128, D], F32)
    sb_st = sbuf("sb_st", [128, 8], F32)
    sb_scr = sbuf("sb_scr2", [128, D], F32)
    sb_id16 = sbuf("sb_id16", [128, 128], BF16)

    ps_a = psum("ps_a", [128, 512], F32)
    ps_b = psum("ps_b", [128, 512], F32)
    ps_av = psum("ps_av", [128, 512], F32)
    ps_tr = psum("ps_tr", [128, 512], BF16)
    ps_tr2 = psum("ps_tr2", [128, 512], BF16)

    d_c = p.sem("d_c")
    d_cn = p.sem("d_cn")
    d_x2 = p.sem("d_x2")

    p.dma("pool", wcat_bf.ap(), w_cat.ap(), d_c)
    cast_done = p.ev("d_c")

    s = emit_prelude(p, nc, xb, alloc=(sbuf, psum, dramt), router=False, store_ln=False, ident_dram=ident,
                     n_rows=NR, xnT_buf=slab)

    p.dma("act", sb_gB[:, :], bass.AP(ln_g, 0, [[0, 128], [1, D]]), d_cn)
    p.dma("act", sb_bB[:, :], bass.AP(ln_b, 0, [[0, 128], [1, D]]), d_cn)
    p.dma("act", sb_lam[:, 0:1], bass.AP(lamb, 0, [[0, 128], [1, 1]]), d_cn)
    e_k0 = p.ev("d_cn")
    e_id16 = p.op("dve", lambda E: E.tensor_copy(sb_id16[:, :], s.sb_id32[:, :]), wait=(s.e_in0,), inc=True)
    e_l1 = p.op("dve", lambda E: E.tensor_scalar(sb_lam[:, 1:2], sb_lam[:, 0:1], -1.0, None, ALU.mult),
                wait=(e_k0,), inc=True)
    e_l2 = p.op("dve", lambda E: E.tensor_scalar(sb_lam[:, 2:3], sb_lam[:, 0:1], -1.0, 1.0, ALU.mult, ALU.add),
                wait=(e_l1,), inc=True)
    e_g = p.op("dve", lambda E: E.tensor_scalar(sb_gB[:, :], sb_gB[:, :], sb_lam[:, 2:3], None, ALU.mult),
               wait=(e_l2,), inc=True)
    e_b = p.op("dve", lambda E: E.tensor_scalar(sb_bB[:, :], sb_bB[:, :], sb_lam[:, 2:3], None, ALU.mult),
               wait=(e_g,), inc=True)
    ev_const = e_b

    psum_last = {0: None, 1: None}
    stg_last = {0: None, 1: None}
    wnat_use = {0: None, 1: None}
    p.wait("sp", cast_done)
    for oc in range(32):
        par = oc % 2
        ev_wl = p.dma("sp", sb_wnat[:, par * 4096:par * 4096 + 1024].rearrange("p (i c) -> p i c", c=128),
                      wcat_bf.ap().rearrange("(i p) c -> p i c", p=128)[:, :, 128 * oc:128 * (oc + 1)],
                      p.sem("d_w%d" % par), wait=(wnat_use[par],))
        ntf = 4 if oc < 16 else 8
        for tf in range(ntf):
            psx = ps_a if tf % 2 == 0 else ps_b
            ev_mm = None
            for dc in range(8):
                wl2 = []
                if dc == 0:
                    wl2 = [ev_wl, psum_last[tf % 2]]
                    if oc == 0 and tf == 0:
                        wl2 += [s.ev_dve_done]
                ev_mm = p.op("pe", lambda E, dc=dc, tf=tf, par=par, psx=psx: E.matmul(
                    psx[:, :], sb_wnat[:, par * 4096 + dc * 128:par * 4096 + dc * 128 + 128],
                    slab[:, dc * NR + 512 * tf: dc * NR + 512 * (tf + 1)],
                    start=(dc == 0), stop=(dc == 7)),
                    wait=tuple([w for w in wl2 if w]), inc=(dc == 7))
            spar = tf % 2
            e_ev = p.op("act", lambda E, spar=spar, psx=psx: E.activation(
                sb_stg[:, spar * 512:(spar + 1) * 512], psx[:, :], AF.Copy),
                wait=tuple([w for w in (ev_mm, stg_last[spar]) if w]), inc=True)
            psum_last[spar] = e_ev
            if oc < 8:
                dst = qT_sp[128 * oc:128 * (oc + 1), 512 * tf:512 * (tf + 1)]
            elif oc < 16:
                dst = qT_sp[1024 + 128 * (oc - 8):1024 + 128 * (oc - 7), 512 * tf:512 * (tf + 1)]
            elif oc < 24:
                dst = kT_sp[128 * (oc - 16):128 * (oc - 15), 512 * tf:512 * (tf + 1)]
            else:
                dst = kT_sp[1024 + 128 * (oc - 24):1024 + 128 * (oc - 23), 512 * tf:512 * (tf + 1)]
            e_st = p.dma("act", dst, sb_stg[:, spar * 512:(spar + 1) * 512], p.sem("d_sp%d" % spar), wait=(e_ev,))
            stg_last[spar] = e_st
        wnat_use[par] = ev_mm

    ev_vw = {}
    for vh in range(2):
        ev_vw[vh] = p.dma("sp", sb_wnat[:, vh * 4096:(vh + 1) * 4096].rearrange("p (i c) -> p i c", c=512),
                          wcat_bf.ap().rearrange("(i p) c -> p i c", p=128)[:, :, 4096 + 512 * vh:4096 + 512 * (vh + 1)],
                          p.sem("d_w%d" % vh), wait=(wnat_use[vh],))
    for tc in range(NTK):
        for vh in range(2):
            psx = ps_a if vh == 0 else ps_b
            ev_mm = None
            for dc in range(8):
                wl2 = []
                if dc == 0:
                    wl2 = [ev_vw[vh], psum_last[vh]]
                ev_mm = p.op("pe", lambda E, dc=dc, tc=tc, vh=vh, psx=psx: E.matmul(
                    psx[:, :], slab[:, dc * NR + 128 * tc: dc * NR + 128 * (tc + 1)],
                    sb_wnat[:, vh * 4096 + dc * 512: vh * 4096 + dc * 512 + 512],
                    start=(dc == 0), stop=(dc == 7)),
                    wait=tuple([w for w in wl2 if w]), inc=(dc == 7))
            e_ev = p.op("act", lambda E, vh=vh, psx=psx: E.activation(
                sb_stg[:, vh * 512:(vh + 1) * 512], psx[:, :], AF.Copy),
                wait=tuple([w for w in (ev_mm, stg_last[vh]) if w]), inc=True)
            psum_last[vh] = e_ev
            e_st = p.dma("act", vv_sp[128 * tc:128 * (tc + 1), 512 * vh:512 * (vh + 1)],
                         sb_stg[:, vh * 512:(vh + 1) * 512], p.sem("d_sp%d" % vh), wait=(e_ev,))
            stg_last[vh] = e_st
    ev_proj = (p.ev("d_sp0"), p.ev("d_sp1"))

    p.wait("sp", ev_proj[0])
    p.wait("sp", ev_proj[1])
    p.wait("sp", s.ev_dve_done)
    for tc in range(NTK):
        p.dma("sp", slab[:, 1024 * tc:1024 * (tc + 1)], vv_sp[128 * tc:128 * (tc + 1), :], p.sem("d_vl"))
    e_vload = p.ev("d_vl")

    kv_last = None
    res_last = {0: None, 1: None}
    prev_store = None
    prev_oq_written = None
    prev_ep_used = None
    prev_P1_use0 = None
    prev_P2_use0 = None
    e_ex1 = e_ex2 = None
    for qt in range(NTQ):
        for h in range(16):
            e_kl = p.dma("sp", sb_kT[0:64, :], kT_sp[64 * h:64 * (h + 1), :], p.sem("d_kv0"),
                         wait=(kv_last,))
            p.dma("sp", sb_kT[64:128, :], kT_sp[1024 + 64 * h:1024 + 64 * (h + 1), :], p.sem("d_kv0"))
            p.dma("sp", sb_qTs[0:64, :], qT_sp[64 * h:64 * (h + 1), 128 * qt:128 * (qt + 1)], p.sem("d_kv0"))
            p.dma("sp", sb_qTs[64:128, :], qT_sp[1024 + 64 * h:1024 + 64 * (h + 1), 128 * qt:128 * (qt + 1)], p.sem("d_kv0"))
            e_kv = p.ev("d_kv0")
            for w12 in range(2):
                po = 64 * w12
                sc_evac = {0: psum_last[0], 1: psum_last[1]}
                prev_eex = e_ex2 if w12 == 0 else e_ex1
                for c in range(8):
                    cpar = c % 2
                    psx = ps_a if cpar == 0 else ps_b
                    wl2 = [e_kv] if c == 0 else []
                    if c == 0 and qt == 0 and h == 0 and w12 == 0:
                        wl2.append(e_vload)
                    wl2.append(sc_evac[cpar])
                    ev_mm = p.op("pe", lambda E, c=c, po=po, psx=psx: E.matmul(
                        psx[:, :], sb_qTs[po:po + 64, :], sb_kT[po:po + 64, 512 * c: 512 * (c + 1)],
                        start=True, stop=True),
                        wait=tuple([w for w in wl2 if w]), inc=True)
                    e_sv = p.op("dve", lambda E, c=c, psx=psx: E.tensor_scalar(
                        sb_S[:, 512 * c:512 * (c + 1)], psx[:, :], 0.125, None, ALU.mult),
                        wait=(ev_mm,) + ((prev_eex,) if c == 0 and prev_eex else ()), inc=True)
                    sc_evac[cpar] = e_sv
                psum_last[0], psum_last[1] = sc_evac[0], sc_evac[1]
                e_nm = p.op("dve", lambda E: E.reduce_max(sb_st[:, 1:2], sb_S[:, :], axis=AX.X, negate=True),
                            wait=(sc_evac[0], sc_evac[1]), inc=True)
                if w12 == 0:
                    e_ex1 = p.op("act", lambda E: E.activation(sb_P1[:, :], sb_S[:, :], AF.Exp, bias=sb_st[:, 1:2], accum_out=sb_st[:, 2:3]),
                                 wait=(e_nm, prev_P1_use0), inc=True)
                    e_rz1 = p.op("dve", lambda E: E.reciprocal(sb_st[:, 3:4], sb_st[:, 2:3]), wait=(e_ex1,), inc=True)
                else:
                    e_ex2 = p.op("act", lambda E: E.activation(sb_P2[:, :], sb_S[:, :], AF.Exp, bias=sb_st[:, 1:2], accum_out=sb_st[:, 4:5]),
                                 wait=(e_nm, prev_P2_use0), inc=True)
                    e_rz2 = p.op("dve", lambda E: E.reciprocal(sb_st[:, 5:6], sb_st[:, 4:5]), wait=(e_ex2,), inc=True)
            e_nl = p.op("dve", lambda E: E.tensor_mul(sb_st[:, 6:7], sb_st[:, 5:6], sb_lam[:, 1:2]),
                        wait=(e_rz2, ev_const), inc=True)
            e_cm1 = p.op("dve", lambda E: E.tensor_scalar(sb_P1[:, :], sb_P1[:, :], sb_st[:, 3:4], None, ALU.mult),
                         wait=(e_rz1,), inc=True)
            e_cm2 = p.op("dve", lambda E: E.scalar_tensor_tensor(sb_P1[:, :], sb_P2[:, :], sb_st[:, 6:7], sb_P1[:, :], ALU.mult, ALU.add),
                         wait=(e_nl, e_cm1), inc=True)
            prev_P2_use0 = e_cm2
            ev_avmm = None
            pt_evac = {0: None, 1: None}
            for g in range(8):
                gpar = g % 2
                pstr = ps_tr if gpar == 0 else ps_tr2
                for a in range(4):
                    wl2 = []
                    if a == 0:
                        wl2 = [e_cm2, pt_evac[gpar]]
                        if qt == 0 and h == 0 and g == 0:
                            wl2.append(e_id16)
                    ev_tr = p.op("pe", lambda E, g=g, a=a, pstr=pstr: E.transpose(
                        pstr[:, 128 * a:128 * (a + 1)], sb_P1[:, 512 * g + 128 * a: 512 * g + 128 * (a + 1)], sb_id16[:, :]),
                        wait=tuple([w for w in wl2 if w]), inc=(a == 3))
                e_pt = p.op("dve", lambda E, gpar=gpar, pstr=pstr: E.tensor_copy(sb_PT[:, gpar * 512:(gpar + 1) * 512], pstr[:, :]),
                            wait=(ev_tr, pt_evac[gpar]), inc=True)
                pt_evac[gpar] = e_pt
                for a in range(4):
                    tc4 = 4 * g + a
                    wl2 = [e_pt] if a == 0 else []
                    if g == 0 and a == 0:
                        wl2.append(prev_oq_written)
                    ev_avmm = p.op("pe", lambda E, g=g, a=a, tc4=tc4, h=h: E.matmul(
                        ps_av[:, 0:64], sb_PT[:, (g % 2) * 512 + 128 * a: (g % 2) * 512 + 128 * (a + 1)],
                        slab[:, 1024 * tc4 + 64 * h: 1024 * tc4 + 64 * h + 64],
                        start=(g == 0 and a == 0), stop=(g == 7 and a == 3)),
                        wait=tuple([w for w in wl2 if w]), inc=(g == 7 and a == 3))
            prev_P1_use0 = ev_tr
            prev_oq_written = p.op("dve", lambda E, h=h: E.tensor_copy(sb_oq[:, 64 * h:64 * (h + 1)], ps_av[:, 0:64]),
                                   wait=(ev_avmm, prev_ep_used), inc=True)
            kv_last = ev_avmm
        e1 = p.op("dve", lambda E: E.reduce_sum(sb_st[:, 0:1], sb_oq[:, :], axis=AX.X),
                  wait=(prev_oq_written,), inc=True)
        e2 = p.op("dve", lambda E: E.tensor_scalar(sb_st[:, 0:1], sb_st[:, 0:1], -1.0 / D, None, ALU.mult),
                  wait=(e1,), inc=True)
        e_xc = p.op("act", lambda E: E.activation(sb_scr[:, :], sb_oq[:, :], AF.Identity, bias=sb_st[:, 0:1]),
                    wait=(e2,), inc=True)
        e_v2 = p.op("act", lambda E: E.activation(sb_res[:, :], sb_scr[:, :], AF.Square, accum_out=sb_st[:, 1:2]),
                    wait=(e_xc, res_last[0], res_last[1]), inc=True)
        e5 = p.op("dve", lambda E: E.tensor_scalar(sb_st[:, 1:2], sb_st[:, 1:2], 1.0 / D, 1e-5, ALU.mult, ALU.add),
                  wait=(e_v2,), inc=True)
        e6 = p.op("act", lambda E: E.activation(sb_st[:, 2:3], sb_st[:, 1:2], AF.Sqrt), wait=(e5,), inc=True)
        e7 = p.op("dve", lambda E: E.reciprocal(sb_st[:, 3:4], sb_st[:, 2:3]), wait=(e6,), inc=True)
        e_xn = p.op("dve", lambda E: E.tensor_scalar(sb_scr[:, :], sb_scr[:, :], sb_st[:, 3:4], None, ALU.mult),
                    wait=(e7,), inc=True)
        e_lx = p.dma("sp", sb_xo[:, :], xb[128 * qt:128 * (qt + 1), :], d_x2, wait=(prev_store,))
        e_o1 = p.op("dve", lambda E: E.tensor_mul(sb_res[:, :], sb_scr[:, :], sb_gB[:, :]),
                    wait=(e_xn, ev_const), inc=True)
        e_o2 = p.op("dve", lambda E: E.tensor_add(sb_res[:, :], sb_res[:, :], sb_bB[:, :]), wait=(e_o1,), inc=True)
        e_o3 = p.op("dve", lambda E: E.tensor_add(sb_res[:, :], sb_res[:, :], sb_xo[:, :]), wait=(e_o2, e_lx), inc=True)
        e_st1 = p.dma("act", x_out[128 * qt:128 * (qt + 1), 0:512], sb_res[:, 0:512], p.sem("d_ro0"), wait=(e_o3,))
        e_st2 = p.dma("act", x_out[128 * qt:128 * (qt + 1), 512:1024], sb_res[:, 512:1024], p.sem("d_ro1"))
        res_last[0] = e_st1
        res_last[1] = e_st2
        prev_store = e_st2
        prev_ep_used = e_v2

    p.wait("act", p.ev("d_ro0"))
    p.wait("act", p.ev("d_ro1"))
    return fin()



DEBUG_L1 = False


def build_l1_attn(fz=None, xio=None, pre="a1_"):
    p, nc, pin, dramt, sbuf, psum, fin = _env(fz, pre)
    NR, NQ, H = 4096, 2048, 16
    NTK = NR // 128
    NTQ = NQ // 128
    GAMMAS = [1.0 - 2.0 ** (-5.0 - h) for h in range(H)]

    xb = xio["xb"] if xio else pin("xb", [NR, D], F32)
    kpos = pin("kpos", [NR], F32)
    w_cat = pin("w_cat", [D, 3 * D], F32)
    pkv = pin("pkv", [H, 64, 64], F32)
    gnw_e = pin("gnw_e", [D], F32)
    gnb_e = pin("gnb_e", [D], F32)
    tabU = pin("tabU", [H, 128, 128], BF16)
    gpcol = pin("gpcol", [128, H], BF16)
    gdrow = pin("gdrow", [H, 128], BF16)
    cvec = pin("cvec", [H, NQ], BF16)
    ident = pin("ident", [128, 128], F32)
    x_out = xio["x_out"] if xio else nc.dram_tensor("x_out", [NQ, D], F32, kind="ExternalOutput")
    dbg_ret = nc.dram_tensor("dbg_ret", [NQ, D], F32, kind="ExternalOutput") if DEBUG_L1 else None

    wcat_bf = dramt("wcat_bf", [D, 3 * D], BF16)
    kT_sp = dramt("kT_sp", [D, NR], BF16)
    qT_sp = dramt("qT_sp", [D, NR], BF16)
    vv_sp = dramt("vv_sp", [NR, D], BF16)
    ret_sp = dramt("ret_sp", [NQ, D], F32)
    cc_in = dramt("cc_in", [2], F32)
    cc_out = dramt("cc_out", [2], F32)

    slab = sbuf("slab", [128, 8 * NR], BF16)
    sb_D = sbuf("sb_D", [128, NR], F32)
    sb_M = sbuf("sb_M", [128, NR], BF16)
    sb_P = sbuf("sb_P", [128, NR], BF16)
    sb_PT = sbuf("sb_PT", [128, 2 * 512], BF16)
    sb_k1a = sbuf("sb_k1a", [64, NR], BF16)
    sb_q1a = sbuf("sb_q1a", [64, NR], BF16)
    sb_pp = sbuf("sb_pp", [128, NTK * 64], BF16)
    sb_kpB = sbuf("sb_kpB", [128, NR], F32)
    sb_cut = sbuf("sb_cut", [128, NTQ], F32)
    sb_wnat = sbuf("sb_wnat", [128, 2 * 4096], BF16)
    sb_stg = sbuf("sb_stg", [128, 2 * 512], BF16)
    sb_res = sbuf("sb_res", [128, D], F32)
    sb_ret = sbuf("sb_ret", [128, D], F32)
    sb_gnw = sbuf("sb_gnw", [128, D], F32)
    sb_gnb = sbuf("sb_gnb", [128, D], F32)
    sb_tabU = sbuf("sb_tabU", [128, 128], BF16)
    sb_gp = sbuf("sb_gp", [128, H], BF16)
    sb_gps = sbuf("sb_gps", [128, 1], BF16)
    sb_gdr = sbuf("sb_gdr", [1, 128], BF16)
    sb_cvr = sbuf("sb_cvr", [1, NQ], BF16)
    sb_pkv = sbuf("sb_pkv", [64, 64], F32)
    sb_pkvb = sbuf("sb_pkvb", [64, 64], BF16)
    sb_SstF = sbuf("sb_SstF", [1, 64], F32)
    sb_Sst = sbuf("sb_Sst", [1, 64], BF16)
    sb_OW = sbuf("sb_OW", [1, 64], BF16)
    sb_xo = sbuf("sb_xo", [128, D], F32)
    sb_st = sbuf("sb_st", [128, 8], F32)
    sb_acc2 = sbuf("sb_acc2", [128, 2], F32)
    sb_sc2 = sbuf("sb_sc2", [2, 1], F32)
    sb_mr = sbuf("sb_mr", [1, 2], F32)
    sb_mrB = sbuf("sb_mrB", [128, 2], F32)
    sb_one = sbuf("sb_one", [1, 128], F32)
    sb_onec = sbuf("sb_onec", [128, 1], F32)
    sb_id16 = sbuf("sb_id16", [128, 128], BF16)
    sb_dump = sbuf("sb_dump2", [128, D], F32)

    ps_a = psum("ps_a", [128, 512], F32)
    ps_b = psum("ps_b", [128, 512], F32)
    ps_av = psum("ps_av", [128, 512], F32)
    ps_tr = psum("ps_tr", [128, 512], BF16)
    ps_tr2 = psum("ps_tr2", [128, 512], BF16)

    d_c = p.sem("d_c")
    d_cn = p.sem("d_cn")
    d_x2 = p.sem("d_x2")
    d_cc = p.sem("d_cc")

    p.dma("pool", wcat_bf.ap(), w_cat.ap(), d_c)
    cast_done = p.ev("d_c")

    s = emit_prelude(p, nc, xb, alloc=(sbuf, psum, dramt), router=False, store_ln=False, ident_dram=ident,
                     n_rows=NR, xnT_buf=slab)

    p.dma("act", sb_kpB[:, :], bass.AP(kpos, 0, [[0, 128], [1, NR]]), d_cn)
    p.dma("act", sb_cut[:, :], kpos.ap()[0:NQ].rearrange("(i p) -> p i", p=128), d_cn,
          allow_slow_non_contiguous=True)
    p.dma("act", sb_gnw[:, :], bass.AP(gnw_e, 0, [[0, 128], [1, D]]), d_cn)
    p.dma("act", sb_gnb[:, :], bass.AP(gnb_e, 0, [[0, 128], [1, D]]), d_cn)
    p.dma("act", sb_gp[:, :], gpcol.ap(), d_cn)
    e_k0 = p.ev("d_cn")
    e_id16 = p.op("dve", lambda E: E.tensor_copy(sb_id16[:, :], s.sb_id32[:, :]), wait=(s.e_in0,), inc=True)
    e_one = p.op("dve", lambda E: E.memset(sb_one[:, :], 1.0), inc=True)
    e_onec = p.op("dve", lambda E: E.memset(sb_onec[:, :], 1.0), inc=True)

    psum_last = {0: None, 1: None}
    stg_last = {0: None, 1: None}
    wnat_use = {0: None, 1: None}
    p.wait("sp", cast_done)
    for oc in range(16):
        par = oc % 2
        ev_wl = p.dma("sp", sb_wnat[:, par * 4096:par * 4096 + 1024].rearrange("p (i c) -> p i c", c=128),
                      wcat_bf.ap().rearrange("(i p) c -> p i c", p=128)[:, :, 128 * oc:128 * (oc + 1)],
                      p.sem("d_w%d" % par), wait=(wnat_use[par],))
        for tf in range(8):
            psx = ps_a if tf % 2 == 0 else ps_b
            ev_mm = None
            for dc in range(8):
                wl2 = []
                if dc == 0:
                    wl2 = [ev_wl, psum_last[tf % 2]]
                    if oc == 0 and tf == 0:
                        wl2 += [s.ev_dve_done]
                ev_mm = p.op("pe", lambda E, dc=dc, tf=tf, par=par, psx=psx: E.matmul(
                    psx[:, :], sb_wnat[:, par * 4096 + dc * 128:par * 4096 + dc * 128 + 128],
                    slab[:, dc * NR + 512 * tf: dc * NR + 512 * (tf + 1)],
                    start=(dc == 0), stop=(dc == 7)),
                    wait=tuple([w for w in wl2 if w]), inc=(dc == 7))
            spar = tf % 2
            e_ev = p.op("act", lambda E, spar=spar, psx=psx: E.activation(
                sb_stg[:, spar * 512:(spar + 1) * 512], psx[:, :], AF.Copy),
                wait=tuple([w for w in (ev_mm, stg_last[spar]) if w]), inc=True)
            psum_last[spar] = e_ev
            if oc < 8:
                dst = qT_sp[128 * oc:128 * (oc + 1), 512 * tf:512 * (tf + 1)]
            else:
                dst = kT_sp[128 * (oc - 8):128 * (oc - 7), 512 * tf:512 * (tf + 1)]
            e_st = p.dma("act", dst, sb_stg[:, spar * 512:(spar + 1) * 512], p.sem("d_sp%d" % spar), wait=(e_ev,))
            stg_last[spar] = e_st
        wnat_use[par] = ev_mm

    ev_vw = {}
    for vh in range(2):
        ev_vw[vh] = p.dma("sp", sb_wnat[:, vh * 4096:(vh + 1) * 4096].rearrange("p (i c) -> p i c", c=512),
                          wcat_bf.ap().rearrange("(i p) c -> p i c", p=128)[:, :, 2048 + 512 * vh:2048 + 512 * (vh + 1)],
                          p.sem("d_w%d" % vh), wait=(wnat_use[vh],))
    for tc in range(NTK):
        for vh in range(2):
            psx = ps_a if vh == 0 else ps_b
            ev_mm = None
            for dc in range(8):
                wl2 = []
                if dc == 0:
                    wl2 = [ev_vw[vh], psum_last[vh]]
                ev_mm = p.op("pe", lambda E, dc=dc, tc=tc, vh=vh, psx=psx: E.matmul(
                    psx[:, :], slab[:, dc * NR + 128 * tc: dc * NR + 128 * (tc + 1)],
                    sb_wnat[:, vh * 4096 + dc * 512: vh * 4096 + dc * 512 + 512],
                    start=(dc == 0), stop=(dc == 7)),
                    wait=tuple([w for w in wl2 if w]), inc=(dc == 7))
            e_ev = p.op("act", lambda E, vh=vh, psx=psx: E.activation(
                sb_stg[:, vh * 512:(vh + 1) * 512], psx[:, :], AF.Copy),
                wait=tuple([w for w in (ev_mm, stg_last[vh]) if w]), inc=True)
            psum_last[vh] = e_ev
            e_st = p.dma("act", vv_sp[128 * tc:128 * (tc + 1), 512 * vh:512 * (vh + 1)],
                         sb_stg[:, vh * 512:(vh + 1) * 512], p.sem("d_sp%d" % vh), wait=(e_ev,))
            stg_last[vh] = e_st
    ev_proj = (p.ev("d_sp0"), p.ev("d_sp1"))

    p.wait("sp", ev_proj[0])
    p.wait("sp", ev_proj[1])
    p.wait("sp", s.ev_dve_done)
    for tc in range(NTK):
        p.dma("sp", slab[:, 1024 * tc:1024 * (tc + 1)], vv_sp[128 * tc:128 * (tc + 1), :], p.sem("d_vl"))
    e_vload = p.ev("d_vl")

    kv_last = None
    ret_last = None
    pp_last = None
    prev_Duse = None
    prev_Puse = None
    ev_st_upd = None
    ev_OW_use = None
    ev_gd_use = None
    ev_tabU_use = None
    for h in range(H):
        lng = float(np.log(GAMMAS[h]))
        g128 = float(GAMMAS[h] ** 128)
        e_kl = p.dma("sp", sb_k1a[:, :], kT_sp[64 * h:64 * (h + 1), :], p.sem("d_kv0"), wait=(kv_last,))
        p.dma("sp", sb_q1a[:, :], qT_sp[64 * h:64 * (h + 1), :], p.sem("d_kv0"))
        p.dma("sp", sb_tabU[:, :], tabU[h], p.sem("d_kv0"), wait=(ev_tabU_use,))
        p.dma("sp", sb_gdr[:, :], gdrow.ap()[h:h + 1, :], p.sem("d_kv0"), wait=(ev_gd_use,))
        p.dma("sp", sb_cvr[:, :], cvec.ap()[h:h + 1, :], p.sem("d_kv0"), wait=(ev_OW_use,))
        p.dma("sp", sb_pkv[:, :], pkv[h], p.sem("d_kv0"), wait=(pp_last,))
        e_kv = p.ev("d_kv0")
        e_pkvb = p.op("dve", lambda E: E.tensor_copy(sb_pkvb[:, :], sb_pkv[:, :]), wait=(e_kv,), inc=True)
        ev_ppmm = None
        e_ppe = pp_last
        for tc in range(NTK):
            wl2 = [e_pkvb, psum_last[0]] if tc == 0 else []
            if h == 0 and tc == 0:
                wl2 += [e_vload, psum_last[1]]
            if tc == 0 and pp_last is not None:
                wl2.append(pp_last)
            ev_ppmm = p.op("pe", lambda E, tc=tc: E.matmul(
                ps_a[:, 0:64], sb_q1a[:, 128 * tc:128 * (tc + 1)], sb_pkvb[:, :],
                start=True, stop=True),
                wait=tuple([w for w in wl2 if w]) if wl2 else ((e_ppe,) if e_ppe else ()), inc=True)
            e_ppe = p.op("dve", lambda E, tc=tc: E.tensor_copy(sb_pp[:, 64 * tc:64 * (tc + 1)], ps_a[:, 0:64]),
                         wait=(ev_ppmm,), inc=True)
            p.wait("pe", e_ppe)
        pp_done = e_ppe
        psum_last[0] = pp_done
        ev_owmm = None
        for a in range(16):
            e_gs = p.op("dve", lambda E, a=a, h=h: E.tensor_scalar(
                sb_gps[:, :], sb_gp[:, h:h + 1], float(GAMMAS[h] ** (128 * a)), None, ALU.mult),
                wait=(pp_done, e_k0, psum_last[1]) if a == 0 else (ev_owmm,), inc=True)
            ev_owmm = p.op("pe", lambda E, a=a: E.matmul(
                ps_b[0:1, 0:64], sb_gps[:, :], sb_pp[:, 64 * (16 + a):64 * (17 + a)],
                start=(a == 0), stop=(a == 15)),
                wait=(e_gs,), inc=True)
        e_OW = p.op("dve", lambda E: E.tensor_copy(sb_OW[:, :], ps_b[0:1, 0:64]), wait=(ev_owmm,), inc=True)
        psum_last[1] = e_OW
        e_S0 = p.op("dve", lambda E: E.memset(sb_SstF[:, :], 0.0), wait=(ev_st_upd,), inc=True)
        e_Sb = p.op("dve", lambda E: E.memset(sb_Sst[:, :], 0.0), wait=(e_S0,), inc=True)
        ev_st_upd = e_Sb

        for qi in range(NTQ - 1, -1, -1):
            e_d1 = p.op("dve", lambda E, qi=qi, lng=lng: E.tensor_scalar(
                sb_D[:, :], sb_kpB[:, :], sb_cut[:, qi:qi + 1], -lng, ALU.subtract, ALU.mult),
                wait=tuple([w for w in (e_k0, prev_Duse, e_OW) if w]), inc=True)
            e_d2 = p.op("dve", lambda E: E.tensor_scalar_min(sb_D[:, :], sb_D[:, :], 0.0), wait=(e_d1,), inc=True)
            e_d3 = p.op("act", lambda E: E.activation(sb_D[:, :], sb_D[:, :], AF.Exp), wait=(e_d2,), inc=True)
            e_m1 = p.op("dve", lambda E, qi=qi: E.tensor_scalar(
                sb_M[:, :], sb_kpB[:, :], sb_cut[:, qi:qi + 1], None, ALU.is_le), wait=(e_d3,), inc=True)
            e_d4 = p.op("dve", lambda E: E.tensor_mul(sb_D[:, :], sb_D[:, :], sb_M[:, :]), wait=(e_m1,), inc=True)
            sc_evac = {0: psum_last[0], 1: psum_last[1]}
            for c in range(8):
                cpar = c % 2
                psx = ps_a if cpar == 0 else ps_b
                wl2 = [e_kv, e_d4] if c == 0 else []
                if c == 0 and prev_Puse is not None:
                    wl2.append(prev_Puse)
                wl2.append(sc_evac[cpar])
                ev_mm = p.op("pe", lambda E, c=c, qi=qi, psx=psx: E.matmul(
                    psx[:, :], sb_q1a[:, 128 * qi:128 * (qi + 1)], sb_k1a[:, 512 * c:512 * (c + 1)],
                    start=True, stop=True),
                    wait=tuple([w for w in wl2 if w]), inc=True)
                e_sv = p.op("dve", lambda E, c=c, psx=psx: E.tensor_tensor(
                    sb_P[:, 512 * c:512 * (c + 1)], psx[:, :], sb_D[:, 512 * c:512 * (c + 1)], ALU.mult),
                    wait=(ev_mm,), inc=True)
                sc_evac[cpar] = e_sv
            psum_last[0], psum_last[1] = sc_evac[0], sc_evac[1]
            prev_Duse = sc_evac[0]
            ev_avmm = None
            pt_evac = {0: None, 1: None}
            for g in range(8):
                gpar = g % 2
                pstr = ps_tr if gpar == 0 else ps_tr2
                for a in range(4):
                    wl2 = []
                    if a == 0:
                        wl2 = [sc_evac[0], sc_evac[1], pt_evac[gpar]]
                        if h == 0 and qi == NTQ - 1 and g == 0:
                            wl2.append(e_id16)
                    ev_tr = p.op("pe", lambda E, g=g, a=a, pstr=pstr: E.transpose(
                        pstr[:, 128 * a:128 * (a + 1)], sb_P[:, 512 * g + 128 * a: 512 * g + 128 * (a + 1)], sb_id16[:, :]),
                        wait=tuple([w for w in wl2 if w]), inc=(a == 3))
                e_pt = p.op("dve", lambda E, gpar=gpar, pstr=pstr: E.tensor_copy(sb_PT[:, gpar * 512:(gpar + 1) * 512], pstr[:, :]),
                            wait=(ev_tr, pt_evac[gpar]), inc=True)
                pt_evac[gpar] = e_pt
                for a in range(4):
                    tc4 = 4 * g + a
                    wl2 = [e_pt] if a == 0 else []
                    if g == 0 and a == 0:
                        wl2.append(ret_last)
                    ev_avmm = p.op("pe", lambda E, g=g, a=a, tc4=tc4, h=h: E.matmul(
                        ps_av[:, 0:64], sb_PT[:, (g % 2) * 512 + 128 * a: (g % 2) * 512 + 128 * (a + 1)],
                        slab[:, 1024 * tc4 + 64 * h: 1024 * tc4 + 64 * h + 64],
                        start=(g == 0 and a == 0), stop=False),
                        wait=tuple([w for w in wl2 if w]), inc=False)
            prev_Puse = ev_avmm
            e_x1 = p.op("pe", lambda E, qi=qi: E.matmul(
                ps_av[:, 0:64], sb_tabU[:, :], sb_pp[:, 64 * qi:64 * (qi + 1)],
                start=False, stop=False), wait=(e_kv, pp_done), inc=False)
            e_x2 = p.op("pe", lambda E: E.matmul(
                ps_av[:, 0:64], sb_gdr[:, :], sb_Sst[:, :],
                start=False, stop=False), wait=(ev_st_upd,), inc=False)
            e_x3 = p.op("pe", lambda E, qi=qi: E.matmul(
                ps_av[:, 0:64], sb_cvr[:, 128 * qi:128 * (qi + 1)], sb_OW[:, :],
                start=False, stop=True), wait=(e_OW,), inc=True)
            e_rc = p.op("dve", lambda E: E.tensor_copy(sb_ret[:, 0:64], ps_av[:, 0:64]),
                        wait=(e_x3, ret_last), inc=True)
            ret_last = p.dma("act", ret_sp[128 * qi:128 * (qi + 1), 64 * h:64 * (h + 1)], sb_ret[:, 0:64],
                             p.sem("d_rt"), wait=(e_rc,))
            e_w1 = p.op("pe", lambda E, qi=qi, h=h: E.matmul(
                ps_b[0:1, 64:128], sb_gp[:, h:h + 1], sb_pp[:, 64 * qi:64 * (qi + 1)],
                start=True, stop=True), wait=(e_x3, psum_last[1]), inc=True)
            e_w2 = p.op("dve", lambda E, g128=g128: E.scalar_tensor_tensor(
                sb_SstF[:, :], sb_SstF[:, :], g128, ps_b[0:1, 64:128], ALU.mult, ALU.add),
                wait=(e_w1, ev_st_upd), inc=True)
            ev_st_upd = p.op("dve", lambda E: E.tensor_copy(sb_Sst[:, :], sb_SstF[:, :]), wait=(e_w2,), inc=True)
            psum_last[1] = e_w2
        kv_last = e_w1
        pp_last = e_w1
        ev_OW_use = e_w1
        ev_gd_use = e_w1
        ev_tabU_use = e_w1

    if DEBUG_L1:
        p.dma("pool", dbg_ret.ap(), ret_sp.ap(), p.sem("d_dbg"), wait=(p.ev("d_rt"),))
    e_z = p.op("dve", lambda E: E.memset(sb_acc2[:, :], 0.0), inc=True)
    acc_last = e_z
    e_lr_prev = None
    ret_all = p.ev("d_rt")
    for i in range(NTQ):
        e_lr = p.dma("sp", sb_ret[:, :], ret_sp[128 * i:128 * (i + 1), :], p.sem("d_r2"),
                     wait=(ret_all if i == 0 else None, e_lr_prev))
        e_r1 = p.op("dve", lambda E: E.reduce_sum(sb_st[:, 0:1], sb_ret[:, :], axis=AX.X),
                    wait=(e_lr,), inc=True)
        e_r2 = p.op("act", lambda E: E.activation(sb_dump[:, :], sb_ret[:, :], AF.Square, accum_out=sb_st[:, 1:2]),
                    wait=(e_lr, e_lr_prev2 if i > 0 else None) if i > 0 else (e_lr,), inc=True)
        e_a1 = p.op("dve", lambda E: E.tensor_add(sb_acc2[:, 0:1], sb_acc2[:, 0:1], sb_st[:, 0:1]),
                    wait=(e_r1, acc_last), inc=True)
        e_a2 = p.op("dve", lambda E: E.tensor_add(sb_acc2[:, 1:2], sb_acc2[:, 1:2], sb_st[:, 1:2]),
                    wait=(e_r2, e_a1), inc=True)
        acc_last = e_a2
        e_lr_prev = e_a2
        e_lr_prev2 = e_r2
    e_pr = p.op("pe", lambda E: E.matmul(ps_b[0:2, 0:1], sb_acc2[:, :], sb_onec[:, :], start=True, stop=True),
                wait=(acc_last, e_onec, ev_st_upd, psum_last[0], psum_last[1]), inc=True)
    e_pc = p.op("dve", lambda E: E.tensor_copy(sb_sc2[:, :], ps_b[0:2, 0:1]), wait=(e_pr,), inc=True)
    e_cw = p.dma("sp", cc_in.ap(), sb_sc2.ap(), p.sem("d_ci"), wait=(e_pc,))
    e_cc = p.op("pool", lambda E: E.collective_compute(
        "AllReduce", ALU.add, replica_groups=[[0, 1], [2, 3], [4, 5], [6, 7]],
        ins=[cc_in.ap()], outs=[cc_out.ap()]),
        wait=(e_cw,), inc="d_cc", inc_amt=1)
    e_cl = p.dma("sp", sb_mr[:, :], cc_out.ap().rearrange("(o k) -> o k", o=1), p.sem("d_cl"), wait=(e_cc,))
    NTOT = float(NR * D)
    e_s1 = p.op("dve", lambda E: E.tensor_scalar(sb_mr[:, 0:1], sb_mr[:, 0:1], 1.0 / NTOT, None, ALU.mult),
                wait=(e_cl,), inc=True)
    e_s2 = p.op("dve", lambda E: E.tensor_scalar(sb_mr[:, 1:2], sb_mr[:, 1:2], 1.0 / NTOT, None, ALU.mult),
                wait=(e_s1,), inc=True)
    e_s3 = p.op("dve", lambda E: E.scalar_tensor_tensor(sb_st[0:1, 4:5], sb_mr[:, 0:1], -1.0, sb_mr[:, 0:1], ALU.mult, ALU.mult),
                wait=(e_s2,), inc=True)
    e_s4 = p.op("dve", lambda E: E.tensor_add(sb_mr[:, 1:2], sb_mr[:, 1:2], sb_st[0:1, 4:5]), wait=(e_s3,), inc=True)
    e_s4b = p.op("dve", lambda E: E.tensor_scalar(sb_mr[:, 1:2], sb_mr[:, 1:2], 1e-5, None, ALU.add), wait=(e_s4,), inc=True)
    e_s5 = p.op("act", lambda E: E.activation(sb_mr[:, 1:2], sb_mr[:, 1:2], AF.Sqrt), wait=(e_s4b,), inc=True)
    e_s6 = p.op("dve", lambda E: E.reciprocal(sb_mr[:, 1:2], sb_mr[:, 1:2]), wait=(e_s5,), inc=True)
    e_s7 = p.op("dve", lambda E: E.tensor_scalar(sb_mr[:, 0:1], sb_mr[:, 0:1], -1.0, None, ALU.mult),
                wait=(e_s6,), inc=True)
    e_bm = p.op("pe", lambda E: E.matmul(ps_b[:, 0:2], sb_one[:, :], sb_mr[:, :], start=True, stop=True),
                wait=(e_s7, e_one, e_pc), inc=True)
    e_bc = p.op("dve", lambda E: E.tensor_copy(sb_mrB[:, :], ps_b[:, 0:2]), wait=(e_bm,), inc=True)

    prev_store = None
    e_n5 = None
    for i in range(NTQ):
        e_lr = p.dma("sp", sb_ret[:, :], ret_sp[128 * i:128 * (i + 1), :], p.sem("d_r2"),
                     wait=(e_lr_prev if i == 0 else None, e_n5))
        e_lx = p.dma("sp", sb_xo[:, :], xb[128 * i:128 * (i + 1), :], d_x2, wait=(prev_store,))
        e_n1 = p.op("act", lambda E: E.activation(sb_res[:, :], sb_ret[:, :], AF.Identity, bias=sb_mrB[:, 0:1]),
                    wait=(e_lr, e_bc, prev_store), inc=True)
        e_n2 = p.op("dve", lambda E: E.tensor_scalar(sb_res[:, :], sb_res[:, :], sb_mrB[:, 1:2], None, ALU.mult),
                    wait=(e_n1,), inc=True)
        e_n3 = p.op("dve", lambda E: E.tensor_mul(sb_res[:, :], sb_res[:, :], sb_gnw[:, :]), wait=(e_n2,), inc=True)
        e_n4 = p.op("dve", lambda E: E.tensor_add(sb_res[:, :], sb_res[:, :], sb_gnb[:, :]), wait=(e_n3,), inc=True)
        e_n5 = p.op("dve", lambda E: E.tensor_add(sb_res[:, :], sb_res[:, :], sb_xo[:, :]), wait=(e_n4, e_lx), inc=True)
        prev_store = p.dma("act", x_out[128 * i:128 * (i + 1), :], sb_res[:, :], p.sem("d_ro0"), wait=(e_n5,))

    p.wait("act", p.ev("d_ro0"))
    return fin()



def build_l2_inner_attn(fz=None, xio=None, pre="ia_"):
    p, nc, pin, dramt, sbuf, psum, fin = _env(fz, pre)
    HALO = 384
    NRA = HALO + 2048
    NTA = NRA // 128
    NTQ = 16

    xh = xio["xh"] if xio else pin("xh", [NRA, D], F32)
    hpos = pin("hpos", [NRA], F32)
    w_qkv = pin("w_qkv", [D, 3 * D], F32)
    w_o = pin("w_o", [D, D], F32)
    q_scale = pin("q_scale", [64], F32)
    k_scale = pin("k_scale", [64], F32)
    csq = pin("csq", [2048, 64], F32)
    ssq = pin("ssq", [2048, 64], F32)
    csk = pin("csk", [NRA, 64], F32)
    ssk = pin("ssk", [NRA, 64], F32)
    ident = pin("ident", [128, 128], F32)
    x_out = xio["x_out"] if xio else nc.dram_tensor("x_out", [2048, D], F32, kind="ExternalOutput")

    wqkv_bf = dramt("wqkv_bf", [D, 3 * D], BF16)
    wo_bf = dramt("wo_bf", [D, D], BF16)
    v_sp = dramt("v_sp", [NRA, D], BF16)

    sb_gen = sbuf("sb_gen", [128, 2 * D], F32)
    sb_xt = sb_gen
    sb_sh = sbuf("sb_sh", [128, D], F32)
    sb_sh16 = sbuf("sb_sh16", [128, D], BF16)
    sb_xshT = sbuf("sb_xshT", [128, 8 * NRA], BF16)
    sb_qT = sbuf("sb_qT", [128, 8 * 2048], BF16)
    sb_kT = sbuf("sb_kT", [128, 8 * NRA], BF16)
    sb_wq = sbuf("sb_wq", [128, 8 * D], BF16)
    sb_wo = sb_wq
    sb_vt = sbuf("sb_vt", [128, 2 * 3 * D], BF16)
    sb_qs = sbuf("sb_qs", [128, 64], F32)
    sb_ks = sbuf("sb_ks", [128, 64], F32)
    sb_tabq = sbuf("sb_tabq", [128, 16 * 128], F32)
    sb_tabk = sbuf("sb_tabk", [128, NTA * 128], F32)
    sb_qkf = sbuf("sb_qkf", [128, D], F32)
    sb_rot = sbuf("sb_rot", [128, D], F32)
    sb_nrm = sbuf("sb_nrm", [128, 16], F32)
    sb_out16 = sbuf("sb_out16", [128, D], BF16)
    sb_hpB = sbuf("sb_hpB", [128, NRA], F32)
    sb_cut = sbuf("sb_cut", [128, NTQ], F32)
    sb_msk = sbuf("sb_msk", [128, 896], F32)
    sb_S = sbuf("sb_S", [128, 512], F32)
    sb_P = sbuf("sb_P", [128, 512], BF16)
    sb_PT = sbuf("sb_PT", [128, 2 * 128], BF16)
    sb_oq = sb_gen[:, 0:D]
    sb_oT = sbuf("sb_oT", [128, 8 * 128], BF16)
    sb_res = sb_gen[:, D:2 * D]
    sb_xo = sbuf("sb_xo", [128, D], F32)
    sb_st = sbuf("sb_st", [128, 8], F32)
    sb_id16 = sbuf("sb_id16", [128, 128], BF16)
    sb_id32v = sbuf("sb_id32v", [128, 128], F32)

    ps_a = psum("ps_a", [128, 512], F32)
    ps_b = psum("ps_b", [128, 512], F32)
    ps_av = psum("ps_av", [128, 512], F32)
    ps_tr = psum("ps_tr", [128, 512], BF16)
    ps_tr2 = psum("ps_tr2", [128, 512], BF16)

    d_c = p.sem("d_c")
    d_cn = p.sem("d_cn")
    d_x = p.sem("d_x")
    d_sh = p.sem("d_sh")
    d_x2 = p.sem("d_x2")

    p.dma("pool", wqkv_bf.ap(), w_qkv.ap(), d_c)
    p.dma("pool", wo_bf.ap(), w_o.ap(), d_c)
    cast_done = p.ev("d_c")

    p.dma("act", sb_hpB[:, :], bass.AP(hpos, 0, [[0, 128], [1, NRA]]), d_cn)
    p.dma("act", sb_cut[:, :], hpos.ap()[HALO:HALO + 2048].rearrange("(i p) -> p i", p=128), d_cn,
          allow_slow_non_contiguous=True)
    p.dma("act", sb_qs[:, :], bass.AP(q_scale, 0, [[0, 128], [1, 64]]), d_cn)
    p.dma("act", sb_ks[:, :], bass.AP(k_scale, 0, [[0, 128], [1, 64]]), d_cn)
    p.dma("act", sb_id32v[:, :], ident.ap(), d_cn)
    p.dma("act", sb_tabq.ap().rearrange("p (i c) -> p i c", c=128)[:, :, 0:64],
          csq.ap().rearrange("(i p) c -> p i c", p=128), d_cn)
    p.dma("act", sb_tabq.ap().rearrange("p (i c) -> p i c", c=128)[:, :, 64:128],
          ssq.ap().rearrange("(i p) c -> p i c", p=128), d_cn)
    p.dma("act", sb_tabk.ap().rearrange("p (i c) -> p i c", c=128)[:, :, 0:64],
          csk.ap().rearrange("(i p) c -> p i c", p=128), d_cn)
    p.dma("act", sb_tabk.ap().rearrange("p (i c) -> p i c", c=128)[:, :, 64:128],
          ssk.ap().rearrange("(i p) c -> p i c", p=128), d_cn)
    e_cn = p.ev("d_cn")
    e_id16 = p.op("dve", lambda E: E.tensor_copy(sb_id16[:, :], sb_id32v[:, :]), wait=(e_cn,), inc=True)

    pv_sh = None
    pv_tr = None
    prev_ev = None
    for i in range(NTA):
        par = i % 2
        e_lx = p.dma("sp", sb_xt[:, par * D:(par + 1) * D], xh[128 * i:128 * (i + 1), :], d_x,
                     wait=(pv_tr,))
        e_c1 = p.op("dve", lambda E, par=par: E.tensor_copy(sb_sh[:, 0:512], sb_xt[:, par * D:par * D + 512]),
                    wait=(e_lx, pv_sh), inc=True)
        e_d1 = p.dma("sp", sb_sh[1:128, 512:1024], sb_xt[0:127, par * D + 512:par * D + 1024], d_sh,
                     wait=(e_c1,))
        if i == 0:
            e_d2 = p.op("dve", lambda E: E.memset(sb_sh[0:1, 512:1024], 0.0), wait=(e_c1,), inc=True)
            e_row = (e_d1, e_d2)
        else:
            e_d2 = p.dma("sp", sb_sh[0:1, 512:1024], sb_xt[127:128, (1 - par) * D + 512:(1 - par) * D + 1024], d_sh)
            e_row = (e_d2,)
        e_cast = p.op("dve", lambda E: E.tensor_copy(sb_sh16[:, :], sb_sh[:, :]), wait=(e_d1,) + e_row, inc=True)
        pv_sh = e_cast
        for dc in range(8):
            wl = [e_cast] if dc == 0 else [prev_ev]
            if i == 0 and dc == 0:
                wl.append(e_id16)
            et = p.op("pe", lambda E, dc=dc: E.transpose(ps_tr[:, 0:128], sb_sh16[:, 128 * dc:128 * (dc + 1)], sb_id16[:, :]),
                      wait=tuple(wl), inc=True)
            prev_ev = p.op("dve", lambda E, dc=dc, i=i: E.tensor_copy(
                sb_xshT[:, dc * NRA + 128 * i: dc * NRA + 128 * (i + 1)], ps_tr[:, 0:128]),
                wait=(et,), inc=True)
        pv_tr = prev_ev
    ev_phA = p.ev("c_dve")

    psum_last = {0: None, 1: None}
    vst_last = {0: None, 1: None}
    e_wuse = None
    rot_busy = None
    prev_trr = None

    def rotary_pass(kind):
        nonlocal psum_last, vst_last, e_wuse, rot_busy, prev_trr
        e_wl = p.dma("sp", sb_wq.ap().rearrange("p (i c) -> p i c", c=D),
                     wqkv_bf.ap().rearrange("(i p) c -> p i c", p=128)[:, :, D * kind:D * (kind + 1)],
                     p.sem("d_wk%d" % kind), wait=(e_wuse, cast_done if kind == 0 else None))
        tiles = range(3, NTA) if kind == 0 else range(NTA)
        last_mm = None
        for ti in tiles:
            ev_mm2 = {}
            for half in range(2):
                psx = ps_a if half == 0 else ps_b
                ev_mm = None
                for dc in range(8):
                    wl = []
                    if dc == 0:
                        wl = [e_wl, psum_last[half]]
                        if kind == 0 and ti == 3 and half == 0:
                            wl.append(ev_phA)
                    ev_mm = p.op("pe", lambda E, dc=dc, ti=ti, half=half, psx=psx: E.matmul(
                        psx[:, :], sb_xshT[:, dc * NRA + 128 * ti: dc * NRA + 128 * (ti + 1)],
                        sb_wq[:, dc * D + 512 * half:dc * D + 512 * half + 512],
                        start=(dc == 0), stop=(dc == 7)),
                        wait=tuple([w for w in wl if w]), inc=(dc == 7))
                ev_mm2[half] = ev_mm
            last_mm = ev_mm2[1]
            if kind == 2:
                for half in range(2):
                    psx = ps_a if half == 0 else ps_b
                    e_ev = p.op("act", lambda E, half=half, psx=psx: E.activation(
                        sb_out16[:, half * 512:(half + 1) * 512], psx[:, :], AF.Copy),
                        wait=(ev_mm2[half], vst_last[half], rot_busy, prev_trr), inc=True)
                    psum_last[half] = e_ev
                    vst_last[half] = p.dma("act", v_sp[128 * ti:128 * (ti + 1), 512 * half:512 * (half + 1)],
                                           sb_out16[:, half * 512:(half + 1) * 512], p.sem("d_v%d" % half), wait=(e_ev,))
                continue
            e_e0 = p.op("dve", lambda E: E.tensor_copy(sb_qkf[:, 0:512], ps_a[:, :]),
                        wait=(ev_mm2[0], rot_busy), inc=True)
            e_e1 = p.op("dve", lambda E: E.tensor_copy(sb_qkf[:, 512:1024], ps_b[:, :]), wait=(ev_mm2[1],), inc=True)
            psum_last[0] = e_e0
            psum_last[1] = e_e1
            e_sq = p.op("act", lambda E: E.activation(sb_rot[:, :], sb_qkf[:, :], AF.Square),
                        wait=(e_e1, prev_trr), inc=True)
            e_ss = p.op("dve", lambda E: E.reduce_sum(sb_nrm[:, :], sb_rot.ap().rearrange("p (h d) -> p h d", d=64), axis=AX.X),
                        wait=(e_sq,), inc=True)
            e_sr = p.op("act", lambda E: E.activation(sb_nrm[:, :], sb_nrm[:, :], AF.Sqrt), wait=(e_ss,), inc=True)
            e_mx = p.op("dve", lambda E: E.tensor_scalar_max(sb_nrm[:, :], sb_nrm[:, :], 1e-12), wait=(e_sr,), inc=True)
            e_rc = p.op("dve", lambda E: E.reciprocal(sb_nrm[:, :], sb_nrm[:, :]), wait=(e_mx,), inc=True)
            nb = bass.AP(sb_nrm, 0, [[16, 128], [1, 16], [0, 64]])
            e_n1 = p.op("dve", lambda E, nb=nb: E.tensor_tensor(
                sb_qkf.ap().rearrange("p (h d) -> p h d", d=64),
                sb_qkf.ap().rearrange("p (h d) -> p h d", d=64), nb, ALU.mult),
                wait=(e_rc,), inc=True)
            scl = sb_qs if kind == 0 else sb_ks
            sb_ = bass.AP(scl, 0, [[64, 128], [0, 16], [1, 64]])
            e_n2 = p.op("dve", lambda E, sb_=sb_: E.tensor_tensor(
                sb_qkf.ap().rearrange("p (h d) -> p h d", d=64),
                sb_qkf.ap().rearrange("p (h d) -> p h d", d=64), sb_, ALU.mult),
                wait=(e_n1, e_cn), inc=True)
            e_r1 = p.op("dve", lambda E: E.tensor_scalar(
                sb_rot.ap().rearrange("p (h d) -> p h d", d=64)[:, :, 0:32],
                sb_qkf.ap().rearrange("p (h d) -> p h d", d=64)[:, :, 32:64], -1.0, None, ALU.mult),
                wait=(e_n2,), inc=True)
            e_r2 = p.op("dve", lambda E: E.tensor_copy(
                sb_rot.ap().rearrange("p (h d) -> p h d", d=64)[:, :, 32:64],
                sb_qkf.ap().rearrange("p (h d) -> p h d", d=64)[:, :, 0:32]),
                wait=(e_r1,), inc=True)
            tab = sb_tabq if kind == 0 else sb_tabk
            toff = (ti - 3) * 128 if kind == 0 else ti * 128
            cs = bass.AP(tab, toff, [[tab.ap().ap[0][0], 128], [0, 16], [1, 64]])
            ss = bass.AP(tab, toff + 64, [[tab.ap().ap[0][0], 128], [0, 16], [1, 64]])
            e_m1 = p.op("dve", lambda E, cs=cs: E.tensor_tensor(
                sb_qkf.ap().rearrange("p (h d) -> p h d", d=64),
                sb_qkf.ap().rearrange("p (h d) -> p h d", d=64), cs, ALU.mult),
                wait=(e_r2,), inc=True)
            e_m2 = p.op("dve", lambda E, ss=ss: E.tensor_tensor(
                sb_rot.ap().rearrange("p (h d) -> p h d", d=64),
                sb_rot.ap().rearrange("p (h d) -> p h d", d=64), ss, ALU.mult),
                wait=(e_m1,), inc=True)
            e_add = p.op("dve", lambda E: E.tensor_add(sb_out16[:, :], sb_qkf[:, :], sb_rot[:, :]),
                         wait=(e_m2, vst_last[0], vst_last[1]), inc=True)
            rot_busy = e_add
            for dc in range(8):
                wl = [e_add] if dc == 0 else [prev_trr]
                et = p.op("pe", lambda E, dc=dc: E.transpose(ps_tr2[:, 0:128], sb_out16[:, 128 * dc:128 * (dc + 1)], sb_id16[:, :]),
                          wait=tuple(wl), inc=True)
                if kind == 0:
                    dst = sb_qT[:, dc * 2048 + 128 * (ti - 3): dc * 2048 + 128 * (ti - 2)]
                else:
                    dst = sb_kT[:, dc * NRA + 128 * ti: dc * NRA + 128 * (ti + 1)]
                prev_trr = p.op("dve", lambda E, dst=dst: E.tensor_copy(dst, ps_tr2[:, 0:128]), wait=(et,), inc=True)
        e_wuse = last_mm

    rotary_pass(0)
    rotary_pass(1)
    rotary_pass(2)
    ev_phB = p.ev("c_dve")

    p.wait("sp", cast_done)
    p.wait("sp", e_wuse)
    for dc in range(8):
        p.dma("sp", sb_wo[:, dc * D:(dc + 1) * D], wo_bf[dc * 128:(dc + 1) * 128, :], p.sem("d_wo"))
    e_wo = p.ev("d_wo")

    vld_last = {0: None, 1: None}
    res_last = {0: None, 1: None}
    prev_store = None
    prev_oq_written = None
    prev_oT_used = None
    e_ex = None
    ev_msk = None
    for qt in range(NTQ):
        par = qt % 2
        e_vl = p.dma("sp", sb_vt[:, par * 3 * D:(par + 1) * 3 * D].rearrange("p (i c) -> p i c", c=D),
                     v_sp.ap().rearrange("(i p) c -> p i c", p=128)[:, qt + 1:qt + 4, :],
                     p.sem("d_vt%d" % par), wait=(vld_last[par], (p.ev("d_v0") if qt == 0 else None),
                                                  (p.ev("d_v1") if qt == 0 else None)))
        ev_k1 = p.op("dve", lambda E, qt=qt: E.tensor_scalar(
            sb_msk[:, 0:384], sb_hpB[:, 128 * (qt + 1):128 * (qt + 4)], sb_cut[:, qt:qt + 1], None, ALU.subtract),
            wait=tuple([w for w in (e_cn, ev_msk) if w]), inc=True)
        ev_k2 = p.op("dve", lambda E: E.tensor_scalar(sb_msk[:, 512:896], sb_msk[:, 0:384], 0.0, None, ALU.is_le),
                     wait=(ev_k1,), inc=True)
        ev_k3 = p.op("dve", lambda E: E.scalar_tensor_tensor(sb_msk[:, 512:896], sb_msk[:, 0:384], -256.0, sb_msk[:, 512:896], ALU.is_ge, ALU.mult),
                     wait=(ev_k2,), inc=True)
        ev_msk = p.op("dve", lambda E: E.tensor_scalar(sb_msk[:, 512:896], sb_msk[:, 512:896], 1e9, -1e9, ALU.mult, ALU.add),
                     wait=(ev_k3,), inc=True)
        for h in range(16):
            po = 64 * (h % 2)
            dcq = h // 2
            wl = [ev_phB, e_vl] if h == 0 else []
            wl.append(psum_last[0])
            ev_mm = p.op("pe", lambda E, qt=qt, h=h, po=po, dcq=dcq: E.matmul(
                ps_a[:, 0:384], sb_qT[po:po + 64, dcq * 2048 + 128 * qt: dcq * 2048 + 128 * (qt + 1)],
                sb_kT[po:po + 64, dcq * NRA + 128 * (qt + 1): dcq * NRA + 128 * (qt + 4)],
                start=True, stop=True),
                wait=tuple([w for w in wl if w]), inc=True)
            e_sv = p.op("dve", lambda E: E.scalar_tensor_tensor(
                sb_S[:, 0:384], ps_a[:, 0:384], 8.0, sb_msk[:, 512:896], ALU.mult, ALU.add),
                wait=(ev_mm, ev_msk, e_ex), inc=True)
            psum_last[0] = e_sv
            e_nm = p.op("dve", lambda E: E.reduce_max(sb_st[:, 1:2], sb_S[:, 0:384], axis=AX.X, negate=True),
                        wait=(e_sv,), inc=True)
            e_ex = p.op("act", lambda E: E.activation(sb_P[:, 0:384], sb_S[:, 0:384], AF.Exp, bias=sb_st[:, 1:2], accum_out=sb_st[:, 0:1]),
                        wait=(e_nm,), inc=True)
            e_rz = p.op("dve", lambda E: E.reciprocal(sb_st[:, 2:3], sb_st[:, 0:1]), wait=(e_ex,), inc=True)
            ev_avmm = None
            pt_evac = {0: None, 1: None}
            for a in range(3):
                gpar = a % 2
                pstr = ps_tr if gpar == 0 else ps_tr2
                et = p.op("pe", lambda E, a=a, pstr=pstr: E.transpose(
                    pstr[:, 0:128], sb_P[:, 128 * a:128 * (a + 1)], sb_id16[:, :]),
                    wait=tuple([w for w in (e_ex if a == 0 else None, pt_evac[gpar]) if w]), inc=True)
                e_pt = p.op("dve", lambda E, gpar=gpar, pstr=pstr: E.tensor_copy(sb_PT[:, gpar * 128:(gpar + 1) * 128], pstr[:, 0:128]),
                            wait=(et, pt_evac[gpar]), inc=True)
                pt_evac[gpar] = e_pt
                wl = [e_pt]
                if a == 0:
                    wl.append(prev_oq_written)
                ev_avmm = p.op("pe", lambda E, a=a, gpar=gpar, h=h, qt=qt, par=par: E.matmul(
                    ps_av[:, 0:64], sb_PT[:, gpar * 128:(gpar + 1) * 128],
                    sb_vt[:, par * 3 * D + a * D + 64 * h: par * 3 * D + a * D + 64 * h + 64],
                    start=(a == 0), stop=(a == 2)),
                    wait=tuple([w for w in wl if w]), inc=(a == 2))
            prev_oq_written = p.op("dve", lambda E, h=h: E.tensor_scalar(
                sb_oq[:, 64 * h:64 * (h + 1)], ps_av[:, 0:64], sb_st[:, 2:3], None, ALU.mult),
                wait=(ev_avmm, e_rz, prev_oT_used), inc=True)
        vld_last[par] = ev_avmm
        e_cb = p.op("dve", lambda E: E.tensor_copy(sb_out16[:, :], sb_oq[:, :]), wait=(prev_oq_written, psum_last[1], vst_last[0], vst_last[1]), inc=True)
        prev_tr = None
        for dc in range(8):
            et = p.op("pe", lambda E, dc=dc: E.transpose(ps_tr[:, 128:256], sb_out16[:, 128 * dc:128 * (dc + 1)], sb_id16[:, :]),
                      wait=(e_cb, psum_last[0]) if dc == 0 else (prev_tr,), inc=True)
            prev_tr = p.op("dve", lambda E, dc=dc: E.tensor_copy(sb_oT[:, 128 * dc:128 * (dc + 1)], ps_tr[:, 128:256]),
                           wait=(et,), inc=True)
        e_lx = p.dma("sp", sb_xo[:, :], xh[HALO + 128 * qt:HALO + 128 * (qt + 1), :], d_x2, wait=(prev_store,))
        ev_res = []
        ev_mm = None
        for half in range(2):
            psx = ps_a if half == 0 else ps_b
            for dc in range(8):
                wl = []
                if dc == 0:
                    wl = [prev_tr, e_wo, psum_last[half]]
                ev_mm = p.op("pe", lambda E, dc=dc, half=half, psx=psx: E.matmul(
                    psx[:, :], sb_oT[:, 128 * dc:128 * (dc + 1)], sb_wo[:, dc * D + 512 * half: dc * D + 512 * (half + 1)],
                    start=(dc == 0), stop=(dc == 7)),
                    wait=tuple([w for w in wl if w]), inc=(dc == 7))
            e_r2 = p.op("dve", lambda E, half=half, psx=psx: E.tensor_add(
                sb_res[:, half * 512:(half + 1) * 512], psx[:, :], sb_xo[:, 512 * half:512 * (half + 1)]),
                wait=(ev_mm, e_lx, res_last[half]), inc=True)
            psum_last[half] = e_r2
            ev_res.append(e_r2)
        e_st1 = p.dma("act", x_out[128 * qt:128 * (qt + 1), 0:512], sb_res[:, 0:512], p.sem("d_ro0"), wait=(ev_res[0], ev_res[1]))
        e_st2 = p.dma("act", x_out[128 * qt:128 * (qt + 1), 512:1024], sb_res[:, 512:1024], p.sem("d_ro1"))
        res_last[0] = e_st1
        res_last[1] = e_st2
        prev_store = e_st2
        prev_oT_used = ev_mm
        vst_last = {0: None, 1: None}

    p.wait("act", p.ev("d_ro0"))
    p.wait("act", p.ev("d_ro1"))
    return fin()


def build_l2_inner_ffn(fz=None, xio=None, pre="if_"):
    p, nc, pin, dramt, sbuf, psum, fin = _env(fz, pre)
    HALO = 128
    NRF = HALO + 2048
    NTF = NRF // 128
    DI = 2730
    NJ = 22
    NTB = 4
    TB = 512

    xh = xio["xh"] if xio else pin("xh", [NRF, D], F32)
    w1 = pin("w1", [D, 2 * DI], F32)
    b1 = pin("b1", [2 * DI], F32)
    w2 = pin("w2", [DI, D], F32)
    b2 = pin("b2", [D], F32)
    ln_g = pin("ln_g", [D], F32)
    ln_b = pin("ln_b", [D], F32)
    ident = pin("ident", [128, 128], F32)
    x_out = xio["x_out"] if xio else nc.dram_tensor("x_out", [2048, D], F32, kind="ExternalOutput")

    w1_bf = dramt("w1_bf", [D, 2 * DI], BF16)
    w2_bf = dramt("w2_bf", [DI, D], BF16)

    sb_xt = sbuf("sb_xt", [128, 2 * D], F32)
    sb_sh = sbuf("sb_sh", [128, D], F32)
    sb_scr = sbuf("sb_scr", [128, D], F32)
    sb_dump = sbuf("sb_dump", [128, D], F32)
    sb_ln16 = sbuf("sb_ln16", [128, D], BF16)
    sb_xnT = sbuf("sb_xnT", [128, 8 * 2048], BF16)
    sb_w1s = sbuf("sb_w1s", [128, 2 * 2 * 8 * 128], BF16)
    sb_w2 = sbuf("sb_w2", [128, NJ * D], BF16)
    sb_b1a = sbuf("sb_b1a", [128, NJ], F32)
    sb_b1g = sbuf("sb_b1g", [128, NJ], F32)
    sb_b2B = sbuf("sb_b2B", [128, D], F32)
    sb_gB = sbuf("sb_gB", [128, D], F32)
    sb_bB = sbuf("sb_bB", [128, D], F32)
    sb_ab = sbuf("sb_ab", [128, TB], BF16)
    sb_hT = sbuf("sb_hT", [128, NJ * TB], BF16)
    sb_res = sbuf("sb_res", [128, D], F32)
    sb_xo = sbuf("sb_xo", [128, D], F32)
    sb_st = sbuf("sb_st", [128, 8], F32)
    sb_id16 = sbuf("sb_id16", [128, 128], BF16)
    sb_id32v = sbuf("sb_id32v", [128, 128], F32)

    ps_a = psum("ps_a", [128, 512], F32)
    ps_b = psum("ps_b", [128, 512], F32)
    ps_c = psum("ps_c", [128, 512], F32)
    ps_tr = psum("ps_tr", [128, 512], BF16)

    d_c = p.sem("d_c")
    d_cn = p.sem("d_cn")
    d_x = p.sem("d_x")
    d_sh = p.sem("d_sh")
    d_x2 = p.sem("d_x2")

    p.dma("pool", w1_bf.ap(), w1.ap(), d_c)
    p.dma("pool", w2_bf.ap(), w2.ap(), d_c)
    cast_done = p.ev("d_c")

    p.dma("act", sb_b2B[:, :], bass.AP(b2, 0, [[0, 128], [1, D]]), d_cn)
    p.dma("act", sb_gB[:, :], bass.AP(ln_g, 0, [[0, 128], [1, D]]), d_cn)
    p.dma("act", sb_bB[:, :], bass.AP(ln_b, 0, [[0, 128], [1, D]]), d_cn)
    p.dma("act", sb_id32v[:, :], ident.ap(), d_cn)
    for j in range(NJ):
        cw = min(128, DI - 128 * j)
        p.dma("act", sb_b1a[0:cw, j:j + 1], b1.ap()[128 * j:128 * j + cw].rearrange("(k o) -> k o", o=1), d_cn)
        p.dma("act", sb_b1g[0:cw, j:j + 1], b1.ap()[DI + 128 * j:DI + 128 * j + cw].rearrange("(k o) -> k o", o=1), d_cn)
    e_cn = p.ev("d_cn")
    e_id16 = p.op("dve", lambda E: E.tensor_copy(sb_id16[:, :], sb_id32v[:, :]), wait=(e_cn,), inc=True)

    p.wait("sp", cast_done)
    for j in range(NJ):
        cw = min(128, DI - 128 * j)
        p.dma("sp", sb_w2[0:cw, j * D:(j + 1) * D], w2_bf[128 * j:128 * j + cw, :], p.sem("d_w2"))
    e_w2l = p.ev("d_w2")

    pv_sh = None
    pv_tr = None
    prev_ev = None
    for i in range(16):
        ii = i + 1
        par = i % 2
        e_lx = p.dma("sp", sb_xt[:, par * D:(par + 1) * D], xh[128 * ii:128 * (ii + 1), :], d_x,
                     wait=(pv_tr,))
        if i == 0:
            p.dma("sp", sb_xt[:, D:2 * D], xh[0:128, :], d_x)
        e_jx = p.ev("d_x")
        e_c1 = p.op("dve", lambda E, par=par: E.tensor_copy(sb_sh[:, 0:512], sb_xt[:, par * D:par * D + 512]),
                    wait=(e_jx, pv_sh), inc=True)
        e_d1 = p.dma("sp", sb_sh[1:128, 512:1024], sb_xt[0:127, par * D + 512:par * D + 1024], d_sh,
                     wait=(e_c1,))
        e_d2 = p.dma("sp", sb_sh[0:1, 512:1024], sb_xt[127:128, (1 - par) * D + 512:(1 - par) * D + 1024], d_sh)
        e1 = p.op("dve", lambda E: E.reduce_sum(sb_st[:, 0:1], sb_sh[:, :], axis=AX.X),
                  wait=(e_d1, e_d2), inc=True)
        e2 = p.op("dve", lambda E: E.tensor_scalar(sb_st[:, 0:1], sb_st[:, 0:1], -1.0 / D, None, ALU.mult),
                  wait=(e1,), inc=True)
        e_xc = p.op("act", lambda E: E.activation(sb_scr[:, :], sb_sh[:, :], AF.Identity, bias=sb_st[:, 0:1]),
                    wait=(e2,), inc=True)
        pv_sh = e_xc
        e_v = p.op("act", lambda E: E.activation(sb_dump[:, :], sb_scr[:, :], AF.Square, accum_out=sb_st[:, 1:2]),
                   wait=(e_xc,), inc=True)
        e5 = p.op("dve", lambda E: E.tensor_scalar(sb_st[:, 1:2], sb_st[:, 1:2], 1.0 / D, 1e-5, ALU.mult, ALU.add),
                  wait=(e_v,), inc=True)
        e6 = p.op("act", lambda E: E.activation(sb_st[:, 2:3], sb_st[:, 1:2], AF.Sqrt), wait=(e5,), inc=True)
        e7 = p.op("dve", lambda E: E.reciprocal(sb_st[:, 3:4], sb_st[:, 2:3]), wait=(e6,), inc=True)
        e_ln = p.op("dve", lambda E: E.tensor_scalar(sb_scr[:, :], sb_scr[:, :], sb_st[:, 3:4], None, ALU.mult),
                    wait=(e7,), inc=True)
        e_g1 = p.op("dve", lambda E: E.tensor_mul(sb_scr[:, :], sb_scr[:, :], sb_gB[:, :]), wait=(e_ln, e_cn), inc=True)
        e_g2 = p.op("dve", lambda E: E.tensor_add(sb_scr[:, :], sb_scr[:, :], sb_bB[:, :]), wait=(e_g1,), inc=True)
        e_cast = p.op("dve", lambda E: E.tensor_copy(sb_ln16[:, :], sb_scr[:, :]), wait=(e_g2,), inc=True)
        for dc in range(8):
            wl = [e_cast] if dc == 0 else [prev_ev]
            if i == 0 and dc == 0:
                wl.append(e_id16)
            et = p.op("pe", lambda E, dc=dc: E.transpose(ps_tr[:, 0:128], sb_ln16[:, 128 * dc:128 * (dc + 1)], sb_id16[:, :]),
                      wait=tuple(wl), inc=True)
            prev_ev = p.op("dve", lambda E, dc=dc, i=i: E.tensor_copy(
                sb_xnT[:, dc * 2048 + 128 * i: dc * 2048 + 128 * (i + 1)], ps_tr[:, 0:128]),
                wait=(et,), inc=True)
        pv_tr = prev_ev
    ev_phA = p.ev("c_dve")

    psum_last = {0: None, 1: None, 2: None}
    w1_use = {0: None, 1: None}
    e_habuse = None
    res_last = {0: None, 1: None}
    prev_store = None
    e_h = None
    for tb in range(NTB):
        for j in range(NJ):
            cw = min(128, DI - 128 * j)
            par = j % 2
            if True:
                e_wl = p.dma("sp", sb_w1s[:, par * 2048:par * 2048 + 8 * 128].rearrange("p (i c) -> p i c", c=128)[:, :, 0:cw],
                             w1_bf.ap().rearrange("(i p) c -> p i c", p=128)[:, :, 128 * j:128 * j + cw],
                             p.sem("d_w1%d" % par), wait=(w1_use[par],))
                e_wl2 = p.dma("sp", sb_w1s[:, par * 2048 + 1024:par * 2048 + 1024 + 8 * 128].rearrange("p (i c) -> p i c", c=128)[:, :, 0:cw],
                              w1_bf.ap().rearrange("(i p) c -> p i c", p=128)[:, :, DI + 128 * j:DI + 128 * j + cw],
                              p.sem("d_w1%d" % par))
                e_wld = p.ev("d_w1%d" % par)
            ev_a = ev_g = None
            for dc in range(8):
                wl = []
                if dc == 0:
                    wl = [e_wld, psum_last[0]]
                    if tb == 0 and j == 0:
                        wl.append(ev_phA)
                        wl.append(cast_done)
                ev_a = p.op("pe", lambda E, dc=dc, j=j, cw=cw, tb=tb, par=par: E.matmul(
                    ps_a[0:cw, :], sb_w1s[:, par * 2048 + dc * 128:par * 2048 + dc * 128 + cw],
                    sb_xnT[:, dc * 2048 + TB * tb: dc * 2048 + TB * (tb + 1)],
                    start=(dc == 0), stop=(dc == 7)),
                    wait=tuple([w for w in wl if w]), inc=(dc == 7))
            for dc in range(8):
                wl = [psum_last[1]] if dc == 0 else []
                ev_g = p.op("pe", lambda E, dc=dc, j=j, cw=cw, tb=tb, par=par: E.matmul(
                    ps_b[0:cw, :], sb_w1s[:, par * 2048 + 1024 + dc * 128:par * 2048 + 1024 + dc * 128 + cw],
                    sb_xnT[:, dc * 2048 + TB * tb: dc * 2048 + TB * (tb + 1)],
                    start=(dc == 0), stop=(dc == 7)),
                    wait=tuple([w for w in wl if w]), inc=(dc == 7))
            w1_use[par] = ev_g
            e_ab = p.op("act", lambda E, j=j, cw=cw: E.activation(
                sb_ab[0:cw, :], ps_a[0:cw, :], AF.Identity, bias=sb_b1a[0:cw, j:j + 1]),
                wait=(ev_a, e_cn, e_habuse), inc=True)
            psum_last[0] = e_ab
            e_gl = p.op("act", lambda E, j=j, cw=cw: E.activation(
                sb_hT[0:cw, TB * j:TB * j + TB], ps_b[0:cw, :], AF.Gelu, bias=sb_b1g[0:cw, j:j + 1]),
                wait=(ev_g, e_h), inc=True)
            psum_last[1] = e_gl
            e_h = p.op("dve", lambda E, j=j, cw=cw: E.tensor_tensor(
                sb_hT[0:cw, TB * j:TB * j + TB], sb_hT[0:cw, TB * j:TB * j + TB], sb_ab[0:cw, :], ALU.mult),
                wait=(e_gl, e_ab), inc=True)
            e_habuse = e_h
        e_hT_done = e_h
        for tc in range(4):
            ti = 4 * tb + tc
            e_lx = p.dma("sp", sb_xo[:, :], xh[HALO + 128 * ti:HALO + 128 * (ti + 1), :], d_x2, wait=(prev_store,))
            ev_res = []
            for half in range(2):
                ev_d = None
                for j in range(NJ):
                    cw = min(128, DI - 128 * j)
                    wl = []
                    if j == 0:
                        wl = [e_hT_done, psum_last[2]]
                        if tb == 0 and tc == 0 and half == 0:
                            wl.append(e_w2l)
                    ev_d = p.op("pe", lambda E, j=j, cw=cw, tc=tc, half=half: E.matmul(
                        ps_c[:, :], sb_hT[0:cw, TB * j + 128 * tc: TB * j + 128 * (tc + 1)],
                        sb_w2[0:cw, j * D + 512 * half: j * D + 512 * half + 512],
                        start=(j == 0), stop=(j == NJ - 1)),
                        wait=tuple([w for w in wl if w]), inc=(j == NJ - 1))
                e_r1 = p.op("dve", lambda E, half=half: E.tensor_add(
                    sb_res[:, half * 512:(half + 1) * 512], ps_c[:, :], sb_b2B[:, 512 * half:512 * (half + 1)]),
                    wait=(ev_d, res_last[half]), inc=True)
                e_r2 = p.op("dve", lambda E, half=half: E.tensor_add(
                    sb_res[:, half * 512:(half + 1) * 512], sb_res[:, half * 512:(half + 1) * 512], sb_xo[:, 512 * half:512 * (half + 1)]),
                    wait=(e_r1, e_lx), inc=True)
                psum_last[2] = e_r2
                ev_res.append(e_r2)
            e_st1 = p.dma("act", x_out[128 * ti:128 * (ti + 1), 0:512], sb_res[:, 0:512], p.sem("d_ro0"), wait=(ev_res[0], ev_res[1]))
            e_st2 = p.dma("act", x_out[128 * ti:128 * (ti + 1), 512:1024], sb_res[:, 512:1024], p.sem("d_ro1"))
            res_last[0] = e_st1
            res_last[1] = e_st2
            prev_store = e_st2

    p.wait("act", p.ev("d_ro0"))
    p.wait("act", p.ev("d_ro1"))
    return fin()



def build_l3_attn(fz=None, xio=None, pre="a3_"):
    p, nc, pin, dramt, sbuf, psum, fin = _env(fz, pre)
    NB = 4

    xb4 = xio["xb4"] if xio else pin("xb4", [NB * TOK_PER_CORE, D], F32)
    w_in = pin("w_in", [3 * D, D], F32)
    b_in = pin("b_in", [3 * D], F32)
    gamma = pin("gamma", [3 * D], F32)
    alpha = pin("alpha", [D], F32)
    lgam = pin("lgam", [D], F32)
    w_out = pin("w_out", [D, D], F32)
    b_out = pin("b_out", [D], F32)
    maskvec = pin("maskvec", [64], F32)
    ident = pin("ident", [128, 128], F32)
    x_out = xio["x_out"] if xio else nc.dram_tensor("x_out", [TOK_PER_CORE, D], F32, kind="ExternalOutput")

    win_bf = dramt("win_bf", [3 * D, D], BF16)
    wout_bf = dramt("wout_bf", [D, D], BF16)
    bq_sp = dramt("bq_sp", [3 * D], F32)
    q_sp = dramt("q_sp", [TOK_PER_CORE, D], BF16)
    kv_sp = dramt("kv_sp", [NB, TOK_PER_CORE, 2 * D], BF16)

    sb_wprep = sbuf("sb_wprep", [128, D], F32)
    sb_wprep16 = sbuf("sb_wprep16", [128, D], BF16)
    sb_gcols = sbuf("sb_gcols", [128, 24], F32)
    sb_bcols = sbuf("sb_bcols", [128, 24], F32)
    sb_bqB = sbuf("sb_bqB", [128, 3 * D], F32)
    sb_aB = sbuf("sb_aB", [128, D], F32)
    sb_lgB = sbuf("sb_lgB", [128, D], F32)
    sb_boB = sbuf("sb_boB", [128, D], F32)
    sb_mvB = sbuf("sb_mvB", [128, 64], F32)
    sb_wT = sbuf("sb_wT", [128, 2 * 8 * 512], BF16)
    sb_stg = sbuf("sb_stg", [128, 2 * 512], BF16)
    sb_q = sbuf("sb_q", [128, D], BF16)
    sb_kj = sbuf("sb_kj", [128, NB * D], BF16)
    sb_vj = sbuf("sb_vj", [128, NB * D], BF16)
    sb_mul = sbuf("sb_mul", [128, D], F32)
    sb_sc = sbuf("sb_sc", [128, 64], F32)
    sb_e = sbuf("sb_e", [128, 64], F32)
    sb_zs = sbuf("sb_zs", [128, 48], F32)
    sb_oac = sbuf("sb_oac", [128, D], F32)
    sb_o16 = sbuf("sb_o16", [128, D], BF16)
    sb_oT = sbuf("sb_oT", [128, 8 * 128], BF16)
    sb_res = sbuf("sb_res", [128, D], F32)
    sb_xo = sbuf("sb_xo", [128, D], F32)
    sb_id16 = sbuf("sb_id16", [128, 128], BF16)

    ps_a = psum("ps_a", [128, 512], F32)
    ps_b = psum("ps_b", [128, 512], F32)
    ps_tr = psum("ps_tr", [128, 512], BF16)

    d_c = p.sem("d_c")
    d_cn = p.sem("d_cn")
    d_pr = p.sem("d_pr")
    d_x2 = p.sem("d_x2")

    p.dma("pool", wout_bf.ap(), w_out.ap(), d_c)
    cast_done = p.ev("d_c")

    p.dma("act", sb_gcols[:, :], gamma.ap().rearrange("(i p) -> p i", p=128), d_cn,
          allow_slow_non_contiguous=True)
    p.dma("act", sb_bcols[:, :], b_in.ap().rearrange("(i p) -> p i", p=128), d_cn,
          allow_slow_non_contiguous=True)
    p.dma("act", sb_aB[:, :], bass.AP(alpha, 0, [[0, 128], [1, D]]), d_cn)
    p.dma("act", sb_lgB[:, :], bass.AP(lgam, 0, [[0, 128], [1, D]]), d_cn)
    p.dma("act", sb_boB[:, :], bass.AP(b_out, 0, [[0, 128], [1, D]]), d_cn)
    p.dma("act", sb_mvB[:, :], bass.AP(maskvec, 0, [[0, 128], [1, 64]]), d_cn)
    e_cn = p.ev("d_cn")

    e_g1 = p.op("dve", lambda E: E.tensor_scalar(sb_gcols[:, 0:8], sb_gcols[:, 0:8], 0.125, None, ALU.mult),
                wait=(e_cn,), inc=True)
    e_b1 = p.op("dve", lambda E: E.tensor_mul(sb_bcols[:, :], sb_bcols[:, :], sb_gcols[:, :]),
                wait=(e_g1,), inc=True)
    e_bw = p.dma("sp", bq_sp.ap().rearrange("(i p) -> p i", p=128), sb_bcols[:, :], d_pr,
                 wait=(e_b1,), allow_slow_non_contiguous=True)
    e_bB = p.dma("act", sb_bqB[:, :], bass.AP(bq_sp, 0, [[0, 128], [1, 3 * D]]), p.sem("d_bb"), wait=(e_bw,))

    prep_st = None
    for oc in range(24):
        e_l = p.dma("sp", sb_wprep[:, :], w_in[128 * oc:128 * (oc + 1), :], d_pr, wait=(prep_st,))
        e_s = p.op("act", lambda E, oc=oc: E.activation(sb_wprep16[:, :], sb_wprep[:, :], AF.Copy, scale=sb_gcols[:, oc:oc + 1]),
                   wait=(e_l, e_g1), inc=True)
        prep_st = p.dma("act", win_bf[128 * oc:128 * (oc + 1), :], sb_wprep16[:, :], p.sem("d_pw"), wait=(e_s,))
    e_wprep = p.ev("d_pw")

    psum_last = {0: None, 1: None}
    stg_last = {0: None, 1: None}
    wT_use = {0: None, 1: None}
    s = None
    e_id16 = None
    last_mm_j = None
    for j in range(NB):
        s = emit_prelude(p, nc, xb4, alloc=(sbuf, psum, dramt), router=False, store_ln=False, ident_dram=ident,
                         n_rows=TOK_PER_CORE, row_base=TOK_PER_CORE * j,
                         reuse=s, first_dep=last_mm_j)
        if j == 0:
            e_id16 = p.op("dve", lambda E: E.tensor_copy(sb_id16[:, :], s.sb_id32[:, :]), wait=(s.e_in0,), inc=True)
        ots = range(6) if j == 0 else range(2, 6)
        ev_mm = None
        for ot in ots:
            par = ot % 2
            for dc in range(8):
                p.dma("sp", sb_wT[:, par * 4096 + dc * 512:par * 4096 + dc * 512 + 512],
                      win_bf[512 * ot:512 * (ot + 1), 128 * dc:128 * (dc + 1)], p.sem("d_wt%d" % par),
                      wait=(wT_use[par] if dc == 0 else None,
                            e_wprep if (j == 0 and ot <= 1 and dc == 0) else None), transpose=True)
            e_wl = p.ev("d_wt%d" % par)
            for tc in range(NT):
                psx = ps_a if tc % 2 == 0 else ps_b
                for dc in range(8):
                    wl = []
                    if dc == 0:
                        wl = [e_wl, psum_last[tc % 2]]
                        if tc == 0:
                            wl.append(s.ev_dve_done)
                    ev_mm = p.op("pe", lambda E, dc=dc, tc=tc, par=par, psx=psx, s=s: E.matmul(
                        psx[:, :], s.sb_xnT[:, dc * TOK_PER_CORE + 128 * tc: dc * TOK_PER_CORE + 128 * (tc + 1)],
                        sb_wT[:, par * 4096 + dc * 512:par * 4096 + dc * 512 + 512],
                        start=(dc == 0), stop=(dc == 7)),
                        wait=tuple([w for w in wl if w]), inc=(dc == 7))
                spar = tc % 2
                e_ev = p.op("dve", lambda E, ot=ot, spar=spar, psx=psx: E.tensor_add(
                    sb_stg[:, spar * 512:(spar + 1) * 512], psx[:, :], sb_bqB[:, 512 * ot:512 * (ot + 1)]),
                    wait=tuple([w for w in (ev_mm, e_bB, stg_last[spar]) if w]), inc=True)
                psum_last[spar] = e_ev
                if ot < 2:
                    dst = q_sp[128 * tc:128 * (tc + 1), 512 * ot:512 * (ot + 1)]
                else:
                    dst = kv_sp[j][128 * tc:128 * (tc + 1), 512 * (ot - 2):512 * (ot - 1)]
                e_st = p.dma("act", dst, sb_stg[:, spar * 512:(spar + 1) * 512], p.sem("d_sp%d" % spar), wait=(e_ev,))
                stg_last[spar] = e_st
            wT_use[par] = ev_mm
        last_mm_j = ev_mm
    ev_spills = (p.ev("d_sp0"), p.ev("d_sp1"))

    p.wait("sp", wT_use[0])
    p.wait("sp", wT_use[1])
    p.wait("sp", cast_done)
    for dc in range(8):
        p.dma("sp", sb_wT[:, dc * 1024:dc * 1024 + 1024], wout_bf[:, 128 * dc:128 * (dc + 1)],
              p.sem("d_wo"), transpose=True)
    e_wo = p.ev("d_wo")

    ld_last = None
    prev_write = None
    prev_store = None
    prev_oT_used = None
    e_pexp = None
    e_m2 = None
    res_last = {0: None, 1: None}
    for i in range(NT):
        e_lq = p.dma("sp", sb_q[:, :], q_sp[128 * i:128 * (i + 1), :], p.sem("d_q"),
                     wait=(ld_last, ev_spills[0] if i == 0 else None, ev_spills[1] if i == 0 else None))
        for j in range(NB):
            p.dma("sp", sb_kj[:, j * D:(j + 1) * D], kv_sp[j][128 * i:128 * (i + 1), 0:D], p.sem("d_q"))
            p.dma("sp", sb_vj[:, j * D:(j + 1) * D], kv_sp[j][128 * i:128 * (i + 1), D:2 * D], p.sem("d_q"))
        e_lkv = p.ev("d_q")
        ev_sc = None
        for j in range(NB):
            e_m = p.op("dve", lambda E, j=j: E.tensor_mul(sb_mul[:, :], sb_q[:, :], sb_kj[:, j * D:(j + 1) * D]),
                       wait=(e_lkv, prev_write) if j == 0 else (ev_sc,), inc=True)
            ev_sc = p.op("dve", lambda E, j=j: E.reduce_sum(
                bass.AP(sb_sc, j, [[64, 128], [4, 16]]),
                sb_mul.ap().rearrange("p (h d) -> p h d", d=64), axis=AX.X),
                wait=(e_m,), inc=True)
        e_am = p.op("dve", lambda E: E.tensor_add(sb_sc[:, :], sb_sc[:, :], sb_mvB[:, :]), wait=(ev_sc, e_cn), inc=True)
        e_mx = p.op("dve", lambda E: E.tensor_reduce(sb_zs[:, 0:16], sb_sc.ap().rearrange("p (h j) -> p h j", j=4), AX.X, ALU.max),
                    wait=(e_am,), inc=True)
        e_sb = p.op("dve", lambda E: E.tensor_tensor(
            sb_sc.ap().rearrange("p (h j) -> p h j", j=4), sb_sc.ap().rearrange("p (h j) -> p h j", j=4),
            bass.AP(sb_zs, 0, [[48, 128], [1, 16], [0, 4]]), ALU.subtract),
            wait=(e_mx,), inc=True)
        e_ex = p.op("act", lambda E: E.activation(sb_e[:, :], sb_sc[:, :], AF.Exp), wait=(e_sb, e_pexp), inc=True)
        e_z = p.op("dve", lambda E: E.reduce_sum(sb_zs[:, 16:32], sb_e.ap().rearrange("p (h j) -> p h j", j=4), axis=AX.X),
                   wait=(e_ex,), inc=True)
        e_rz = p.op("dve", lambda E: E.reciprocal(sb_zs[:, 32:48], sb_zs[:, 16:32]), wait=(e_z,), inc=True)
        e_pp = p.op("dve", lambda E: E.tensor_tensor(
            sb_e.ap().rearrange("p (h j) -> p h j", j=4), sb_e.ap().rearrange("p (h j) -> p h j", j=4),
            bass.AP(sb_zs, 32, [[48, 128], [1, 16], [0, 4]]), ALU.mult),
            wait=(e_rz,), inc=True)
        e_pexp = e_pp
        e_acc = None
        for j in range(NB):
            e_m2 = p.op("dve", lambda E, j=j: E.tensor_tensor(
                sb_mul.ap().rearrange("p (h d) -> p h d", d=64),
                sb_vj[:, j * D:(j + 1) * D].rearrange("p (h d) -> p h d", d=64),
                bass.AP(sb_e, j, [[64, 128], [4, 16], [0, 64]]), ALU.mult),
                wait=(e_pp,) if j == 0 else (e_acc,), inc=True)
            if j == 0:
                e_acc = p.op("dve", lambda E: E.tensor_copy(sb_oac[:, :], sb_mul[:, :]), wait=(e_m2,), inc=True)
            else:
                e_acc = p.op("dve", lambda E: E.tensor_add(sb_oac[:, :], sb_oac[:, :], sb_mul[:, :]), wait=(e_m2,), inc=True)
        prev_write = e_acc
        e_al = p.op("dve", lambda E: E.tensor_mul(sb_oac[:, :], sb_oac[:, :], sb_aB[:, :]), wait=(e_acc,), inc=True)
        e_cb = p.op("dve", lambda E: E.tensor_copy(sb_o16[:, :], sb_oac[:, :]), wait=(e_al,), inc=True)
        prev_tr = None
        for dc in range(8):
            wl = [e_cb, e_id16, prev_oT_used] if dc == 0 else [prev_tr]
            et = p.op("pe", lambda E, dc=dc: E.transpose(ps_tr[:, 0:128], sb_o16[:, 128 * dc:128 * (dc + 1)], sb_id16[:, :]),
                      wait=tuple([w for w in wl if w]), inc=True)
            prev_tr = p.op("dve", lambda E, dc=dc: E.tensor_copy(sb_oT[:, 128 * dc:128 * (dc + 1)], ps_tr[:, 0:128]),
                           wait=(et,), inc=True)
        e_lx = p.dma("sp", sb_xo[:, :], xb4[128 * i:128 * (i + 1), :], d_x2, wait=(prev_store,))
        ev_mm = None
        ev_res = []
        for half in range(2):
            psx = ps_a if half == 0 else ps_b
            for dc in range(8):
                wl = []
                if dc == 0:
                    wl = [prev_tr, e_wo, psum_last[half]]
                ev_mm = p.op("pe", lambda E, dc=dc, half=half, psx=psx: E.matmul(
                    psx[:, :], sb_oT[:, 128 * dc:128 * (dc + 1)], sb_wT[:, dc * 1024 + 512 * half: dc * 1024 + 512 * (half + 1)],
                    start=(dc == 0), stop=(dc == 7)),
                    wait=tuple([w for w in wl if w]), inc=(dc == 7))
            e_r1 = p.op("dve", lambda E, half=half, psx=psx: E.tensor_add(
                sb_res[:, half * 512:(half + 1) * 512], psx[:, :], sb_boB[:, 512 * half:512 * (half + 1)]),
                wait=tuple([w for w in (ev_mm, res_last[half]) if w]), inc=True)
            e_r2 = p.op("dve", lambda E, half=half: E.tensor_mul(
                sb_res[:, half * 512:(half + 1) * 512], sb_res[:, half * 512:(half + 1) * 512], sb_lgB[:, 512 * half:512 * (half + 1)]),
                wait=(e_r1,), inc=True)
            e_r3 = p.op("dve", lambda E, half=half: E.tensor_add(
                sb_res[:, half * 512:(half + 1) * 512], sb_res[:, half * 512:(half + 1) * 512], sb_xo[:, 512 * half:512 * (half + 1)]),
                wait=(e_r2, e_lx), inc=True)
            psum_last[half] = e_r1
            ev_res.append(e_r3)
        e_st1 = p.dma("act", x_out[128 * i:128 * (i + 1), 0:512], sb_res[:, 0:512], p.sem("d_ro0"), wait=(ev_res[0], ev_res[1]))
        e_st2 = p.dma("act", x_out[128 * i:128 * (i + 1), 512:1024], sb_res[:, 512:1024], p.sem("d_ro1"))
        res_last = {0: e_st1, 1: e_st2}
        prev_store = e_st2
        prev_oT_used = ev_mm
        ld_last = e_m2
    p.wait("act", p.ev("d_ro0"))
    p.wait("act", p.ev("d_ro1"))
    return fin()


_IDENT = np.eye(128, dtype=np.float32)
_NC_CACHE = {}


def _get_nc(key, builder, *args):
    if key not in _NC_CACHE:
        _NC_CACHE[key] = builder(*args)
    return _NC_CACHE[key]


def _run_sharded(nc, x, extra):
    shards = x.reshape(N_CORES, TOK_PER_CORE, D)
    in_maps = [dict(extra, x=np.ascontiguousarray(shards[c])) for c in range(N_CORES)]
    res = run_bass_kernel_spmd(nc, in_maps, core_ids=list(range(N_CORES)))
    return np.concatenate([r["x_out"] for r in res.results], axis=0)


def _attn_in_maps(x, extra):
    x4 = x.reshape(B, S, D)
    in_maps = []
    for c in range(N_CORES):
        b, h = c // 2, c % 2
        own = x4[b, 2048 * h:2048 * (h + 1)]
        oth = x4[b, 2048 * (1 - h):2048 * (2 - h)]
        xb = np.ascontiguousarray(np.concatenate([own, oth], axis=0))
        kpos = np.concatenate([np.arange(2048 * h, 2048 * (h + 1)),
                               np.arange(2048 * (1 - h), 2048 * (2 - h))]).astype(np.float32)
        in_maps.append(dict(extra, xb=xb, kpos=kpos))
    return in_maps


def _attn_gather(res, x):
    out = x.copy().reshape(B, S, D)
    for c in range(N_CORES):
        b, h = c // 2, c % 2
        out[b, 2048 * h:2048 * (h + 1)] = res.results[c]["x_out"]
    return out.reshape(B * S, D)


def _run_l0_attn(x, inputs):
    nc = _get_nc("l0_attn", build_l0_attn)
    in_maps = _attn_in_maps(x, {
        "w_qkv": inputs["L0_attn_W_qkv"],
        "b_qkv": inputs["L0_attn_b_qkv"],
        "w_o": inputs["L0_attn_W_o"],
        "b_o": inputs["L0_attn_b_o"],
        "ident": _IDENT,
    })
    res = run_bass_kernel_spmd(nc, in_maps, core_ids=list(range(N_CORES)))
    return _attn_gather(res, x)


def _run_l2_attn(x, inputs):
    nc = _get_nc("l2_attn", build_l2_attn)
    w_cat = np.concatenate([
        inputs["L2_attn_wq1"].transpose(1, 0, 2).reshape(D, D),
        inputs["L2_attn_wq2"].transpose(1, 0, 2).reshape(D, D),
        inputs["L2_attn_wk1"].transpose(1, 0, 2).reshape(D, D),
        inputs["L2_attn_wk2"].transpose(1, 0, 2).reshape(D, D),
        inputs["L2_attn_wv"].transpose(1, 0, 2).reshape(D, D)], axis=1)
    in_maps = _attn_in_maps(x, {
        "w_cat": np.ascontiguousarray(w_cat),
        "ln_g": inputs["L2_attn_ln_gamma"],
        "ln_b": inputs["L2_attn_ln_beta"],
        "lamb": np.asarray(inputs["L2_attn_lamb"], dtype=np.float32).reshape(1),
        "ident": _IDENT,
    })
    res = run_bass_kernel_spmd(nc, in_maps, core_ids=list(range(N_CORES)))
    return _attn_gather(res, x)


def _run_l1_attn(x, inputs):
    nc = _get_nc("l1_attn", build_l1_attn)
    H = 16
    gam = 1.0 - 2.0 ** (-5.0 - np.arange(H, dtype=np.float64))
    idx = np.arange(128)
    dmat = idx[:, None] - idx[None, :]
    tabU = np.where(dmat[None] >= 0, gam[:, None, None] ** np.maximum(dmat, 0)[None], 0.0)
    tabU = tabU.astype(ml_dtypes.bfloat16)
    gpcol = np.ascontiguousarray((gam[None, :] ** idx[:, None]).astype(ml_dtypes.bfloat16))
    gdrow = np.ascontiguousarray((gam[:, None] ** (128 - idx[None, :])).astype(ml_dtypes.bfloat16))
    w_cat = np.concatenate([inputs["L1_attn_Wq"], inputs["L1_attn_Wk"], inputs["L1_attn_Wv"]], axis=1)
    base = {
        "w_cat": np.ascontiguousarray(w_cat),
        "pkv": inputs["L1_attn_past_kv"],
        "gnw_e": np.ascontiguousarray(np.repeat(inputs["L1_attn_gn_w"], 64)),
        "gnb_e": np.ascontiguousarray(np.repeat(inputs["L1_attn_gn_b"], 64)),
        "tabU": np.ascontiguousarray(tabU),
        "gpcol": gpcol,
        "gdrow": gdrow,
        "ident": _IDENT,
    }
    in_maps = _attn_in_maps(x, base)
    for c in range(N_CORES):
        h = c % 2
        qpos = np.arange(2048 * h, 2048 * (h + 1), dtype=np.float64)
        if h == 0:
            cv = gam[:, None] ** (2048.0 - qpos[None, :])
        else:
            cv = np.zeros((H, 2048))
        in_maps[c]["cvec"] = np.ascontiguousarray(cv.astype(ml_dtypes.bfloat16))
    res = run_bass_kernel_spmd(nc, in_maps, core_ids=list(range(N_CORES)))
    return _attn_gather(res, x)




def build_ln_only(fz=None, xio=None, pre="ln_"):
    p, nc, pin, dramt, sbuf, psum, fin = _env(fz, pre)
    x_in = xio["x"] if xio else pin("x", [TOK_PER_CORE, D], F32)
    x_out = xio["x_out"] if xio else nc.dram_tensor("x_out", [TOK_PER_CORE, D], F32, kind="ExternalOutput")
    sb_x = sbuf("sb_x", [128, D], F32)
    sb_scr = sbuf("sb_scr", [128, D], F32)
    sb_dump = sbuf("sb_dump", [128, D], F32)
    sb_ln = sbuf("sb_ln", [128, D], F32)
    sb_st = sbuf("sb_st", [128, 8], F32)
    d_x = p.sem("d_x")
    d_o = p.sem("d_o")
    pv_xc = None
    pv_st = None
    for i in range(NT):
        e_lx = p.dma("sp", sb_x[:, :], x_in[128 * i:128 * (i + 1), :], d_x, wait=(pv_xc,))
        e1 = p.op("dve", lambda E: E.reduce_sum(sb_st[:, 0:1], sb_x[:, :], axis=AX.X), wait=(e_lx,), inc=True)
        e2 = p.op("dve", lambda E: E.tensor_scalar(sb_st[:, 0:1], sb_st[:, 0:1], -1.0 / D, None, ALU.mult),
                  wait=(e1,), inc=True)
        e_xc = p.op("act", lambda E: E.activation(sb_scr[:, :], sb_x[:, :], AF.Identity, bias=sb_st[:, 0:1]),
                    wait=(e2,), inc=True)
        pv_xc = e_xc
        e_v = p.op("act", lambda E: E.activation(sb_dump[:, :], sb_scr[:, :], AF.Square, accum_out=sb_st[:, 1:2]),
                   wait=(e_xc,), inc=True)
        e5 = p.op("dve", lambda E: E.tensor_scalar(sb_st[:, 1:2], sb_st[:, 1:2], 1.0 / D, 1e-5, ALU.mult, ALU.add),
                  wait=(e_v,), inc=True)
        e6 = p.op("act", lambda E: E.activation(sb_st[:, 2:3], sb_st[:, 1:2], AF.Sqrt), wait=(e5,), inc=True)
        e7 = p.op("dve", lambda E: E.reciprocal(sb_st[:, 3:4], sb_st[:, 2:3]), wait=(e6,), inc=True)
        e_ln = p.op("dve", lambda E: E.tensor_scalar(sb_ln[:, :], sb_scr[:, :], sb_st[:, 3:4], None, ALU.mult),
                    wait=(e7, pv_st), inc=True)
        pv_st = p.dma("act", x_out[128 * i:128 * (i + 1), :], sb_ln[:, :], d_o, wait=(e_ln,))
    p.wait("act", p.ev("d_o"))
    return fin()


def build_axpy(fz=None, xio=None, pre="ax_"):
    p, nc, pin, dramt, sbuf, psum, fin = _env(fz, pre)
    x_in = xio["x"] if xio else pin("x", [TOK_PER_CORE, D], F32)
    y_in = xio["y"] if xio else pin("y", [TOK_PER_CORE, D], F32)
    x_out = xio["x_out"] if xio else nc.dram_tensor("x_out", [TOK_PER_CORE, D], F32, kind="ExternalOutput")
    sb_x = sbuf("sb_x", [128, D], F32)
    sb_y = sbuf("sb_y", [128, D], F32)
    sb_o = sbuf("sb_o", [128, D], F32)
    d_x = p.sem("d_x")
    d_o = p.sem("d_o")
    pv = None
    for i in range(NT):
        e_l = p.dma("sp", sb_x[:, :], x_in[128 * i:128 * (i + 1), :], d_x, wait=(pv,))
        e_l2 = p.dma("sp", sb_y[:, :], y_in[128 * i:128 * (i + 1), :], d_x)
        e_c = p.op("dve", lambda E: E.scalar_tensor_tensor(sb_o[:, :], sb_y[:, :], 0.5, sb_x[:, :], ALU.mult, ALU.add),
                   wait=(e_l2,), inc=True)
        pv = p.dma("act", x_out[128 * i:128 * (i + 1), :], sb_o[:, :], d_o, wait=(e_c,))
    p.wait("act", p.ev("d_o"))
    return fin()


def _rotary_tables(pos):
    half = np.arange(0, 64, 2, dtype=np.float64)
    inv_freq = 1.0 / 10000.0 ** (half / 64)
    t = pos.astype(np.float64)
    freqs = t[:, None] * inv_freq[None, :]
    freqs = np.concatenate([freqs, freqs], -1)
    base_scale = (half + 0.4 * 64) / (1.4 * 64)
    power = (t - 4096 // 2) / (256 // 2)
    scale = base_scale[None, :] ** power[:, None]
    scale = np.concatenate([scale, scale], -1)
    cos, sin = np.cos(freqs), np.sin(freqs)
    return (np.ascontiguousarray((cos * scale).astype(np.float32)),
            np.ascontiguousarray((sin * scale).astype(np.float32)),
            np.ascontiguousarray((cos / scale).astype(np.float32)),
            np.ascontiguousarray((sin / scale).astype(np.float32)))


def _run_simple(nc, x, extra=None):
    shards = x.reshape(N_CORES, TOK_PER_CORE, D)
    in_maps = [dict(extra or {}, x=np.ascontiguousarray(shards[c])) for c in range(N_CORES)]
    res = run_bass_kernel_spmd(nc, in_maps, core_ids=list(range(N_CORES)))
    return np.concatenate([r["x_out"] for r in res.results], axis=0)


def _halo_in_maps(x, halo, extra_fn):
    x4 = x.reshape(B, S, D)
    in_maps = []
    for c in range(N_CORES):
        b, h = c // 2, c % 2
        start = 2048 * h
        lo = start - halo
        pad = max(0, -lo)
        xs = x4[b, max(0, lo):start + 2048]
        if pad:
            xs = np.concatenate([np.zeros((pad, D), np.float32), xs], axis=0)
        pos = np.arange(lo, start + 2048).astype(np.float32)
        if pad:
            pos[:pad] = -1e6
        m = {"xh": np.ascontiguousarray(xs), }
        m.update(extra_fn(b, h, pos))
        in_maps.append(m)
    return in_maps


def _run_l2_inner_attn(x, Wqkv, Wo, qs, ks):
    nc = _get_nc("l2_inner_attn", build_l2_inner_attn)
    HALO = 384

    def extra(b, h, pos):
        start = 2048 * h
        cpos = np.clip(np.arange(start - HALO, start + 2048), 0, 4095)
        csk, ssk, _, _ = None, None, None, None
        cq, sq, ci, si = _rotary_tables(np.arange(start, start + 2048))
        ck_all = _rotary_tables(cpos)
        return {"hpos": pos, "w_qkv": Wqkv, "w_o": Wo,
                "q_scale": qs, "k_scale": ks,
                "csq": cq, "ssq": sq,
                "csk": ck_all[2], "ssk": ck_all[3],
                "ident": _IDENT}
    in_maps = _halo_in_maps(x, HALO, extra)
    res = run_bass_kernel_spmd(nc, in_maps, core_ids=list(range(N_CORES)))
    return np.concatenate([r["x_out"] for r in res.results], axis=0)


def _run_l2_inner_ffn(x, lg, lb, W1, b1, W2, b2):
    nc = _get_nc("l2_inner_ffn", build_l2_inner_ffn)

    def extra(b, h, pos):
        return {"w1": W1, "b1": b1, "w2": W2, "b2": b2, "ln_g": lg, "ln_b": lb, "ident": _IDENT}
    in_maps = _halo_in_maps(x, 128, extra)
    res = run_bass_kernel_spmd(nc, in_maps, core_ids=list(range(N_CORES)))
    return np.concatenate([r["x_out"] for r in res.results], axis=0)


def _run_l2_ffn(x, inputs):
    xi = _run_simple(_get_nc("ln_only", build_ln_only), x)
    for i in range(6):
        xi = _run_l2_inner_attn(xi, inputs["L2_ffn_Wqkv"][i], inputs["L2_ffn_Wo"][i],
                                inputs["L2_ffn_q_scale"][i], inputs["L2_ffn_k_scale"][i])
        xi = _run_l2_inner_ffn(xi, inputs["L2_ffn_ln_g"][i], inputs["L2_ffn_ln_b"][i],
                               inputs["L2_ffn_W1"][i], inputs["L2_ffn_b1"][i],
                               inputs["L2_ffn_W2"][i], inputs["L2_ffn_b2"][i])
    nc = _get_nc("axpy", build_axpy)
    shards_x = x.reshape(N_CORES, TOK_PER_CORE, D)
    shards_y = xi.reshape(N_CORES, TOK_PER_CORE, D)
    in_maps = [{"x": np.ascontiguousarray(shards_x[c]), "y": np.ascontiguousarray(shards_y[c])}
               for c in range(N_CORES)]
    res = run_bass_kernel_spmd(nc, in_maps, core_ids=list(range(N_CORES)))
    return np.concatenate([r["x_out"] for r in res.results], axis=0)


def _run_l3_attn(x, inputs):
    nc = _get_nc("l3_attn", build_l3_attn)
    x4 = x.reshape(B, S, D)
    in_maps = []
    base = {
        "w_in": inputs["L3_attn_in_proj_weight"],
        "b_in": inputs["L3_attn_in_proj_bias"],
        "gamma": np.ascontiguousarray(inputs["L3_attn_gamma"].reshape(-1)),
        "alpha": np.ascontiguousarray(inputs["L3_attn_linear_alpha"].reshape(-1)),
        "lgam": np.ascontiguousarray(inputs["L3_attn_linear_gamma"].reshape(-1)),
        "w_out": inputs["L3_attn_out_proj_weight"],
        "b_out": inputs["L3_attn_out_proj_bias"],
        "ident": _IDENT,
    }
    for c in range(N_CORES):
        b, h = c // 2, c % 2
        sl = slice(2048 * h, 2048 * (h + 1))
        order = [b] + [bb for bb in range(B) if bb != b]
        xb4 = np.concatenate([x4[bb, sl] for bb in order], axis=0)
        mv = np.zeros((16, 4), np.float32)
        for jj, bb in enumerate(order):
            if bb > b:
                mv[:, jj] = -1e9
        in_maps.append(dict(base, xb4=np.ascontiguousarray(xb4),
                            maskvec=np.ascontiguousarray(mv.reshape(-1))))
    res = run_bass_kernel_spmd(nc, in_maps, core_ids=list(range(N_CORES)))
    return _attn_gather(res, x)


def _run_l0_ffn(x, inputs):
    nc = _get_nc("l0_ffn", build_moe_ffn, True, True, False)
    return _run_sharded(nc, x, {
        "norm_scale": inputs["L0_ffn_norm_scale"],
        "w_router": inputs["L0_ffn_w_router"],
        "w_up": inputs["L0_ffn_w_up"],
        "w_down": inputs["L0_ffn_w_down"],
        "ident": _IDENT,
    })


def _run_l3_ffn(x, inputs):
    nc = _get_nc("l3_ffn", build_moe_ffn, False, False, True)
    return _run_sharded(nc, x, {
        "w_router": inputs["L3_ffn_Wg"],
        "w_a": inputs["L3_ffn_W1"],
        "w_b": inputs["L3_ffn_W2"],
        "w_down": inputs["L3_ffn_W3"],
        "ident": _IDENT,
    })


def _run_l1_ffn(x, inputs):
    nc = _get_nc("l1_ffn", build_l1_ffn)
    return _run_sharded(nc, x, {
        "base_w": inputs["L1_ffn_base_w"],
        "base_b": inputs["L1_ffn_base_b"],
        "lora_A": inputs["L1_ffn_lora_A"],
        "lora_B": inputs["L1_ffn_lora_B"],
        "w_router": inputs["L1_ffn_router_w"],
        "router_b": inputs["L1_ffn_router_b"],
        "ident": _IDENT,
    })





PAIR_GROUPS = [[0, 1], [2, 3], [4, 5], [6, 7]]
L3_GROUPS = [[0, 2, 4, 6], [1, 3, 5, 7]]


def _flat(t, off_el, n_el):
    return bass.AP(t, off_el, [[1, n_el]])


def _emit_selsum(p, sbuf, tag, dst, dst_row0, src, src_row0s, sv, rows, dep):
    nb = len(src_row0s)
    sb_s = sbuf(tag + "_s", [128, nb], F32)
    sb_b = sbuf(tag + "_b", [128, nb * D], F32)
    sb_o = sbuf(tag + "_o", [128, D], F32)
    d_g = p.sem("d_g")
    d_go = p.sem("d_go")
    e_s = p.dma("sp", sb_s[:, :], bass.AP(sv, 0, [[0, 128], [1, nb]]), d_g)
    prev_c = None
    prev_st = None
    for t in range(rows // 128):
        evl = []
        for j in range(nb):
            r0 = src_row0s[j] + 128 * t
            evl.append(p.dma("sp", sb_b[:, j * D:(j + 1) * D], src[r0:r0 + 128, :], d_g,
                             wait=(dep, prev_c)))
        e_c = p.op("dve", lambda E: E.tensor_scalar(sb_o[:, :], sb_b[:, 0:D], sb_s[:, 0:1], None, ALU.mult),
                   wait=tuple(evl) + (e_s, prev_st), inc=True)
        for j in range(1, nb):
            e_c = p.op("dve", lambda E, j=j: E.scalar_tensor_tensor(
                sb_o[:, :], sb_b[:, j * D:(j + 1) * D], sb_s[:, j:j + 1], sb_o[:, :], ALU.mult, ALU.add),
                wait=(e_c,), inc=True)
        r0 = dst_row0 + 128 * t
        prev_st = p.dma("sp", dst[r0:r0 + 128, :], sb_o[:, :], d_go, wait=(e_c,))
        prev_c = e_c


def emit_glue_pair_full(fz, xc, xb, g2, sv, pre):
    p, nc, pin, dramt, sbuf, psum, fin = _env(fz, pre)
    e_cc = p.op("pool", lambda E: E.collective_compute(
        "AllGather", ALU.bypass, replica_groups=PAIR_GROUPS,
        ins=[_flat(xc, 0, TOK_PER_CORE * D)], outs=[_flat(g2, 0, 2 * TOK_PER_CORE * D)]),
        inc="d_cc", inc_amt=1)
    p.dma("sp", xb[0:TOK_PER_CORE, :], xc[:, :], p.sem("d_cp"))
    _emit_selsum(p, sbuf, "ss", xb, TOK_PER_CORE, g2, [0, TOK_PER_CORE], sv, TOK_PER_CORE, e_cc)
    fin()


def emit_glue_halo(fz, xl, xh, gt, halo, sv, pre):
    p, nc, pin, dramt, sbuf, psum, fin = _env(fz, pre)
    e_cc = p.op("pool", lambda E: E.collective_compute(
        "AllGather", ALU.bypass, replica_groups=PAIR_GROUPS,
        ins=[_flat(xl, (TOK_PER_CORE - halo) * D, halo * D)], outs=[_flat(gt, 0, 2 * halo * D)]),
        inc="d_cc", inc_amt=1)
    p.dma("sp", xh[halo:halo + TOK_PER_CORE, :], xl[:, :], p.sem("d_cp"))
    _emit_selsum(p, sbuf, "ss", xh, 0, gt, [0, halo], sv, halo, e_cc)
    fin()


def emit_glue_l3(fz, xc, xb4, g4, svs, pre):
    p, nc, pin, dramt, sbuf, psum, fin = _env(fz, pre)
    e_cc = p.op("pool", lambda E: E.collective_compute(
        "AllGather", ALU.bypass, replica_groups=L3_GROUPS,
        ins=[_flat(xc, 0, TOK_PER_CORE * D)], outs=[_flat(g4, 0, 4 * TOK_PER_CORE * D)]),
        inc="d_cc", inc_amt=1)
    p.dma("sp", xb4[0:TOK_PER_CORE, :], xc[:, :], p.sem("d_cp"))
    src0 = [0, TOK_PER_CORE, 2 * TOK_PER_CORE, 3 * TOK_PER_CORE]
    for k in (1, 2, 3):
        _emit_selsum(p, sbuf, f"ss{k}", xb4, k * TOK_PER_CORE, g4, src0, svs[k - 1], TOK_PER_CORE, e_cc)
    fin()


def emit_final_copy(fz, xc, out, pre):
    p, nc, pin, dramt, sbuf, psum, fin = _env(fz, pre)
    p.dma("sp", out[:, :], xc[:, :], p.sem("d_fc"))
    fin()


def _redir(fz, dst_pre, src_pre, names):
    for n in names:
        fz.redirect[dst_pre + n] = src_pre + n


def build_fused(upto=12):
    nc = bass.Bass("TRN2", target_bir_lowering=False)
    p = Prog(nc)
    fz = FZ(p, nc)

    _, _, gpin, _, _, _, _ = fz.env("g_")
    ident = gpin("ident", [128, 128], F32)
    sv_pair = gpin("sv_pair", [2], F32)
    sv_halo = gpin("sv_halo", [2], F32)
    sv_l3 = [gpin(f"sv_l3_{k}", [4], F32) for k in (1, 2, 3)]

    x0 = nc.dram_tensor("x0", [TOK_PER_CORE, D], F32, kind="ExternalInput")
    fz.input_names.append("x0")
    out = nc.dram_tensor("out", [TOK_PER_CORE, D], F32, kind="ExternalOutput")
    xc = nc.dram_tensor("xc", [TOK_PER_CORE, D], F32)
    xl = nc.dram_tensor("xl", [TOK_PER_CORE, D], F32)
    xb = nc.dram_tensor("xbuf", [2 * TOK_PER_CORE, D], F32)
    xb4 = nc.dram_tensor("xb4buf", [4 * TOK_PER_CORE, D], F32)
    g2 = nc.dram_tensor("g2buf", [2 * TOK_PER_CORE, D], F32)
    g4 = nc.dram_tensor("g4buf", [4 * TOK_PER_CORE, D], F32)
    gta = nc.dram_tensor("gtabuf", [2 * 384, D], F32)
    gtf = nc.dram_tensor("gtfbuf", [2 * 128, D], F32)
    xha = nc.dram_tensor("xhabuf", [384 + TOK_PER_CORE, D], F32)
    xhf = nc.dram_tensor("xhfbuf", [128 + TOK_PER_CORE, D], F32)

    def all_ident(pre):
        fz.redirect[pre + "ident"] = "g_ident"

    MOE_L0 = ["w_router", "w_up", "w_down", "norm_scale"]
    MOE_L3 = ["w_router", "w_a", "w_b", "w_down"]
    L1FFN = ["base_w", "base_b", "lora_A", "lora_B", "w_router", "router_b"]
    IA_W = ["w_qkv", "w_o", "q_scale", "k_scale"]
    IA_SH = ["hpos", "csq", "ssq", "csk", "ssk"]
    IF_W = ["w1", "b1", "w2", "b2", "ln_g", "ln_b"]

    k = 0

    def emit_l2_ffn(r):
        pre = f"c{r}"
        all_ident(pre + "ln_")
        build_ln_only(fz=fz, xio={"x": xc, "x_out": xl}, pre=pre + "ln_")
        for i in range(6):
            pa = f"c{r}a{i}_"
            pf = f"c{r}f{i}_"
            all_ident(pa)
            all_ident(pf)
            if r == 1:
                _redir(fz, pa, f"c0a{i}_", IA_W)
                _redir(fz, pf, f"c0f{i}_", IF_W)
            if not (r == 0 and i == 0):
                _redir(fz, pa, "c0a0_", IA_SH)
            emit_glue_halo(fz, xl, xha, gta, 384, sv_halo, pa + "g_")
            build_l2_inner_attn(fz=fz, xio={"xh": xha, "x_out": xl}, pre=pa)
            emit_glue_halo(fz, xl, xhf, gtf, 128, sv_halo, pf + "g_")
            build_l2_inner_ffn(fz=fz, xio={"xh": xhf, "x_out": xl}, pre=pf)
        all_ident(pre + "ax_")
        build_axpy(fz=fz, xio={"x": xc, "y": xl, "x_out": xc}, pre=pre + "ax_")

    if upto > 0:
        all_ident("b0f1_")
        build_moe_ffn(True, True, False, fz=fz, xio={"x": x0, "x_out": xc}, pre="b0f1_")
    k = 1
    if upto > 1:
        emit_glue_pair_full(fz, xc, xb, g2, sv_pair, "ga0_")
        all_ident("b0a_")
        build_l0_attn(fz=fz, xio={"xb": xb, "x_out": xc}, pre="b0a_")
    if upto > 2:
        all_ident("b0f2_")
        _redir(fz, "b0f2_", "b0f1_", MOE_L0)
        build_moe_ffn(True, True, False, fz=fz, xio={"x": xc, "x_out": xc}, pre="b0f2_",
                      cast_share=fz.tensors.get("b0f1_castbufs"))
    if upto > 3:
        all_ident("b1f1_")
        build_l1_ffn(fz=fz, xio={"x": xc, "x_out": xc}, pre="b1f1_")
    if upto > 4:
        emit_glue_pair_full(fz, xc, xb, g2, sv_pair, "ga1_")
        all_ident("b1a_")
        fz.redirect["b1a_kpos"] = "b0a_kpos" if upto > 1 else None
        if fz.redirect["b1a_kpos"] is None:
            del fz.redirect["b1a_kpos"]
        build_l1_attn(fz=fz, xio={"xb": xb, "x_out": xc}, pre="b1a_")
    if upto > 5:
        all_ident("b1f2_")
        _redir(fz, "b1f2_", "b1f1_", L1FFN)
        build_l1_ffn(fz=fz, xio={"x": xc, "x_out": xc}, pre="b1f2_")
    if upto > 6:
        emit_l2_ffn(0)
    if upto > 7:
        emit_glue_pair_full(fz, xc, xb, g2, sv_pair, "ga2_")
        all_ident("b2a_")
        build_l2_attn(fz=fz, xio={"xb": xb, "x_out": xc}, pre="b2a_")
    if upto > 8:
        emit_l2_ffn(1)
    if upto > 9:
        all_ident("b3f1_")
        build_moe_ffn(False, False, True, fz=fz, xio={"x": xc, "x_out": xc}, pre="b3f1_")
    if upto > 10:
        emit_glue_l3(fz, xc, xb4, g4, sv_l3, "gl3_")
        all_ident("b3a_")
        build_l3_attn(fz=fz, xio={"xb4": xb4, "x_out": xc}, pre="b3a_")
    if upto > 11:
        all_ident("b3f2_")
        _redir(fz, "b3f2_", "b3f1_", MOE_L3)
        build_moe_ffn(False, False, True, fz=fz, xio={"x": xc, "x_out": out}, pre="b3f2_",
                      cast_share=fz.tensors.get("b3f1_castbufs"))
    else:
        emit_final_copy(fz, xc, out, "fc_")
    p.finalize()
    return nc, sorted(set(fz.input_names))


def _fused_in_maps(inputs, declared):
    x = np.ascontiguousarray(np.asarray(inputs["L0_attn_x"], dtype=np.float32).reshape(B * S, D))
    shards = x.reshape(N_CORES, TOK_PER_CORE, D)
    H = 16
    gam = 1.0 - 2.0 ** (-5.0 - np.arange(H, dtype=np.float64))
    idx = np.arange(128)
    dmat = idx[:, None] - idx[None, :]
    tabU = np.ascontiguousarray(np.where(dmat[None] >= 0, gam[:, None, None] ** np.maximum(dmat, 0)[None], 0.0).astype(ml_dtypes.bfloat16))
    gpcol = np.ascontiguousarray((gam[None, :] ** idx[:, None]).astype(ml_dtypes.bfloat16))
    gdrow = np.ascontiguousarray((gam[:, None] ** (128 - idx[None, :])).astype(ml_dtypes.bfloat16))
    w_cat_l1 = np.ascontiguousarray(np.concatenate(
        [inputs["L1_attn_Wq"], inputs["L1_attn_Wk"], inputs["L1_attn_Wv"]], axis=1))
    w_cat_l2 = np.ascontiguousarray(np.concatenate([
        inputs["L2_attn_wq1"].transpose(1, 0, 2).reshape(D, D),
        inputs["L2_attn_wq2"].transpose(1, 0, 2).reshape(D, D),
        inputs["L2_attn_wk1"].transpose(1, 0, 2).reshape(D, D),
        inputs["L2_attn_wk2"].transpose(1, 0, 2).reshape(D, D),
        inputs["L2_attn_wv"].transpose(1, 0, 2).reshape(D, D)], axis=1))

    maps = []
    for c in range(N_CORES):
        b, h = c // 2, c % 2
        m = {"x0": np.ascontiguousarray(shards[c]), "g_ident": _IDENT}
        m["g_sv_pair"] = np.array([1.0, 0.0] if h == 1 else [0.0, 1.0], np.float32)
        m["g_sv_halo"] = np.array([1.0, 0.0] if h == 1 else [0.0, 0.0], np.float32)
        order = [b] + [bb for bb in range(B) if bb != b]
        for kk in (1, 2, 3):
            sv = np.zeros(4, np.float32)
            sv[order[kk]] = 1.0
            m[f"g_sv_l3_{kk}"] = sv
        m["b0f1_norm_scale"] = inputs["L0_ffn_norm_scale"]
        m["b0f1_w_router"] = inputs["L0_ffn_w_router"]
        m["b0f1_w_up"] = inputs["L0_ffn_w_up"]
        m["b0f1_w_down"] = inputs["L0_ffn_w_down"]
        kpos = np.concatenate([np.arange(2048 * h, 2048 * (h + 1)),
                               np.arange(2048 * (1 - h), 2048 * (2 - h))]).astype(np.float32)
        m["b0a_kpos"] = kpos
        m["b0a_w_qkv"] = inputs["L0_attn_W_qkv"]
        m["b0a_b_qkv"] = inputs["L0_attn_b_qkv"]
        m["b0a_w_o"] = inputs["L0_attn_W_o"]
        m["b0a_b_o"] = inputs["L0_attn_b_o"]
        m["b1f1_base_w"] = inputs["L1_ffn_base_w"]
        m["b1f1_base_b"] = inputs["L1_ffn_base_b"]
        m["b1f1_lora_A"] = inputs["L1_ffn_lora_A"]
        m["b1f1_lora_B"] = inputs["L1_ffn_lora_B"]
        m["b1f1_w_router"] = inputs["L1_ffn_router_w"]
        m["b1f1_router_b"] = inputs["L1_ffn_router_b"]
        m["b1a_w_cat"] = w_cat_l1
        m["b1a_pkv"] = inputs["L1_attn_past_kv"]
        m["b1a_gnw_e"] = np.ascontiguousarray(np.repeat(inputs["L1_attn_gn_w"], 64))
        m["b1a_gnb_e"] = np.ascontiguousarray(np.repeat(inputs["L1_attn_gn_b"], 64))
        m["b1a_tabU"] = tabU
        m["b1a_gpcol"] = gpcol
        m["b1a_gdrow"] = gdrow
        qpos = np.arange(2048 * h, 2048 * (h + 1), dtype=np.float64)
        if h == 0:
            cv = gam[:, None] ** (2048.0 - qpos[None, :])
        else:
            cv = np.zeros((H, 2048))
        m["b1a_cvec"] = np.ascontiguousarray(cv.astype(ml_dtypes.bfloat16))
        HALO = 384
        start = 2048 * h
        lo = start - HALO
        pos = np.arange(lo, start + 2048).astype(np.float32)
        if h == 0:
            pos[:HALO] = -1e6
        cpos = np.clip(np.arange(start - HALO, start + 2048), 0, 4095)
        cq, sq, _, _ = _rotary_tables(np.arange(start, start + 2048))
        _, _, ck, sk = None, None, None, None
        ck_all = _rotary_tables(cpos)
        m["c0a0_hpos"] = pos
        m["c0a0_csq"] = cq
        m["c0a0_ssq"] = sq
        m["c0a0_csk"] = ck_all[2]
        m["c0a0_ssk"] = ck_all[3]
        for i in range(6):
            m[f"c0a{i}_w_qkv"] = inputs["L2_ffn_Wqkv"][i]
            m[f"c0a{i}_w_o"] = inputs["L2_ffn_Wo"][i]
            m[f"c0a{i}_q_scale"] = inputs["L2_ffn_q_scale"][i]
            m[f"c0a{i}_k_scale"] = inputs["L2_ffn_k_scale"][i]
            m[f"c0f{i}_w1"] = inputs["L2_ffn_W1"][i]
            m[f"c0f{i}_b1"] = inputs["L2_ffn_b1"][i]
            m[f"c0f{i}_w2"] = inputs["L2_ffn_W2"][i]
            m[f"c0f{i}_b2"] = inputs["L2_ffn_b2"][i]
            m[f"c0f{i}_ln_g"] = inputs["L2_ffn_ln_g"][i]
            m[f"c0f{i}_ln_b"] = inputs["L2_ffn_ln_b"][i]
        m["b2a_w_cat"] = w_cat_l2
        m["b2a_ln_g"] = inputs["L2_attn_ln_gamma"]
        m["b2a_ln_b"] = inputs["L2_attn_ln_beta"]
        m["b2a_lamb"] = np.asarray(inputs["L2_attn_lamb"], dtype=np.float32).reshape(1)
        m["b3f1_w_router"] = inputs["L3_ffn_Wg"]
        m["b3f1_w_a"] = inputs["L3_ffn_W1"]
        m["b3f1_w_b"] = inputs["L3_ffn_W2"]
        m["b3f1_w_down"] = inputs["L3_ffn_W3"]
        m["b3a_w_in"] = inputs["L3_attn_in_proj_weight"]
        m["b3a_b_in"] = inputs["L3_attn_in_proj_bias"]
        m["b3a_gamma"] = np.ascontiguousarray(inputs["L3_attn_gamma"].reshape(-1))
        m["b3a_alpha"] = np.ascontiguousarray(inputs["L3_attn_linear_alpha"].reshape(-1))
        m["b3a_lgam"] = np.ascontiguousarray(inputs["L3_attn_linear_gamma"].reshape(-1))
        m["b3a_w_out"] = inputs["L3_attn_out_proj_weight"]
        m["b3a_b_out"] = inputs["L3_attn_out_proj_bias"]
        mv = np.zeros((16, 4), np.float32)
        for jj, bb in enumerate(order):
            if bb > b:
                mv[:, jj] = -1e9
        m["b3a_maskvec"] = np.ascontiguousarray(mv.reshape(-1))
        missing = [kk for kk in declared if kk not in m]
        assert not missing, f"missing fused inputs: {missing[:8]}"
        maps.append({kk: m[kk] for kk in declared})
    return maps


_FUSED_CACHE = {}


def kernel_fused(upto=12, **inputs):
    key = upto
    if key not in _FUSED_CACHE:
        _FUSED_CACHE[key] = build_fused(upto)
    nc, declared = _FUSED_CACHE[key]
    maps = _fused_in_maps(inputs, declared)
    res = run_bass_kernel_spmd(nc, maps, core_ids=list(range(N_CORES)))
    outs = np.stack([res.results[c]["out"] for c in range(N_CORES)], axis=0)
    x4 = outs.reshape(B, 2, 2048, D).reshape(B, S, D)
    return x4


def kernel(**inputs):
    return kernel_fused(upto=12, **inputs)


def kernel_unfused(**inputs):
    x = np.ascontiguousarray(np.asarray(inputs["L0_attn_x"], dtype=np.float32).reshape(B * S, D))
    x = _run_l0_ffn(x, inputs)
    x = _run_l0_attn(x, inputs)
    x = _run_l0_ffn(x, inputs)
    x = _run_l1_ffn(x, inputs)
    x = _run_l1_attn(x, inputs)
    x = _run_l1_ffn(x, inputs)
    x = _run_l2_ffn(x, inputs)
    x = _run_l2_attn(x, inputs)
    x = _run_l2_ffn(x, inputs)
    x = _run_l3_ffn(x, inputs)
    x = _run_l3_attn(x, inputs)
    x = _run_l3_ffn(x, inputs)
    return x.reshape(B, S, D)
